# version 25
# baseline (speedup 1.0000x reference)
import numpy as np
import ml_dtypes
from contextlib import ExitStack

import concourse.bass as bass
import concourse.mybir as mybir
from concourse.bass_utils import run_bass_kernel_spmd

F32 = mybir.dt.float32
BF16 = mybir.dt.bfloat16
AF = mybir.ActivationFunctionType
ALU = mybir.AluOpType
AX = mybir.AxisListType
BF16_NP = ml_dtypes.bfloat16

D = 1024
NCH = 8
H = 16
DH = 64
FM = 64
FI = 16
NEG_INIT = -30.0


class CFG:
    def __init__(self, nseq, seq, T):
        assert seq % T == 0
        self.nseq = nseq
        self.seq = seq
        self.T = T
        self.ntile = seq // T
        self.tok = nseq * seq


class _Op:
    __slots__ = ("eng", "emit", "deps", "is_dma", "need_inc", "val", "idx",
                 "tag", "ndma", "lanes")


class Prog:
    def __init__(self):
        self.ops = []
        self.bufs = {}

    def add(self, eng, emit, reads=(), writes=(), deps=(), dma=False, tag="",
            ndma=1):
        dl = list(deps)
        for k in reads:
            b = self.bufs.get(k)
            if b is not None and b["w"] is not None:
                dl.append(b["w"])
        for k in writes:
            b = self.bufs.get(k)
            if b is not None:
                if b["w"] is not None:
                    dl.append(b["w"])
                dl.extend(b["r"])
        op = _Op()
        op.eng = eng
        op.emit = emit
        op.is_dma = dma
        op.need_inc = False
        op.val = 0
        op.idx = len(self.ops)
        op.tag = tag
        op.ndma = ndma
        seen = set()
        op.deps = []
        for d in dl:
            if d is not None and id(d) not in seen:
                seen.add(id(d))
                op.deps.append(d)
        self.ops.append(op)
        for k in reads:
            self.bufs.setdefault(k, {"w": None, "r": []})["r"].append(op)
        for k in writes:
            self.bufs[k] = {"w": op, "r": []}
        return op

    @staticmethod
    def _needs_wait(dep, op):
        return True

    NLANES = 8

    def finalize(self, sems, paranoid=False):
        if paranoid:
            for i in range(1, len(self.ops)):
                prev = self.ops[i - 1]
                if prev not in self.ops[i].deps:
                    self.ops[i].deps.append(prev)
        for op in self.ops:
            if op.is_dma:
                op.need_inc = True
            for dep in op.deps:
                if self._needs_wait(dep, op):
                    dep.need_inc = True
        counters = {}
        lane_next = {}
        lane_occ = {}
        for op in self.ops:
            if op.is_dma:
                op.lanes = []
                for i in range(op.ndma):
                    ln = lane_next.get(op.eng, 0)
                    lane_next[op.eng] = (ln + 1) % self.NLANES
                    occ = lane_occ.get((op.eng, ln), 0) + 1
                    lane_occ[(op.eng, ln)] = occ
                    op.lanes.append((ln, occ))
            elif op.need_inc:
                key = (op.eng, False)
                counters[key] = counters.get(key, 0) + 1
                op.val = counters[key]
        by_eng = {}
        for op in self.ops:
            by_eng.setdefault(op.eng, []).append(op)

        def run_stream(eng_name, e, clear_sems=None, go=None):
            if eng_name == "gpsimd":
                for s in clear_sems or ():
                    e.sem_clear(s)
                if go is not None:
                    e.sem_inc(go, 1)
            elif go is not None:
                e.wait_ge(go, 1)
            waited = {}

            def wait_to(skey, val):
                if waited.get(skey, -1) < val:
                    e.wait_ge(sems[skey], val)
                    waited[skey] = val

            for op in by_eng.get(eng_name, ()):
                for dep in op.deps:
                    if not self._needs_wait(dep, op):
                        continue
                    if dep.is_dma:
                        for (ln, occ) in dep.lanes:
                            wait_to((dep.eng, "d", ln), 16 * occ)
                    else:
                        wait_to((dep.eng, False), dep.val)
                if op.is_dma:
                    for (ln, occ) in op.lanes:
                        if occ > 1:
                            wait_to((eng_name, "d", ln), 16 * (occ - 1))
                ret = op.emit(e)
                insts = ret if isinstance(ret, (list, tuple)) else [ret]
                if op.is_dma:
                    assert len(insts) == op.ndma, (op.tag, len(insts), op.ndma)
                    for it, (ln, occ) in zip(insts, op.lanes):
                        it.then_inc(sems[(eng_name, "d", ln)], 16)
                elif op.need_inc:
                    insts[-1].then_inc(sems[(eng_name, False)], 1)
            for (eng2, ln), occ in lane_occ.items():
                if eng2 == eng_name:
                    wait_to((eng_name, "d", ln), 16 * occ)
            e.drain()

        return run_stream, counters, lane_occ


def build_nc(cfg: CFG, dbg=False, paranoid=False, only_p1=False):
    T = cfg.T
    nc = bass.Bass("TRN2", target_bir_lowering=False, debug=False)

    def din(name, shape, dt=F32):
        return nc.dram_tensor(name, list(shape), dt, kind="ExternalInput").ap()

    xT = din("xT", [D, cfg.tok])
    wq_l = din("wq_l", [128, NCH, D]); wk_l = din("wk_l", [128, NCH, D])
    wt_l = din("wt_l", [128, NCH, D]); wqT_l = din("wqT_l", [128, NCH, D])
    wqa_l = din("wqa_l", [128, NCH, H]); wka_l = din("wka_l", [128, NCH, H])
    win_l = din("win_l", [128, NCH, FM]); wout_l = din("wout_l", [FM, D])
    w2bd_l = din("w2bd_l", [128, NCH, FM])
    sel_l = din("sel_l", [FM, NCH, 128], BF16)
    i128_l = din("i128_l", [128, 128], BF16)
    amask_l = din("amask_l", [H, NCH], BF16)
    asel_l = din("asel_l", [H, 128], BF16)
    esel_l = din("esel_l", [H, NCH, 128], BF16)
    w1f_l = din("w1f_l", [128, NCH]); b1raw_l = din("b1raw_l", [128, NCH])
    binf_l = din("binf_l", [128, NCH])
    bq_l = din("bq_l", [128, NCH]); bk_l = din("bk_l", [128, NCH])
    bt_l = din("bt_l", [128, NCH]); bout_l = din("bout_l", [128, NCH])
    bqa_l = din("bqa_l", [H, 1]); bka_l = din("bka_l", [H, 1])
    b2_l = din("b2_l", [FM, 1])

    out_d = nc.dram_tensor("out", [D, cfg.tok], F32, kind="ExternalOutput").ap()
    dbg_d = {}
    if dbg:
        for nm, shape in [("dbg_x1", [D, T]), ("dbg_ptq", [D, T]),
                          ("dbg_ptk", [D, T]), ("dbg_sc", [H, T]),
                          ("dbg_e", [H, T]), ("dbg_pq", [128, NCH]),
                          ("dbg_pk", [128, NCH]), ("dbg_x2", [D, T]),
                          ("dbg_r", [4, T]), ("dbg_btt", [128, NCH]),
                          ("dbg_st", [FM, T]), ("dbg_g", [FM, T]),
                          ("dbg_E", [D, T]), ("dbg_pp", [128, NCH]),
                          ("dbg_Pk", [128, NCH]), ("dbg_Sk", [H, 8]),
                          ("dbg_tiny", [H, 40]), ("dbg_Pcur", [128, NCH]),
                          ("dbg_S", [H, 40]), ("dbg_rs", [128, NCH])]:
            dbg_d[nm] = nc.dram_tensor(nm, shape, F32, kind="ExternalOutput").ap()

    ptq_d = nc.dram_tensor("ptq_d", [D, cfg.tok], BF16, kind="Internal").ap()
    ptk_d = nc.dram_tensor("ptk_d", [D, cfg.tok], BF16, kind="Internal").ap()
    x1_d = nc.dram_tensor("x1_d", [D, cfg.tok], BF16, kind="Internal").ap()
    stat_d = nc.dram_tensor("stat_d", [4, cfg.tok], F32, kind="Internal").ap()
    eq_d = nc.dram_tensor("eq_d", [H, cfg.tok], BF16, kind="Internal").ap()
    escq_d = nc.dram_tensor("escq_d", [H, cfg.tok], F32, kind="Internal").ap()
    ek_d = nc.dram_tensor("ek_d", [H, cfg.tok], BF16, kind="Internal").ap()
    nslot = 2 * cfg.nseq * cfg.ntile
    alpha_d = nc.dram_tensor("alpha_d", [nslot, H], F32, kind="Internal").ap()
    rs_d = nc.dram_tensor("rs_d", [2 * cfg.nseq, H], F32, kind="Internal").ap()

    xT_r = xT.rearrange("(c p) t -> p c t", p=128)
    out_r = out_d.rearrange("(c p) t -> p c t", p=128)
    ptq_r = ptq_d.rearrange("(c p) t -> p c t", p=128)
    ptk_r = ptk_d.rearrange("(c p) t -> p c t", p=128)
    x1_r = x1_d.rearrange("(c p) t -> p c t", p=128)
    eq_dr = eq_d.rearrange("(c a) t -> a c t", a=2)
    ek_dr = ek_d.rearrange("(c a) t -> a c t", a=2)

    p = Prog()

    with ExitStack() as stk:
        def sbt(name, shape, dt=F32):
            return stk.enter_context(nc.sbuf_tensor(name, list(shape), dt))

        wq_s = sbt("wq_s", [128, NCH, D], BF16)
        wk_s = sbt("wk_s", [128, NCH, D], BF16)
        wt_s = sbt("wt_s", [128, NCH, D], BF16)
        big_s = sbt("big_s", [128, NCH, D], BF16)
        wqa_s = sbt("wqa_s", [128, NCH, H], BF16)
        wkae_s = sbt("wkae_s", [128, NCH, H], BF16)
        wkaq_s = sbt("wkaq_s", [128, NCH, H], BF16)
        weff_s = sbt("weff_s", [128, NCH, H], BF16)
        win_s = sbt("win_s", [128, NCH, FM], BF16)
        wout_s = sbt("wout_s", [FM, D], BF16)
        w2bd_s = sbt("w2bd_s", [128, NCH, FM], BF16)
        sel_s = sbt("sel_s", [FM, NCH, 128], BF16)
        i128_s = sbt("i128_s", [128, 128], BF16)
        ones_s = sbt("ones_s", [128, 4], BF16)
        onerow_s = sbt("onerow_s", [1, 128], BF16)
        rows_s = sbt("rows_s", [1, 2 * D + FM + H], BF16)
        w1f_s = sbt("w1f_s", [128, NCH]); b1raw_s = sbt("b1raw_s", [128, NCH])
        binf_s = sbt("binf_s", [128, NCH]); b1f_s = sbt("b1f_s", [128, NCH])
        bq_s = sbt("bq_s", [128, NCH]); bk_s = sbt("bk_s", [128, NCH])
        bt_s = sbt("bt_s", [128, NCH]); bout_s = sbt("bout_s", [128, NCH])
        bout2_s = sbt("bout2_s", [128, NCH]); bqbt_s = sbt("bqbt_s", [128, NCH])
        bq_bf = sbt("bq_bf", [128, NCH], BF16); bk_bf = sbt("bk_bf", [128, NCH], BF16)
        bqa_s = sbt("bqa_s", [H, 1]); bka_s = sbt("bka_s", [H, 1])
        bqa2_s = sbt("bqa2_s", [H, 1]); bka2_s = sbt("bka2_s", [H, 2])
        b2_s = sbt("b2_s", [FM, 1]); b2_bf = sbt("b2_bf", [FM, 1], BF16)
        eps_s = sbt("eps_s", [1, 1]); zero_s = sbt("zero_s", [128, 1])

        x_s = sbt("x_s", [128, 2, NCH, T])
        x_bf = sbt("x_bf", [128, 2, NCH, T], BF16)
        x1_bf = sbt("x1_bf", [128, 2, NCH, T], BF16)
        ptq_s = sbt("ptq_s", [128, NCH, T], BF16)
        ptk_s = sbt("ptk_s", [128, NCH, T], BF16)
        st_bf = sbt("st_bf", [FM, 2, T], BF16)
        h_bf = sbt("h_bf", [128, 2, T], BF16)
        g_bf = sbt("g_bf", [FM, T], BF16)
        r64_s = sbt("r64_s", [64, 2, T], BF16)
        r16_s = sbt("r16_s", [16, 2, T], BF16)
        rf_s = sbt("rf_s", [128, 2, T])
        mrow_s = sbt("mrow_s", [1, 6, T], BF16)
        stat_s = sbt("stat_s", [1, 2, 2, T])
        esc_s = sbt("esc_s", [H, 4, T])
        ee_s = sbt("ee_s", [H, 2, T], BF16)
        eb_s = sbt("eb_s", [H, 2, T], BF16)
        tiny_s = sbt("tiny_s", [H, 40])
        amask_s = sbt("amask_s", [H, NCH], BF16)
        asel_s = sbt("asel_s", [H, 128], BF16)
        esel_s = sbt("esel_s", [H, NCH, 128], BF16)
        am_bf = sbt("am_bf", [H, 4, NCH], BF16)
        pool_s = sbt("pool_s", [128, 11, NCH])
        pq_s = sbt("pq_s", [128, 2, NCH]); pk_s = sbt("pk_s", [128, 2, NCH])
        pkb_bf = sbt("pkb_bf", [128, NCH], BF16)
        btt_s = sbt("btt_s", [128, 2, NCH])

        pb = [stk.enter_context(nc.psum_tensor("pb%d" % i, [128, T], F32))
              for i in range(8)]
        PB_S1, PB_FF, PB_E0, PB_E1, PB_W0, PB_W1, PB_S2, PB_SC = range(8)

        sems = {}
        for eng in ("tensor", "vector", "scalar", "gpsimd", "sync"):
            sems[(eng, False)] = stk.enter_context(
                nc.semaphore("sem_%s_c" % eng))
        for eng in ("sync", "scalar", "gpsimd"):
            for ln in range(Prog.NLANES):
                sems[(eng, "d", ln)] = stk.enter_context(
                    nc.semaphore("sem_%s_d%d" % (eng, ln)))

        def T_op(emit, reads, writes, tag=""):
            return p.add("tensor", emit, reads, writes, tag=tag)

        def V_op(emit, reads, writes, tag=""):
            return p.add("vector", emit, reads, writes, tag=tag)

        def S_op(emit, reads, writes, tag=""):
            return p.add("scalar", emit, reads, writes, tag=tag)

        def dma(eng, emit, reads, writes, tag="", ndma=1):
            return p.add(eng, emit, reads, writes, dma=True, tag=tag, ndma=ndma)

        V_op(lambda e: [
                e.memset(onerow_s[:], 1.0),
                e.memset(ones_s[:, 0:1], 1.0),
                e.memset(ones_s[:, 1:2], -1.0 / D),
                e.memset(ones_s[:, 2:3], 1.0 / D),
                e.memset(eps_s[:], 1e-5),
                e.memset(zero_s[:], 0.0)],
             [], ["ones", "eps", "zero", "onerow"], "memset")

        def cast_load(key, dst, src):
            dma("gpsimd", lambda e: e.dma_start(out=dst, in_=src), [], [key],
                "cast_" + key)

        cast_load("wq", wq_s[:], wq_l)
        cast_load("wk", wk_s[:], wk_l)
        cast_load("wt", wt_s[:], wt_l)
        cast_load("wqT", big_s[:], wqT_l)
        cast_load("wqa", wqa_s[:], wqa_l)
        cast_load("wkae", wkae_s[:], wka_l)
        cast_load("win", win_s[:], win_l)
        cast_load("wout", wout_s[:], wout_l)
        cast_load("w2bd", w2bd_s[:], w2bd_l)

        def plain_load(key, dst, src):
            dma("sync", lambda e: e.dma_start(out=dst, in_=src), [], [key],
                "load_" + key)

        plain_load("sel", sel_s[:], sel_l)
        plain_load("amask", amask_s[:], amask_l)
        plain_load("asel", asel_s[:], asel_l)
        plain_load("esel", esel_s[:], esel_l)
        plain_load("i128", i128_s[:], i128_l)
        plain_load("w1f", w1f_s[:], w1f_l)
        plain_load("b1raw", b1raw_s[:], b1raw_l)
        plain_load("binf", binf_s[:], binf_l)
        plain_load("bq", bq_s[:], bq_l)
        plain_load("bk", bk_s[:], bk_l)
        plain_load("bt", bt_s[:], bt_l)
        plain_load("bout", bout_s[:], bout_l)
        plain_load("bqa", bqa_s[:], bqa_l)
        plain_load("bka", bka_s[:], bka_l)
        plain_load("b2", b2_s[:], b2_l)

        S_op(lambda e: e.activation(out=wkae_s[:], in_=wkae_s[:], func=AF.Copy,
                                    bias=0.0, scale=0.125),
             ["wkae"], ["wkae"], "scale_wka")
        S_op(lambda e: e.activation(out=w2bd_s[:], in_=w2bd_s[:], func=AF.Copy,
                                    bias=0.0, scale=0.5),
             ["w2bd"], ["w2bd"], "scale_w2bd")

        def b1f_mk(e):
            return [e.tensor_scalar(out=b1f_s[:, c:c + 1], in0=w1f_s[:, c:c + 1],
                                    scalar1=binf_s[:, c:c + 1], scalar2=None,
                                    op0=ALU.mult)
                    for c in range(NCH)]
        V_op(b1f_mk, ["w1f", "binf"], ["b1f_m"], "b1f_m")
        V_op(lambda e: e.tensor_add(b1f_s[:], b1f_s[:], b1raw_s[:]),
             ["b1f_m", "b1raw"], ["b1f"], "b1f")
        V_op(lambda e: e.tensor_copy(bq_bf[:], bq_s[:]), ["bq"], ["bq_bf"])
        V_op(lambda e: e.tensor_copy(bk_bf[:], bk_s[:]), ["bk"], ["bk_bf"])
        V_op(lambda e: e.tensor_copy(b2_bf[:], b2_s[:]), ["b2"], ["b2_bf"])
        V_op(lambda e: e.tensor_add(bqbt_s[:], bq_s[:], bt_s[:]),
             ["bq", "bt"], ["bqbt"])

        def weff_mm(e):
            ins = []
            for dc in range(NCH):
                for jc in range(NCH):
                    ins.append(e.matmul(
                        pb[PB_SC][:, dc * H:(dc + 1) * H],
                        big_s[:, jc, dc * 128:(dc + 1) * 128],
                        wqa_s[:, jc, :],
                        start=(jc == 0), stop=(jc == NCH - 1)))
            return ins
        T_op(weff_mm, ["wqT", "wqa"], ["pb_sc"], "weff_mm")
        S_op(lambda e: e.activation(
                out=weff_s[:].rearrange("p c h -> p (c h)"),
                in_=pb[PB_SC][:, 0:NCH * H],
                func=AF.Copy, bias=0.0, scale=0.125),
             ["pb_sc"], ["weff"], "weff")

        CSQ, CSK, CSWIN, CSWEFF = 0, D, 2 * D, 2 * D + FM

        def cs_generic(name, src, ncols, width, dst_off):
            for j in range(ncols):
                def mm(e, j=j):
                    return [e.matmul(pb[PB_SC][0:1, 0:width], ones_s[:, 0:1],
                                     src[:, kc, j * width:(j + 1) * width],
                                     start=(kc == 0), stop=(kc == NCH - 1))
                            for kc in range(NCH)]
                T_op(mm, [name, "ones"], ["pb_sc"], "cs_mm_%s%d" % (name, j))
                S_op(lambda e, j=j: e.activation(
                        out=rows_s[0:1, dst_off + j * width:dst_off + (j + 1) * width],
                        in_=pb[PB_SC][0:1, 0:width], func=AF.Copy, bias=0.0,
                        scale=1.0),
                     ["pb_sc"], ["rows"], "cs_ev_%s%d" % (name, j))

        cs_generic("wq", wq_s, NCH, 128, CSQ)
        cs_generic("wk", wk_s, NCH, 128, CSK)
        cs_generic("win", win_s, 1, FM, CSWIN)
        cs_generic("weff", weff_s, 1, H, CSWEFF)

        def bqa2_mm(e):
            return [e.matmul(pb[PB_SC][0:H, 0:1], wqa_s[:, kc, :],
                             bq_bf[:, kc:kc + 1],
                             start=(kc == 0), stop=(kc == NCH - 1))
                    for kc in range(NCH)]
        T_op(bqa2_mm, ["wqa", "bq_bf"], ["pb_sc"], "bqa2_mm")
        V_op(lambda e: e.tensor_scalar(out=bqa2_s[:], in0=pb[PB_SC][0:H, 0:1],
                                       scalar1=bqa_s[:, 0:1], scalar2=0.125,
                                       op0=ALU.add, op1=ALU.mult),
             ["pb_sc", "bqa"], ["bqa2"], "bqa2")

        def bout2_mm(e):
            return [e.matmul(pb[PB_SC][:, j:j + 1],
                             wout_s[:, j * 128:(j + 1) * 128],
                             b2_bf[:, 0:1], start=True, stop=True)
                    for j in range(NCH)]
        T_op(bout2_mm, ["wout", "b2_bf"], ["pb_sc"], "bout2_mm")
        V_op(lambda e: [e.tensor_scalar(
                out=bout2_s[:, j:j + 1], in0=pb[PB_SC][:, j:j + 1],
                scalar1=bout_s[:, j:j + 1], scalar2=0.5,
                op0=ALU.add, op1=ALU.mult) for j in range(NCH)],
             ["pb_sc", "bout"], ["bout2"], "bout2")

        def ln_stats(pba, pbb, par, mslot, stat_row, t0, tag,
                     onchip_r64=False, store=True):
            key = "stt%d" % par

            V_op(lambda e: e.tensor_copy(stat_s[0:1, par, 0, :],
                                         pb[pba][0:1, :]),
                 ["pb%d" % pba], [key + "_m"], tag + "_cm")
            V_op(lambda e: e.tensor_mul(stat_s[0:1, par, 1, :],
                                        stat_s[0:1, par, 0, :],
                                        stat_s[0:1, par, 0, :]),
                 [key + "_m"],
                 [key + "_msq", key + "_v", key + "_ln", key + "_r"],
                 tag + "_msq")
            V_op(lambda e: e.scalar_tensor_tensor(
                    out=stat_s[0:1, par, 1, :], in0=stat_s[0:1, par, 1, :],
                    scalar=-1.0, in1=pb[pbb][0:1, :],
                    op0=ALU.mult, op1=ALU.add),
                 [key + "_msq", "pb%d" % pbb], [key + "_v"], tag + "_v")
            S_op(lambda e: e.activation(out=stat_s[0:1, par, 1, :],
                                        in_=stat_s[0:1, par, 1, :], func=AF.Ln,
                                        bias=eps_s[0:1, 0:1], scale=1.0),
                 [key + "_v", "eps"], [key + "_ln"], tag + "_ln")
            S_op(lambda e: e.activation(out=stat_s[0:1, par, 1, :],
                                        in_=stat_s[0:1, par, 1, :], func=AF.Exp,
                                        bias=0.0, scale=-0.5),
                 [key + "_ln"], [key + "_r"], tag + "_r")
            V_op(lambda e: e.tensor_copy(mrow_s[0:1, mslot, :],
                                         stat_s[0:1, par, 0, :]),
                 [key + "_m"], ["mrow%d" % mslot], tag + "_mbf")
            if store:
                dma("scalar", lambda e: e.dma_start(
                        out=stat_d[stat_row:stat_row + 1, t0:t0 + T],
                        in_=stat_s[0:1, par, 1, :]),
                    [key + "_r"], ["statd%d_%d" % (stat_row, t0)], tag + "_st")
            if onchip_r64:
                V_op(lambda e: e.tensor_copy(eb_s[0:1, par, :],
                                             stat_s[0:1, par, 1, :]),
                     [key + "_r"], ["ebbf%d" % par], tag + "_rb")

        def r64_mm(par, tag):
            T_op(lambda e: e.matmul(pb[PB_SC][0:64, :],
                                    onerow_s[0:1, 0:64],
                                    eb_s[0:1, par, :],
                                    start=True, stop=True),
                 ["ebbf%d" % par, "onerow"], ["pb_sc"], tag + "_rmm")
            S_op(lambda e: e.activation(out=r64_s[:, par, :],
                                        in_=pb[PB_SC][0:64, :],
                                        func=AF.Copy, bias=0.0, scale=1.0),
                 ["pb_sc"], ["r64_%d" % par], tag + "_rcp")

        def bcast_r(stat_row, t0, dst_key, dst_ap, nparts, tag,
                    eng="gpsimd"):
            dma(eng, lambda e: e.dma_start(
                    out=dst_ap.unsqueeze(1),
                    in_=stat_d[stat_row:stat_row + 1, t0:t0 + T]
                        .partition_broadcast(nparts)),
                ["statd%d_%d" % (stat_row, t0)], [dst_key], tag)

        def ffn(xin_bf, xin_keys, mslot, reg, r64_key, fin, tag, tap=False,
                filler=None):
            def win_mm(e):
                ins = [e.matmul(pb[PB_FF][0:FM, :], win_s[:, c, :],
                                xin_bf[:, c, :], start=(c == 0), stop=False)
                       for c in range(NCH)]
                ins.append(e.matmul(pb[PB_FF][0:FM, :],
                                    rows_s[0:1, CSWIN:CSWIN + FM],
                                    mrow_s[0:1, mslot, :], start=False, stop=True))
                return ins
            T_op(win_mm, list(xin_keys) + ["mrow%d" % mslot, "win", "rows"],
                 ["pb_ff"], tag + "_win")
            V_op(lambda e: e.tensor_tensor(out=st_bf[:, reg, :],
                                           in0=pb[PB_FF][0:FM, :],
                                           in1=r64_s[:, reg, :], op=ALU.mult),
                 ["pb_ff", r64_key], ["st%d" % reg], tag + "_st")
            if tap:
                dma("gpsimd", lambda e: e.dma_start(out=dbg_d["dbg_st"],
                                                    in_=st_bf[:, reg, :]),
                    ["st%d" % reg], ["dbgst"], "dbg_st")
            for c in range(NCH):
                eb = PB_E0 if (c % 2 == 0) else PB_E1
                T_op(lambda e, c=c, eb=eb: e.matmul(
                        pb[eb][:], sel_s[:, c, :], st_bf[:, reg, :],
                        start=True, stop=True),
                     ["st%d" % reg, "sel"], ["pb_e%d" % (c % 2)],
                     tag + "_ex%d" % c)
                if filler is not None:
                    filler(c)
                S_op(lambda e, c=c, eb=eb: e.activation(
                        out=h_bf[:, c % 2, :], in_=pb[eb][:], func=AF.Relu,
                        bias=b1f_s[:, c:c + 1], scale=w1f_s[:, c:c + 1]),
                     ["pb_e%d" % (c % 2), "b1f", "w1f"],
                     ["h%d" % (c % 2)], tag + "_rl%d" % c)
                T_op(lambda e, c=c: e.matmul(
                        pb[PB_FF][0:FM, :], w2bd_s[:, c, :], h_bf[:, c % 2, :],
                        start=(c == 0), stop=(c == NCH - 1)),
                     ["h%d" % (c % 2), "w2bd"], ["pb_ff"], tag + "_bd%d" % c)
            S_op(lambda e: e.activation(out=g_bf[:], in_=pb[PB_FF][0:FM, :],
                                        func=AF.Copy, bias=0.0, scale=1.0),
                 ["pb_ff"], ["g"], tag + "_g")
            if tap:
                dma("gpsimd", lambda e: e.dma_start(out=dbg_d["dbg_g"],
                                                    in_=g_bf[:]),
                    ["g"], ["dbgg"], "dbg_g")
            for j in range(NCH):
                wb = PB_W0 if (j % 2 == 0) else PB_W1
                T_op(lambda e, j=j, wb=wb: e.matmul(
                        pb[wb][:], wout_s[:, j * 128:(j + 1) * 128], g_bf[:],
                        start=True, stop=True),
                     ["g", "wout"], ["pb_w%d" % (j % 2)], tag + "_wo%d" % j)
                fin(j, wb)

        def score_pass(si, ti, par, side, t0, tag, sc_bank=None,
                       spill=None):
            base = (si * 2 + side) * 8
            if sc_bank is None:
                sc_bank, sc_key = PB_SC, "pb_sc"
            else:
                sc_bank, sc_key = sc_bank
            rm_prev = tiny_s[:, base + (ti % 2):base + (ti % 2) + 1]
            rm_new = tiny_s[:, base + ((ti + 1) % 2):base + ((ti + 1) % 2) + 1]
            es = 2 * par + ti % 2
            kA = "escA%d" % es
            V_op(lambda e: e.tensor_tensor(
                    out=esc_s[:, es, :], in0=pb[sc_bank][0:H, :],
                    in1=r16_s[:, par, :], op=ALU.mult),
                 [sc_key, "r16_%d" % par], [kA], tag + "_sc")
            V_op(lambda e: e.reduce_max(tiny_s[:, base + 2:base + 3],
                                        esc_s[:, es, :], axis=AX.X),
                 [kA], ["tm%d" % base], tag + "_tm")
            V_op(lambda e: e.tensor_max(rm_new, rm_prev,
                                        tiny_s[:, base + 2:base + 3]),
                 ["tm%d" % base, "rm%d_%d" % (base, ti % 2)],
                 ["rm%d_%d" % (base, (ti + 1) % 2)], tag + "_nm")
            if spill is not None:
                dma("sync" if par == 0 else "scalar",
                    lambda e: e.dma_start(out=spill[:, t0:t0 + T],
                                          in_=esc_s[:, es, :]),
                    [kA], ["escd_%d" % t0], tag + "_escd")

        def eb2_final(si, side, bias2_ap, bias2_k, tag):
            base = (si * 2 + side) * 8
            rmf = base + (cfg.ntile % 2)
            V_op(lambda e: e.tensor_sub(tiny_s[:, base + 7:base + 8], bias2_ap,
                                        tiny_s[:, rmf:rmf + 1]),
                 ["rm%d_%d" % (base, cfg.ntile % 2), bias2_k],
                 ["mx%d" % base], tag + "_eb2")

        def pool_core(si, ti, par, side, pcur, t0, tag,
                      buf, buf_keys):
            base = (si * 2 + side) * 8
            es = 2 * par + ti % 2
            kA = "escA%d" % es
            kB = "escB%d" % par
            S_op(lambda e: e.activation(out=ee_s[:, par, :],
                                        in_=esc_s[:, es, :],
                                        func=AF.Exp,
                                        bias=tiny_s[:, base + 7:base + 8],
                                        scale=1.0,
                                        accum_out=tiny_s[:, base + 2:base + 3]),
                 ["mx%d" % base, kA, "tm%d" % base],
                 [kB, "ts%d" % base], tag + "_exp")
            V_op(lambda e: e.tensor_add(tiny_s[:, base + 5:base + 6],
                                        tiny_s[:, base + 5:base + 6],
                                        tiny_s[:, base + 2:base + 3]),
                 ["ts%d" % base, "S%d" % base], ["S%d" % base], tag + "_sa")
            V_op(lambda e: e.tensor_tensor(out=eb_s[:, par, :],
                                           in0=ee_s[:, par, :],
                                           in1=r16_s[:, par, :], op=ALU.mult),
                 [kB, "r16_%d" % par], ["ebbf%d" % par], tag + "_eb")
            for c in range(NCH):
                ebk = PB_E0 if c % 2 == 0 else PB_E1
                T_op(lambda e, c=c, ebk=ebk: e.matmul(
                        pb[ebk][:], esel_s[:, c, :], eb_s[:, par, :],
                        start=True, stop=True),
                     ["ebbf%d" % par, "esel"], ["pb_e%d" % (c % 2)],
                     tag + "_em%d" % c)
                V_op(lambda e, c=c, ebk=ebk: e.scalar_tensor_tensor(
                        out=h_bf[:, 0, :], in0=buf[:, c, :], scalar=1.0,
                        in1=pb[ebk][:], op0=ALU.mult, op1=ALU.mult,
                        accum_out=pool_s[:, 2, c:c + 1]),
                     list(buf_keys) + ["pb_e%d" % (c % 2)],
                     ["Ppart%d" % c, "h0"], tag + "_ttr%d" % c)
            p_update(si, side, pcur, tag)

        def p_update(si, side, pcur, tag):
            V_op(lambda e: e.tensor_add(pool_s[:, pcur, :], pool_s[:, pcur, :],
                                        pool_s[:, 2, :]),
                 ["P%d" % pcur] + ["Ppart%d" % c for c in range(NCH)],
                 ["P%d" % pcur], tag + "_pu")

        def pool_finalize(si, side, pcur, rslot, tag):
            base = (si * 2 + side) * 8
            V_op(lambda e: e.reciprocal(tiny_s[:, base + 6:base + 7],
                                        tiny_s[:, base + 5:base + 6]),
                 ["S%d" % base], ["rsv%d" % base], tag + "_rs")
            bslot = si * 2 + side
            V_op(lambda e: e.tensor_scalar(
                    out=am_bf[:, bslot, :], in0=amask_s[:],
                    scalar1=tiny_s[:, base + 6:base + 7], scalar2=None,
                    op0=ALU.mult),
                 ["rsv%d" % base, "amask"], ["am%d" % bslot], tag + "_rm")
            T_op(lambda e: e.matmul(pb[PB_S2][:, 0:NCH], asel_s[:],
                                    am_bf[:, bslot, :],
                                    start=True, stop=True),
                 ["am%d" % bslot, "asel"], ["pb%d" % PB_S2], tag + "_rmm2")
            V_op(lambda e: e.tensor_tensor(out=pool_s[:, 6, :],
                                           in0=pool_s[:, pcur, :],
                                           in1=pb[PB_S2][:, 0:NCH],
                                           op=ALU.mult),
                 ["P%d" % pcur, "pb%d" % PB_S2], ["Pmean"], tag + "_pm")

        def p1_A0(si, ti):
            par = si % 2
            t0 = si * cfg.seq + ti * T
            xk, xbk, sqk = "x%d" % par, "xbf%d" % par, "sq%d" % par
            dma("sync", lambda e: e.dma_start(out=x_s[:, par, :, :],
                                              in_=xT_r[:, :, t0:t0 + T]),
                [xk], [xk], "p1_xload")
            V_op(lambda e: e.tensor_copy(x_bf[:, par, :, :],
                                         x_s[:, par, :, :]),
                 [xk], [xbk], "p1_xcast")
            sq_ap = big_s[:, :, par * T:(par + 1) * T]
            S_op(lambda e: e.activation(out=sq_ap,
                                        in_=x_bf[:, par, :, :],
                                        func=AF.Square, bias=0.0, scale=1.0),
                 [xbk], [sqk, "wqT"], "p1_sq")

        def p1_A1(si, ti):
            par = si % 2
            t0 = si * cfg.seq + ti * T
            xk, xbk, sqk = "x%d" % par, "xbf%d" % par, "sq%d" % par
            sq_ap = big_s[:, :, par * T:(par + 1) * T]

            def sums1(e):
                ins = []
                for c in range(NCH):
                    ins.append(e.matmul(pb[PB_S1][0:1, :], ones_s[:, 1:2],
                                        x_bf[:, par, c, :],
                                        start=(c == 0), stop=(c == NCH - 1)))
                for c in range(NCH):
                    ins.append(e.matmul(pb[PB_S2][0:1, :], ones_s[:, 2:3],
                                        sq_ap[:, c, :],
                                        start=(c == 0), stop=(c == NCH - 1)))
                return ins
            T_op(sums1, [xbk, sqk, "ones"],
                 ["pb%d" % PB_S1, "pb%d" % PB_S2], "p1_sums1")
            ln_stats(PB_S1, PB_S2, par, par, 0, t0, "p1ln1_%d_%d" % (si, ti),
                     onchip_r64=True, store=False)


        def p1_A2(si, ti, prev_ti=None):
            par = si % 2
            t0 = si * cfg.seq + ti * T
            xk, xbk, sqk = "x%d" % par, "xbf%d" % par, "sq%d" % par
            sq_ap = big_s[:, :, par * T:(par + 1) * T]
            filler = None
            if prev_ti is not None:
                filler = lambda c: p1_D3_j(si, prev_ti, c)
            r64_mm(par, "p1r64_%d_%d" % (si, ti))

            def fin(j, wb):
                V_op(lambda e, j=j, wb=wb: e.scalar_tensor_tensor(
                        out=x1_bf[:, par, j, :], in0=pb[wb][:],
                        scalar=bout2_s[:, j:j + 1], in1=x_s[:, par, j, :],
                        op0=ALU.add, op1=ALU.add),
                     ["pb_w%d" % (j % 2), xk, "bout2"],
                     ["x1bf%d_%d" % (par, j)], "p1_x1_%d" % j)
            ffn(x_bf[:, par, :, :], [xbk], par, par, "r64_%d" % par, fin,
                "p1f_%d_%d" % (si, ti), tap=(dbg and si == 0 and ti == 0),
                filler=filler)

            x1keys = ["x1bf%d_%d" % (par, j) for j in range(NCH)]
            dma("scalar", lambda e: e.dma_start(out=x1_r[:, :, t0:t0 + T],
                                                in_=x1_bf[:, par, :, :]),
                x1keys, ["x1d_%d" % t0], "p1_x1sp")
            if dbg and si == 0 and ti == 0:
                dma("gpsimd", lambda e: e.dma_start(
                        out=dbg_d["dbg_x1"].rearrange("(c p) t -> p c t", p=128),
                        in_=x1_bf[:, par, :, :]),
                    x1keys, ["dbgx1"], "dbg_x1")

        def p1_A4(si, ti):
            par = si % 2
            t0 = si * cfg.seq + ti * T
            xk, xbk, sqk = "x%d" % par, "xbf%d" % par, "sq%d" % par
            sq_ap = big_s[:, :, par * T:(par + 1) * T]
            x1keys = ["x1bf%d_%d" % (par, j) for j in range(NCH)]
            S_op(lambda e: e.activation(out=sq_ap,
                                        in_=x1_bf[:, par, :, :],
                                        func=AF.Square, bias=0.0, scale=1.0),
                 x1keys, [sqk, "wqT"], "p1_sq1")

            def sums2(e):
                ins = []
                for c in range(NCH):
                    ins.append(e.matmul(pb[PB_S1][0:1, :], ones_s[:, 1:2],
                                        x1_bf[:, par, c, :],
                                        start=(c == 0), stop=(c == NCH - 1)))
                for c in range(NCH):
                    ins.append(e.matmul(pb[PB_S2][0:1, :], ones_s[:, 2:3],
                                        sq_ap[:, c, :],
                                        start=(c == 0), stop=(c == NCH - 1)))
                return ins
            T_op(sums2, x1keys + [sqk, "ones"],
                 ["pb%d" % PB_S1, "pb%d" % PB_S2], "p1_sums2")
            ln_stats(PB_S1, PB_S2, par, 2 + par, 1, t0, "p1ln2_%d_%d" % (si, ti))
            bcast_r(1, t0, "r16_%d" % par, r16_s[:, par, :], H, "p1_r16")

        BANKS = [(PB_W0, "pb_w0"), (PB_W1, "pb_w1"),
                 (PB_E0, "pb_e0"), (PB_E1, "pb_e1")]

        def p1_D1(si, ti):
            par = si % 2
            t0 = si * cfg.seq + ti * T
            x1keys = ["x1bf%d_%d" % (par, j) for j in range(NCH)]

            def sc_mm(e):
                ins = [e.matmul(pb[PB_SC][0:H, :], weff_s[:, kc, :],
                                x1_bf[:, par, kc, :],
                                start=(kc == 0), stop=False)
                       for kc in range(NCH)]
                ins.append(e.matmul(pb[PB_SC][0:H, :],
                                    rows_s[0:1, CSWEFF:CSWEFF + H],
                                    mrow_s[0:1, 2 + par, :],
                                    start=False, stop=True))
                return ins
            T_op(sc_mm, x1keys + ["mrow%d" % (2 + par), "weff", "rows"],
                 ["pb_sc"], "p1_scq")
            score_pass(si, ti, par, 0, t0, "p1sm_%d_%d" % (si, ti),
                       spill=escq_d)
            if dbg and si == 0 and ti == 0:
                dma("sync", lambda e: e.dma_start(out=dbg_d["dbg_sc"],
                                                  in_=esc_s[:, 2 * par, :]),
                    ["escA%d" % (2 * par)], ["dbgsc"], "dbg_sc")

        def p1_D2(si, ti):
            par = si % 2
            t0 = si * cfg.seq + ti * T
            x1keys = ["x1bf%d_%d" % (par, j) for j in range(NCH)]
            for j in range(NCH):
                wb, wkey = BANKS[j % 4]

                def wq_mm(e, j=j, wb=wb):
                    ins = [e.matmul(pb[wb][:],
                                    wq_s[:, kc, j * 128:(j + 1) * 128],
                                    x1_bf[:, par, kc, :],
                                    start=(kc == 0), stop=False)
                           for kc in range(NCH)]
                    ins.append(e.matmul(
                        pb[wb][:], rows_s[0:1, CSQ + j * 128:CSQ + (j + 1) * 128],
                        mrow_s[0:1, 2 + par, :], start=False, stop=True))
                    return ins
                T_op(wq_mm, x1keys + ["mrow%d" % (2 + par), "wq", "rows"],
                     [wkey], "p1_wq%d" % j)
                V_op(lambda e, j=j, wb=wb: e.tensor_copy(ptq_s[:, j, :],
                                                         pb[wb][:]),
                     [wkey], ["ptq_%d" % j], "p1_ptq%d" % j)

            ptqkeys = ["ptq_%d" % j for j in range(NCH)]
            dma("sync", lambda e: e.dma_start(out=ptq_r[:, :, t0:t0 + T],
                                              in_=ptq_s[:, :, :]),
                ptqkeys, ["ptqd_%d" % t0], "p1_ptqsp")
            if dbg and si == 0 and ti == 0:
                dma("gpsimd", lambda e: e.dma_start(
                        out=dbg_d["dbg_ptq"].rearrange("(c p) t -> p c t", p=128),
                        in_=ptq_s[:, :, :]),
                    ptqkeys, ["dbgptq"], "dbg_ptq")

        def p1_D3_j(si, ti, j):
            par = si % 2
            x1keys = ["x1bf%d_%d" % (par, j2) for j2 in range(NCH)]
            for j in (j,):
                wb, wkey = BANKS[j % 2]

                def wk_mm(e, j=j, wb=wb):
                    ins = [e.matmul(pb[wb][:],
                                    wk_s[:, kc, j * 128:(j + 1) * 128],
                                    x1_bf[:, par, kc, :],
                                    start=(kc == 0), stop=False)
                           for kc in range(NCH)]
                    ins.append(e.matmul(
                        pb[wb][:], rows_s[0:1, CSK + j * 128:CSK + (j + 1) * 128],
                        mrow_s[0:1, 2 + par, :], start=False, stop=True))
                    return ins
                T_op(wk_mm, x1keys + ["mrow%d" % (2 + par), "wk", "rows"],
                     [wkey], "p1_wk%d" % j)
                V_op(lambda e, j=j, wb=wb: e.tensor_copy(ptk_s[:, j, :],
                                                         pb[wb][:]),
                     [wkey], ["ptk_%d" % j], "p1_ptk%d" % j)

        def p1_D3sp(si, ti):
            t0 = si * cfg.seq + ti * T
            ptkkeys = ["ptk_%d" % j for j in range(NCH)]
            dma("sync", lambda e: e.dma_start(out=ptk_r[:, :, t0:t0 + T],
                                              in_=ptk_s[:, :, :]),
                ptkkeys, ["ptkd_%d" % t0], "p1_ptksp")

        def tile_buf(primary, par, ti):
            if ti % 2 == 0:
                return primary
            ap = big_s[:, :, par * T:(par + 1) * T]
            keys = (["sq0", "u", "wqT"] if par == 0
                    else ["sq1", "sqP3", "wqT"])
            return ap, keys

        def pbq_buf(par, ti):
            primary = ((ptq_s, ["ptq_%d" % j for j in range(NCH)])
                       if par == 0
                       else (ptk_s, ["ptk_%d" % j for j in range(NCH)]))
            return tile_buf(primary, par, ti)

        def pbq_load(si, ti):
            par = si % 2
            t0 = si * cfg.seq + ti * T
            ring = "sync" if par == 0 else "scalar"
            buf, bkeys = pbq_buf(par, ti)
            dma(ring, lambda e: e.dma_start(out=buf[:, :, :],
                                            in_=ptq_r[:, :, t0:t0 + T]),
                ["ptqd_%d" % t0] + bkeys, bkeys, "pbq_pld")
            bcast_r(1, t0, "r16_%d" % par, r16_s[:, par, :], H, "pbq_r16")

        def pbq_chain(si, ti):
            par = si % 2
            t0 = si * cfg.seq + ti * T
            buf, bkeys = pbq_buf(par, ti)
            ring = "sync" if par == 0 else "scalar"
            es = 2 * par + ti % 2
            dma(ring, lambda e: e.dma_start(out=esc_s[:, es, :],
                                            in_=escq_d[:, t0:t0 + T]),
                ["escd_%d" % t0, "escA%d" % es], ["escA%d" % es],
                "pbq_escld")
            pool_core(si, ti, par, 0, q_slot(si), t0,
                      "pbq_%d_%d" % (si, ti), buf, bkeys)


        def q_slot(si):
            return 0 if si == 0 else 8

        def k_slot(si):
            return 3 if si == 0 else 9

        def p1_init(si):
            base_q = (si * 2 + 0) * 8
            qs_ = q_slot(si)
            V_op(lambda e: [
                    e.memset(tiny_s[:, base_q + 0:base_q + 1], NEG_INIT),
                    e.memset(tiny_s[:, base_q + 5:base_q + 6], 0.0),
                    e.memset(pool_s[:, qs_, :], 0.0)],
                 ["P%d" % qs_, "S%d" % base_q],
                 ["rm%d_0" % base_q, "S%d" % base_q, "P%d" % qs_], "p1_init")

        def p1_fin(si):
            pool_finalize(si, 0, q_slot(si), 2 * si + 0, "p1pf_%d" % si)
            V_op(lambda e: e.tensor_add(pq_s[:, si, :], pool_s[:, 6, :],
                                        bq_s[:]),
                 ["Pmean", "bq"], ["pq%d" % si], "p1_pq")
            if dbg and si == 0:
                dma("sync", lambda e: e.dma_start(out=dbg_d["dbg_pq"],
                                                  in_=pq_s[:, si, :]),
                    ["pq%d" % si], ["dbgpq"], "dbg_pq")

            wkaq_buf = wkaq_s if si == 0 else wqa_s
            wkaq_key = "wkaq" if si == 0 else "wqa"
            V_op(lambda e: [e.tensor_scalar(
                    out=wkaq_buf[:, c, :], in0=wkae_s[:, c, :],
                    scalar1=pq_s[:, si, c:c + 1], scalar2=None, op0=ALU.mult)
                    for c in range(NCH)],
                 ["wkae", "pq%d" % si], [wkaq_key], "p1_wkaq")
            V_op(lambda e: [e.tensor_scalar(
                    out=pkb_bf[:, c:c + 1], in0=bk_s[:, c:c + 1],
                    scalar1=pq_s[:, si, c:c + 1], scalar2=None, op0=ALU.mult)
                    for c in range(NCH)],
                 ["bk", "pq%d" % si], ["pkb"], "p1_pqbk")

            def bka2_mm(e):
                return [e.matmul(pb[PB_SC][0:H, 0:1], wkaq_buf[:, kc, :],
                                 pkb_bf[:, kc:kc + 1],
                                 start=(kc == 0), stop=(kc == NCH - 1))
                        for kc in range(NCH)]
            T_op(bka2_mm, [wkaq_key, "pkb"], ["pb_sc"], "p1_bka2mm")
            V_op(lambda e: e.tensor_scalar(out=bka2_s[:, si:si + 1],
                                           in0=bka_s[:, 0:1],
                                           scalar1=0.125, scalar2=None,
                                           op0=ALU.mult),
                 ["bka"], ["bka2a"], "p1_bka2a")
            V_op(lambda e: e.tensor_add(bka2_s[:, si:si + 1],
                                        bka2_s[:, si:si + 1],
                                        pb[PB_SC][0:H, 0:1]),
                 ["pb_sc", "bka2a"], ["bka2"], "p1_bka2")

        def p2_buf(par, ti):
            primary = ((ptq_s, ["ptq_%d" % j for j in range(NCH)])
                       if par == 0
                       else (ptk_s, ["ptk_%d" % j for j in range(NCH)]))
            return tile_buf(primary, par, ti)

        def p2_load(si, ti, tag):
            par = si % 2
            t0 = si * cfg.seq + ti * T
            ring = "sync" if par == 0 else "scalar"
            buf, bkeys = p2_buf(par, ti)
            dma(ring, lambda e: e.dma_start(out=buf[:, :, :],
                                            in_=ptk_r[:, :, t0:t0 + T]),
                ["ptkd_%d" % t0] + bkeys, bkeys, tag + "_ld")
            bcast_r(1, t0, "r16_%d" % par, r16_s[:, par, :], H, tag + "_r16")

        def p2_scmm(si, par, buf, bkeys, tag):
            scb = (PB_SC, "pb_sc") if par == 0 else (PB_S1, "pb%d" % PB_S1)
            wkaq_buf = wkaq_s if si == 0 else wqa_s
            wkaq_key = "wkaq" if si == 0 else "wqa"

            def sc_mm(e):
                return [e.matmul(pb[scb[0]][0:H, :], wkaq_buf[:, kc, :],
                                 buf[:, kc, :],
                                 start=(kc == 0), stop=(kc == NCH - 1))
                        for kc in range(NCH)]
            T_op(sc_mm, bkeys + [wkaq_key], [scb[1]], tag + "_sck")
            return scb

        def p2a_chain(si, ti):
            par = si % 2
            t0 = si * cfg.seq + ti * T
            buf, bkeys = p2_buf(par, ti)
            scb = p2_scmm(si, par, buf, bkeys, "p2a_%d_%d" % (si, ti))
            score_pass(si, ti, par, 1, t0, "p2a_%d_%d" % (si, ti), sc_bank=scb)

        def p2b_chain(si, ti):
            par = si % 2
            t0 = si * cfg.seq + ti * T
            buf, bkeys = p2_buf(par, ti)
            scb = p2_scmm(si, par, buf, bkeys, "p2b_%d_%d" % (si, ti))
            es = 2 * par + ti % 2
            kA = "escA%d" % es
            V_op(lambda e: e.tensor_tensor(
                    out=esc_s[:, es, :], in0=pb[scb[0]][0:H, :],
                    in1=r16_s[:, par, :], op=ALU.mult),
                 [scb[1], "r16_%d" % par], [kA], "p2b_%d_%d_sc" % (si, ti))
            pool_core(si, ti, par, 1, k_slot(si), t0,
                      "p2b_%d_%d" % (si, ti), buf, bkeys)

        def p2_init(si):
            base_k = (si * 2 + 1) * 8
            ks_ = k_slot(si)
            V_op(lambda e: [
                    e.memset(tiny_s[:, base_k + 0:base_k + 1], NEG_INIT),
                    e.memset(tiny_s[:, base_k + 5:base_k + 6], 0.0),
                    e.memset(pool_s[:, ks_, :], 0.0)],
                 ["P%d" % ks_, "S%d" % base_k],
                 ["rm%d_0" % base_k, "S%d" % base_k, "P%d" % ks_], "p2_init")

        def p2_fin(si):
            pool_finalize(si, 1, k_slot(si), 2 * si + 1, "p2pf_%d" % si)
            V_op(lambda e: e.tensor_add(pool_s[:, 7, :], pool_s[:, 6, :],
                                        bk_s[:]),
                 ["Pmean", "bk"], ["pmk"], "p2_pmk")
            V_op(lambda e: e.tensor_tensor(out=pk_s[:, si, :],
                                           in0=pool_s[:, 7, :],
                                           in1=pq_s[:, si, :], op=ALU.mult),
                 ["pmk", "pq%d" % si], ["pk%d" % si], "p2_pk")
            if dbg and si == 0:
                dma("sync", lambda e: e.dma_start(out=dbg_d["dbg_pk"],
                                                  in_=pk_s[:, si, :]),
                    ["pk%d" % si], ["dbgpk"], "dbg_pk")
                dma("sync", lambda e: e.dma_start(out=dbg_d["dbg_Pk"],
                                                  in_=pool_s[:, k_slot(si), :]),
                    ["P%d" % k_slot(si)], ["dbgPk"], "dbg_Pk")
                dma("sync", lambda e: e.dma_start(out=dbg_d["dbg_Sk"],
                                                  in_=tiny_s[:, (si*2+1)*8:
                                                             (si*2+1)*8+8]),
                    ["S%d" % ((si*2+1)*8)], ["dbgSk"], "dbg_Sk")
            V_op(lambda e: [e.tensor_scalar(
                    out=pkb_bf[:, c:c + 1], in0=bq_s[:, c:c + 1],
                    scalar1=pk_s[:, si, c:c + 1], scalar2=None, op0=ALU.mult)
                    for c in range(NCH)],
                 ["bq", "pk%d" % si], ["pkb"], "p2_pkbq")

            def btt_mm(e):
                ins = []
                for j in range(NCH):
                    for kc in range(NCH):
                        ins.append(e.matmul(
                            pb[PB_SC][:, j:j + 1],
                            wt_s[:, kc, j * 128:(j + 1) * 128],
                            pkb_bf[:, kc:kc + 1],
                            start=(kc == 0), stop=(kc == NCH - 1)))
                return ins
            T_op(btt_mm, ["pkb", "wt"], ["pb_sc"], "p2_bttmm")
            V_op(lambda e: e.tensor_add(btt_s[:, si, :], pb[PB_SC][:, 0:NCH],
                                        bqbt_s[:]),
                 ["pb_sc", "bqbt"], ["btt%d" % si], "p2_btt")
            if dbg and si == 0:
                dma("sync", lambda e: e.dma_start(out=dbg_d["dbg_btt"],
                                                  in_=btt_s[:, si, :]),
                    ["btt%d" % si], ["dbgbtt"], "dbg_btt")

        def p3_bufs(si, ti):
            pp = ti % 2
            if pp == 0:
                qb = (ptq_s, ["p3ptq0"] + ["ptq_%d" % j for j in range(NCH)])
            else:
                qb = (ptk_s, ["p3ptq1"] + ["ptk_%d" % j for j in range(NCH)])
            x1b = (x1_bf[:, pp], ["p3x1%d" % pp])
            return pp, qb, x1b

        def p3_wtscale(si, reload):
            if reload:
                dma("gpsimd", lambda e: e.dma_start(out=wt_s[:], in_=wt_l),
                    ["wt"], ["wt"], "p3_wtrl")
            V_op(lambda e: [e.tensor_scalar(
                    out=wt_s[:, kc, :], in0=wt_s[:, kc, :],
                    scalar1=pk_s[:, si, kc:kc + 1], scalar2=None,
                    op0=ALU.mult) for kc in range(NCH)],
                 ["wt", "pk%d" % si], ["wt"], "p3_wtsc")

        def p3_load(si, ti):
            t0 = si * cfg.seq + ti * T
            pp, (qbuf, qkeys), (x1buf, x1keysb) = p3_bufs(si, ti)
            dma("sync", lambda e: e.dma_start(out=qbuf[:, :, :],
                                              in_=ptq_r[:, :, t0:t0 + T]),
                ["ptqd_%d" % t0] + qkeys, qkeys, "p3_ptqld")
            dma("sync", lambda e: e.dma_start(out=x1buf[:, :, :],
                                              in_=x1_r[:, :, t0:t0 + T]),
                ["x1d_%d" % t0] + x1keysb, x1keysb, "p3_x1ld")
            dma("scalar", lambda e: e.dma_start(
                    out=rf_s[:, pp, :].unsqueeze(1),
                    in_=stat_d[1:2, t0:t0 + T].partition_broadcast(128)),
                ["statd1_%d" % t0, "rf%d" % pp], ["rf%d" % pp], "p3_rf")

        def p3_wt_j(si, ti, j):
            pp, (qbuf, qkeys), _ = p3_bufs(si, ti)
            wb = PB_W0 if (j % 2 == 0) else PB_W1

            def wt_mm(e, j=j, wb=wb):
                ins = [e.matmul(pb[wb][:],
                                wt_s[:, kc, j * 128:(j + 1) * 128],
                                qbuf[:, kc, :],
                                start=(kc == 0), stop=False)
                       for kc in range(NCH)]
                ins.append(e.matmul(pb[wb][:], i128_s[:],
                                    qbuf[:, j, :],
                                    start=False, stop=True))
                return ins
            T_op(wt_mm, qkeys + ["wt", "i128"], ["pb_w%d" % (j % 2)],
                 "p3_wt%d" % j)

        def p3_vx2(si, ti, j):
            pp, (qbuf, qkeys), (x1buf, x1keysb) = p3_bufs(si, ti)
            x1k = x1keysb[0]
            v_s = x_s[:, 1, 0:2]
            x2_ap = x_bf[:, pp]
            wb = PB_W0 if (j % 2 == 0) else PB_W1
            V_op(lambda e, j=j, wb=wb, pp=pp: e.tensor_tensor(
                    out=v_s[:, j % 2, :], in0=pb[wb][:],
                    in1=rf_s[:, pp, :], op=ALU.mult),
                 ["pb_w%d" % (j % 2), "rf%d" % pp],
                 ["v%d" % (j % 2), "x1"], "p3_v%d" % j)
            V_op(lambda e, j=j: e.scalar_tensor_tensor(
                    out=x2_ap[:, j, :], in0=v_s[:, j % 2, :],
                    scalar=btt_s[:, si, j:j + 1], in1=x1buf[:, j, :],
                    op0=ALU.add, op1=ALU.add),
                 ["v%d" % (j % 2), x1k, "btt%d" % si],
                 ["x2_%d_%d" % (pp, j), "xbf%d" % pp], "p3_x2%d" % j)
            S_op(lambda e, j=j: e.activation(
                    out=big_s[:, j, T:2 * T], in_=x2_ap[:, j, :],
                    func=AF.Square, bias=0.0, scale=1.0),
                 ["x2_%d_%d" % (pp, j)],
                 ["sqc%d" % j, "sq1", "sqP3", "wqT"], "p3_sq%d" % j)

        def mk_p3_filler(si, ti):
            def filler(c):
                if c >= 2:
                    p3_vx2(si, ti, c - 2)
                p3_wt_j(si, ti, c)
                if c == NCH - 1:
                    p3_vx2(si, ti, NCH - 2)
                    p3_vx2(si, ti, NCH - 1)
            return filler

        def p3_wtblock(si, ti):
            for c in range(NCH):
                if c >= 2:
                    p3_vx2(si, ti, c - 2)
                p3_wt_j(si, ti, c)
            p3_vx2(si, ti, NCH - 2)
            p3_vx2(si, ti, NCH - 1)

        def p3_stats(si, ti):
            pp = ti % 2
            t0 = si * cfg.seq + ti * T
            x2_ap = x_bf[:, pp]
            x2keys = ["x2_%d_%d" % (pp, j) for j in range(NCH)]
            sq_ap = big_s[:, :, T:2 * T]
            if dbg and si == 0 and ti == 0:
                dma("gpsimd", lambda e: e.dma_start(
                        out=dbg_d["dbg_x2"].rearrange("(c p) t -> p c t", p=128),
                        in_=x2_ap[:]),
                    x2keys, ["dbgx2"], "dbg_x2")

            def sums3a(e):
                return [e.matmul(pb[PB_S1][0:1, :], ones_s[:, 1:2],
                                 x2_ap[:, c, :],
                                 start=(c == 0), stop=(c == NCH - 1))
                        for c in range(NCH)]
            T_op(sums3a, x2keys + ["ones"], ["pb%d" % PB_S1], "p3_sums3a")

            def sums3b(e):
                return [e.matmul(pb[PB_S2][0:1, :], ones_s[:, 2:3],
                                 sq_ap[:, c, :],
                                 start=(c == 0), stop=(c == NCH - 1))
                        for c in range(NCH)]
            T_op(sums3b, ["sqc%d" % c for c in range(NCH)] + ["ones"],
                 ["pb%d" % PB_S2], "p3_sums3b")
            ln_stats(PB_S1, PB_S2, pp, 4 + pp, 2, t0,
                     "p3ln_%d_%d" % (si, ti), onchip_r64=True, store=False)

        def p3_B(si, ti, filler=None):
            pp = ti % 2
            t0 = si * cfg.seq + ti * T
            out_ap = x_s[:, 0]
            x2_ap = x_bf[:, pp]
            x2keys = ["x2_%d_%d" % (pp, j) for j in range(NCH)]
            r64_mm(pp, "p3r64_%d_%d" % (si, ti))

            def fin(j, wb):
                V_op(lambda e, j=j, wb=wb: e.scalar_tensor_tensor(
                        out=out_ap[:, j, :], in0=pb[wb][:],
                        scalar=bout2_s[:, j:j + 1], in1=x2_ap[:, j, :],
                        op0=ALU.add, op1=ALU.add),
                     ["pb_w%d" % (j % 2), "x2_%d_%d" % (pp, j), "bout2",
                      "outst"],
                     ["outb_%d" % j, "x0"], "p3_out%d" % j)
            ffn(x2_ap[:], x2keys, 4 + pp, pp, "r64_%d" % pp, fin,
                "p3f_%d_%d" % (si, ti), filler=filler)
            outkeys = ["outb_%d" % j for j in range(NCH)]
            dma("sync", lambda e: e.dma_start(out=out_r[:, :, t0:t0 + T],
                                              in_=out_ap[:]),
                outkeys, ["outst"], "p3_store")

        seqs = list(range(cfg.nseq))
        for si in seqs:
            p1_init(si)
        for si in seqs:
            p1_A0(si, 0)
        for ti in range(cfg.ntile):
            for si in seqs:
                p1_A1(si, ti)
                if not only_p1 and ti > 0:
                    p1_D1(si, ti - 1)
                    p1_D2(si, ti - 1)
                p1_A2(si, ti, prev_ti=(ti - 1 if (not only_p1 and ti > 0)
                                       else None))
                if not only_p1 and ti > 0:
                    p1_D3sp(si, ti - 1)
                p1_A4(si, ti)
                if ti + 1 < cfg.ntile:
                    p1_A0(si, ti + 1)
        if not only_p1:
            ti = cfg.ntile - 1
            for si in seqs:
                p1_D1(si, ti)
                p1_D2(si, ti)
                for c in range(NCH):
                    p1_D3_j(si, ti, c)
                p1_D3sp(si, ti)
            eb2_final(0, 0, bqa2_s[:], "bqa2", "p1eb2_0")
            eb2_final(1, 0, bqa2_s[:], "bqa2", "p1eb2_1")
            for ti in range(cfg.ntile + 1):
                if ti < cfg.ntile:
                    pbq_load(0, ti)
                if ti > 0:
                    pbq_chain(0, ti - 1)
            p1_fin(0)
            p2_init(0)
            for ti in range(cfg.ntile + 1):
                if ti < cfg.ntile:
                    pbq_load(1, ti)
                    p2_load(0, ti, "p2a_0_%d" % ti)
                if ti > 0:
                    pbq_chain(1, ti - 1)
                    p2a_chain(0, ti - 1)
            p1_fin(1)
            p2_init(1)
            eb2_final(0, 1, bka2_s[:, 0:1], "bka2", "p2eb2_0")
            for ti in range(cfg.ntile + 1):
                if ti < cfg.ntile:
                    p2_load(1, ti, "p2a_1_%d" % ti)
                    p2_load(0, ti, "p2b_0_%d" % ti)
                if ti > 0:
                    p2a_chain(1, ti - 1)
                    p2b_chain(0, ti - 1)
            eb2_final(1, 1, bka2_s[:, 1:2], "bka2", "p2eb2_1")
            p2_fin(0)
            for ti in range(cfg.ntile + 1):
                if ti < cfg.ntile:
                    p2_load(1, ti, "p2b_1_%d" % ti)
                if ti > 0:
                    p2b_chain(1, ti - 1)
            p2_fin(1)
            for sidx, si in enumerate(seqs):
                p3_wtscale(si, reload=(sidx > 0))
                p3_load(si, 0)
                for ti in range(cfg.ntile):
                    if ti + 1 < cfg.ntile:
                        p3_load(si, ti + 1)
                    if ti == 0:
                        p3_wtblock(si, 0)
                    else:
                        p3_B(si, ti - 1, filler=mk_p3_filler(si, ti))
                    p3_stats(si, ti)
                p3_B(si, cfg.ntile - 1)
        else:
            def cpout(e):
                return e.dma_start(out=out_d.rearrange("(c p) t -> p c t", p=128)[:, :, 0:T],
                                   in_=x1_bf[:, (cfg.ntile - 1) % 2, :, :])
            dma("gpsimd", cpout,
                ["x1bf%d_%d" % ((cfg.ntile - 1) % 2, j) for j in range(NCH)],
                ["outst"], "p1only_out")

        go = stk.enter_context(nc.semaphore("sem_go"))
        clear_sems = list(sems.values()) + [go]
        run_stream, counters, lane_occ = p.finalize(sems, paranoid=paranoid)
        with nc.Block() as block:
            block.sync(lambda e: run_stream("sync", e, go=go))
            block.tensor(lambda e: run_stream("tensor", e, go=go))
            block.vector(lambda e: run_stream("vector", e, go=go))
            block.scalar(lambda e: run_stream("scalar", e, go=go))
            block.gpsimd(lambda e: run_stream("gpsimd", e,
                                              clear_sems=clear_sems, go=go))
            nc.all_engine_barrier()
            for s in clear_sems:
                nc.gpsimd.sem_clear(s)

    return nc


def _weight_maps(W):
    def chunked(a):
        return np.ascontiguousarray(
            np.asarray(a, np.float32).reshape(NCH, 128, -1).transpose(1, 0, 2))

    W1 = np.asarray(W["L0_ffn_W1"], np.float32)
    W2 = np.asarray(W["L0_ffn_W2"], np.float32)
    b1 = np.asarray(W["L0_ffn_b1"], np.float32)

    w2bd = np.zeros((128, NCH, FM), np.float32)
    for c in range(NCH):
        for ml in range(NCH):
            w2bd[ml * FI:(ml + 1) * FI, c, NCH * c + ml] = W2[NCH * c + ml, :]
    sel = np.zeros((FM, NCH, 128), np.float32)
    for c in range(NCH):
        for pp in range(128):
            sel[NCH * c + pp // FI, c, pp] = 1.0

    def col(a, n=128):
        return np.ascontiguousarray(
            np.asarray(a, np.float32).reshape(NCH, n).T)

    m = {
        "wq_l": chunked(W["L0_attn_Wq"]),
        "wk_l": chunked(W["L0_attn_Wk"]),
        "wt_l": chunked(W["L0_attn_Wt"]),
        "wqT_l": chunked(np.asarray(W["L0_attn_Wq"], np.float32).T),
        "wqa_l": chunked(W["L0_attn_Wqa"]),
        "wka_l": chunked(W["L0_attn_Wka"]),
        "win_l": chunked(W["L0_ffn_Win"]),
        "wout_l": np.ascontiguousarray(np.asarray(W["L0_ffn_Wout"], np.float32)),
        "w2bd_l": w2bd,
        "sel_l": sel.astype(BF16_NP),
        "i128_l": np.eye(128, dtype=BF16_NP),
        "amask_l": np.equal(np.arange(H)[:, None] // 2,
                            np.arange(NCH)[None, :]).astype(BF16_NP),
        "asel_l": np.equal(np.arange(H)[:, None] % 2,
                           np.arange(128)[None, :] // 64).astype(BF16_NP),
        "esel_l": np.equal(np.arange(H)[:, None, None],
                           2 * np.arange(NCH)[None, :, None]
                           + np.arange(128)[None, None, :] // 64)
                    .astype(BF16_NP),
        "w1f_l": np.ascontiguousarray(W1.reshape(-1).reshape(NCH, 128).T),
        "b1raw_l": np.ascontiguousarray(b1.reshape(-1).reshape(NCH, 128).T),
        "binf_l": np.ascontiguousarray(
            np.repeat(np.asarray(W["L0_ffn_bin_"], np.float32), FI)
            .reshape(NCH, 128).T),
        "bq_l": col(W["L0_attn_bq"]),
        "bk_l": col(W["L0_attn_bk"]),
        "bt_l": col(W["L0_attn_bt"]),
        "bout_l": col(W["L0_ffn_bout"]),
        "bqa_l": np.ascontiguousarray(
            np.asarray(W["L0_attn_bqa"], np.float32).reshape(H, 1)),
        "bka_l": np.ascontiguousarray(
            np.asarray(W["L0_attn_bka"], np.float32).reshape(H, 1)),
        "b2_l": np.ascontiguousarray(
            np.asarray(W["L0_ffn_b2"], np.float32).reshape(FM, 1)),
    }
    return m


_NC_CACHE = {}


def kernel(**inputs):
    x = np.asarray(inputs["L0_attn_hidden_states"], np.float32)
    B, S, Dm = x.shape
    assert Dm == D
    ncores = 8
    per = B // ncores
    cfg = CFG(per, S, 512)

    key = (per, S)
    if key not in _NC_CACHE:
        _NC_CACHE[key] = build_nc(cfg)
    nc = _NC_CACHE[key]

    wm = _weight_maps(inputs)
    in_maps = []
    for c in range(ncores):
        m = dict(wm)
        m["xT"] = np.ascontiguousarray(
            x[c * per:(c + 1) * per].transpose(2, 0, 1).reshape(D, per * S))
        in_maps.append(m)

    res = run_bass_kernel_spmd(nc, in_maps, core_ids=list(range(ncores)))
    outs = []
    for c in range(ncores):
        o = np.asarray(res.results[c]["out"])
        outs.append(o.reshape(D, per, S).transpose(1, 2, 0))
    return np.ascontiguousarray(np.concatenate(outs, axis=0),
                                dtype=np.float32)


# version 26
# speedup vs baseline: 1.0186x; 1.0186x over previous
import numpy as np
import ml_dtypes
from contextlib import ExitStack

import concourse.bass as bass
import concourse.mybir as mybir
from concourse.bass_utils import run_bass_kernel_spmd

F32 = mybir.dt.float32
BF16 = mybir.dt.bfloat16
AF = mybir.ActivationFunctionType
ALU = mybir.AluOpType
AX = mybir.AxisListType
BF16_NP = ml_dtypes.bfloat16

D = 1024
NCH = 8
H = 16
DH = 64
FM = 64
FI = 16
NEG_INIT = -30.0


class CFG:
    def __init__(self, nseq, seq, T):
        assert seq % T == 0
        self.nseq = nseq
        self.seq = seq
        self.T = T
        self.ntile = seq // T
        self.tok = nseq * seq


class _Op:
    __slots__ = ("eng", "emit", "deps", "is_dma", "need_inc", "val", "idx",
                 "tag", "ndma", "lanes")


class Prog:
    def __init__(self):
        self.ops = []
        self.bufs = {}

    def add(self, eng, emit, reads=(), writes=(), deps=(), dma=False, tag="",
            ndma=1):
        dl = list(deps)
        for k in reads:
            b = self.bufs.get(k)
            if b is not None and b["w"] is not None:
                dl.append(b["w"])
        for k in writes:
            b = self.bufs.get(k)
            if b is not None:
                if b["w"] is not None:
                    dl.append(b["w"])
                dl.extend(b["r"])
        op = _Op()
        op.eng = eng
        op.emit = emit
        op.is_dma = dma
        op.need_inc = False
        op.val = 0
        op.idx = len(self.ops)
        op.tag = tag
        op.ndma = ndma
        seen = set()
        op.deps = []
        for d in dl:
            if d is not None and id(d) not in seen:
                seen.add(id(d))
                op.deps.append(d)
        self.ops.append(op)
        for k in reads:
            self.bufs.setdefault(k, {"w": None, "r": []})["r"].append(op)
        for k in writes:
            self.bufs[k] = {"w": op, "r": []}
        return op

    @staticmethod
    def _needs_wait(dep, op):
        return True

    NLANES = 8

    def finalize(self, sems, paranoid=False):
        if paranoid:
            for i in range(1, len(self.ops)):
                prev = self.ops[i - 1]
                if prev not in self.ops[i].deps:
                    self.ops[i].deps.append(prev)
        for op in self.ops:
            if op.is_dma:
                op.need_inc = True
            for dep in op.deps:
                if self._needs_wait(dep, op):
                    dep.need_inc = True
        counters = {}
        lane_next = {}
        lane_occ = {}
        for op in self.ops:
            if op.is_dma:
                op.lanes = []
                for i in range(op.ndma):
                    ln = lane_next.get(op.eng, 0)
                    lane_next[op.eng] = (ln + 1) % self.NLANES
                    occ = lane_occ.get((op.eng, ln), 0) + 1
                    lane_occ[(op.eng, ln)] = occ
                    op.lanes.append((ln, occ))
            elif op.need_inc:
                key = (op.eng, False)
                counters[key] = counters.get(key, 0) + 1
                op.val = counters[key]
        by_eng = {}
        for op in self.ops:
            by_eng.setdefault(op.eng, []).append(op)

        def run_stream(eng_name, e, clear_sems=None, go=None):
            if eng_name == "gpsimd":
                for s in clear_sems or ():
                    e.sem_clear(s)
                if go is not None:
                    e.sem_inc(go, 1)
            elif go is not None:
                e.wait_ge(go, 1)
            waited = {}

            def wait_to(skey, val):
                if waited.get(skey, -1) < val:
                    e.wait_ge(sems[skey], val)
                    waited[skey] = val

            for op in by_eng.get(eng_name, ()):
                for dep in op.deps:
                    if not self._needs_wait(dep, op):
                        continue
                    if dep.is_dma:
                        for (ln, occ) in dep.lanes:
                            wait_to((dep.eng, "d", ln), 16 * occ)
                    else:
                        wait_to((dep.eng, False), dep.val)
                if op.is_dma:
                    for (ln, occ) in op.lanes:
                        if occ > 1:
                            wait_to((eng_name, "d", ln), 16 * (occ - 1))
                ret = op.emit(e)
                insts = ret if isinstance(ret, (list, tuple)) else [ret]
                if op.is_dma:
                    assert len(insts) == op.ndma, (op.tag, len(insts), op.ndma)
                    for it, (ln, occ) in zip(insts, op.lanes):
                        it.then_inc(sems[(eng_name, "d", ln)], 16)
                elif op.need_inc:
                    insts[-1].then_inc(sems[(eng_name, False)], 1)
            for (eng2, ln), occ in lane_occ.items():
                if eng2 == eng_name:
                    wait_to((eng_name, "d", ln), 16 * occ)
            e.drain()

        return run_stream, counters, lane_occ


def build_nc(cfg: CFG, dbg=False, paranoid=False, only_p1=False):
    T = cfg.T
    nc = bass.Bass("TRN2", target_bir_lowering=False, debug=False)

    def din(name, shape, dt=F32):
        return nc.dram_tensor(name, list(shape), dt, kind="ExternalInput").ap()

    xT = din("xT", [D, cfg.tok])
    wq_l = din("wq_l", [128, NCH, D]); wk_l = din("wk_l", [128, NCH, D])
    wt_l = din("wt_l", [128, NCH, D]); wqT_l = din("wqT_l", [128, NCH, D])
    wqa_l = din("wqa_l", [128, NCH, H]); wka_l = din("wka_l", [128, NCH, H])
    win_l = din("win_l", [128, NCH, FM]); wout_l = din("wout_l", [FM, D])
    w2bd_l = din("w2bd_l", [128, NCH, FM])
    sel_l = din("sel_l", [FM, NCH, 128], BF16)
    i128_l = din("i128_l", [128, 128], BF16)
    amask_l = din("amask_l", [H, NCH], BF16)
    asel_l = din("asel_l", [H, 128], BF16)
    esel_l = din("esel_l", [H, NCH, 128], BF16)
    w1f_l = din("w1f_l", [128, NCH]); b1raw_l = din("b1raw_l", [128, NCH])
    binf_l = din("binf_l", [128, NCH])
    bq_l = din("bq_l", [128, NCH]); bk_l = din("bk_l", [128, NCH])
    bt_l = din("bt_l", [128, NCH]); bout_l = din("bout_l", [128, NCH])
    bqa_l = din("bqa_l", [H, 1]); bka_l = din("bka_l", [H, 1])
    b2_l = din("b2_l", [FM, 1])

    out_d = nc.dram_tensor("out", [D, cfg.tok], F32, kind="ExternalOutput").ap()
    dbg_d = {}
    if dbg:
        for nm, shape in [("dbg_x1", [D, T]), ("dbg_ptq", [D, T]),
                          ("dbg_ptk", [D, T]), ("dbg_sc", [H, T]),
                          ("dbg_e", [H, T]), ("dbg_pq", [128, NCH]),
                          ("dbg_pk", [128, NCH]), ("dbg_x2", [D, T]),
                          ("dbg_r", [4, T]), ("dbg_btt", [128, NCH]),
                          ("dbg_st", [FM, T]), ("dbg_g", [FM, T]),
                          ("dbg_E", [D, T]), ("dbg_pp", [128, NCH]),
                          ("dbg_Pk", [128, NCH]), ("dbg_Sk", [H, 8]),
                          ("dbg_tiny", [H, 40]), ("dbg_Pcur", [128, NCH]),
                          ("dbg_S", [H, 40]), ("dbg_rs", [128, NCH])]:
            dbg_d[nm] = nc.dram_tensor(nm, shape, F32, kind="ExternalOutput").ap()

    ptq_d = nc.dram_tensor("ptq_d", [D, cfg.tok], BF16, kind="Internal").ap()
    ptk_d = nc.dram_tensor("ptk_d", [D, cfg.tok], BF16, kind="Internal").ap()
    x1_d = nc.dram_tensor("x1_d", [D, cfg.tok], BF16, kind="Internal").ap()
    stat_d = nc.dram_tensor("stat_d", [4, cfg.tok], F32, kind="Internal").ap()
    eq_d = nc.dram_tensor("eq_d", [H, cfg.tok], BF16, kind="Internal").ap()
    escq_d = nc.dram_tensor("escq_d", [H, cfg.tok], F32, kind="Internal").ap()
    ek_d = nc.dram_tensor("ek_d", [H, cfg.tok], BF16, kind="Internal").ap()
    nslot = 2 * cfg.nseq * cfg.ntile
    alpha_d = nc.dram_tensor("alpha_d", [nslot, H], F32, kind="Internal").ap()
    rs_d = nc.dram_tensor("rs_d", [2 * cfg.nseq, H], F32, kind="Internal").ap()

    xT_r = xT.rearrange("(c p) t -> p c t", p=128)
    out_r = out_d.rearrange("(c p) t -> p c t", p=128)
    ptq_r = ptq_d.rearrange("(c p) t -> p c t", p=128)
    ptk_r = ptk_d.rearrange("(c p) t -> p c t", p=128)
    x1_r = x1_d.rearrange("(c p) t -> p c t", p=128)
    eq_dr = eq_d.rearrange("(c a) t -> a c t", a=2)
    ek_dr = ek_d.rearrange("(c a) t -> a c t", a=2)

    p = Prog()

    with ExitStack() as stk:
        def sbt(name, shape, dt=F32):
            return stk.enter_context(nc.sbuf_tensor(name, list(shape), dt))

        wq_s = sbt("wq_s", [128, NCH, D], BF16)
        wk_s = sbt("wk_s", [128, NCH, D], BF16)
        wt_s = sbt("wt_s", [128, NCH, D], BF16)
        big_s = sbt("big_s", [128, NCH, D], BF16)
        wqa_s = sbt("wqa_s", [128, NCH, H], BF16)
        wkae_s = sbt("wkae_s", [128, NCH, H], BF16)
        wkaq_s = sbt("wkaq_s", [128, NCH, H], BF16)
        weff_s = sbt("weff_s", [128, NCH, H], BF16)
        win_s = sbt("win_s", [128, NCH, FM], BF16)
        wout_s = sbt("wout_s", [FM, D], BF16)
        w2bd_s = sbt("w2bd_s", [128, NCH, FM], BF16)
        sel_s = sbt("sel_s", [FM, NCH, 128], BF16)
        i128_s = sbt("i128_s", [128, 128], BF16)
        ones_s = sbt("ones_s", [128, 4], BF16)
        onerow_s = sbt("onerow_s", [1, 128], BF16)
        rows_s = sbt("rows_s", [1, 2 * D + FM + H], BF16)
        w1f_s = sbt("w1f_s", [128, NCH]); b1raw_s = sbt("b1raw_s", [128, NCH])
        binf_s = sbt("binf_s", [128, NCH]); b1f_s = sbt("b1f_s", [128, NCH])
        bq_s = sbt("bq_s", [128, NCH]); bk_s = sbt("bk_s", [128, NCH])
        bt_s = sbt("bt_s", [128, NCH]); bout_s = sbt("bout_s", [128, NCH])
        bout2_s = sbt("bout2_s", [128, NCH]); bqbt_s = sbt("bqbt_s", [128, NCH])
        bq_bf = sbt("bq_bf", [128, NCH], BF16); bk_bf = sbt("bk_bf", [128, NCH], BF16)
        bqa_s = sbt("bqa_s", [H, 1]); bka_s = sbt("bka_s", [H, 1])
        bqa2_s = sbt("bqa2_s", [H, 1]); bka2_s = sbt("bka2_s", [H, 2])
        b2_s = sbt("b2_s", [FM, 1]); b2_bf = sbt("b2_bf", [FM, 1], BF16)
        cswout_f = sbt("cswout_f", [FM, 1]); cswout_bf = sbt("cswout_bf", [FM, 1], BF16)
        sb128_f = sbt("sb128_f", [128, 1]); sb128_bf = sbt("sb128_bf", [128, 1], BF16)
        s2b_s = sbt("s2b_s", [1, 1])
        eps_s = sbt("eps_s", [1, 1]); zero_s = sbt("zero_s", [128, 1])

        x_s = sbt("x_s", [128, 2, NCH, T])
        x_bf = sbt("x_bf", [128, 2, NCH, T], BF16)
        x1_bf = sbt("x1_bf", [128, 2, NCH, T], BF16)
        ptq_s = sbt("ptq_s", [128, NCH, T], BF16)
        ptk_s = sbt("ptk_s", [128, NCH, T], BF16)
        st_bf = sbt("st_bf", [FM, 2, T], BF16)
        h_bf = sbt("h_bf", [128, 2, T], BF16)
        g_bf = sbt("g_bf", [FM, T], BF16)
        r64_s = sbt("r64_s", [64, 2, T], BF16)
        r16_s = sbt("r16_s", [16, 2, T], BF16)
        rf_s = sbt("rf_s", [128, 2, T])
        mrow_s = sbt("mrow_s", [1, 6, T], BF16)
        stat_s = sbt("stat_s", [1, 2, 2, T])
        esc_s = sbt("esc_s", [H, 4, T])
        ee_s = sbt("ee_s", [H, 2, T], BF16)
        eb_s = sbt("eb_s", [H, 2, T], BF16)
        tiny_s = sbt("tiny_s", [H, 40])
        amask_s = sbt("amask_s", [H, NCH], BF16)
        asel_s = sbt("asel_s", [H, 128], BF16)
        esel_s = sbt("esel_s", [H, NCH, 128], BF16)
        am_bf = sbt("am_bf", [H, 4, NCH], BF16)
        pool_s = sbt("pool_s", [128, 11, NCH])
        pq_s = sbt("pq_s", [128, 2, NCH]); pk_s = sbt("pk_s", [128, 2, NCH])
        pkb_bf = sbt("pkb_bf", [128, NCH], BF16)
        btt_s = sbt("btt_s", [128, 2, NCH])

        pb = [stk.enter_context(nc.psum_tensor("pb%d" % i, [128, T], F32))
              for i in range(8)]
        PB_S1, PB_FF, PB_E0, PB_E1, PB_W0, PB_W1, PB_S2, PB_SC = range(8)

        sems = {}
        for eng in ("tensor", "vector", "scalar", "gpsimd", "sync"):
            sems[(eng, False)] = stk.enter_context(
                nc.semaphore("sem_%s_c" % eng))
        for eng in ("sync", "scalar", "gpsimd"):
            for ln in range(Prog.NLANES):
                sems[(eng, "d", ln)] = stk.enter_context(
                    nc.semaphore("sem_%s_d%d" % (eng, ln)))

        def T_op(emit, reads, writes, tag=""):
            return p.add("tensor", emit, reads, writes, tag=tag)

        def V_op(emit, reads, writes, tag=""):
            return p.add("vector", emit, reads, writes, tag=tag)

        def S_op(emit, reads, writes, tag=""):
            return p.add("scalar", emit, reads, writes, tag=tag)

        def dma(eng, emit, reads, writes, tag="", ndma=1):
            return p.add(eng, emit, reads, writes, dma=True, tag=tag, ndma=ndma)

        V_op(lambda e: [
                e.memset(onerow_s[:], 1.0),
                e.memset(ones_s[:, 0:1], 1.0),
                e.memset(ones_s[:, 1:2], -1.0 / D),
                e.memset(ones_s[:, 2:3], 1.0 / D),
                e.memset(eps_s[:], 1e-5),
                e.memset(zero_s[:], 0.0)],
             [], ["ones", "eps", "zero", "onerow"], "memset")

        def cast_load(key, dst, src):
            dma("gpsimd", lambda e: e.dma_start(out=dst, in_=src), [], [key],
                "cast_" + key)

        cast_load("wq", wq_s[:], wq_l)
        cast_load("wk", wk_s[:], wk_l)
        cast_load("wt", wt_s[:], wt_l)
        cast_load("wqT", big_s[:], wqT_l)
        cast_load("wqa", wqa_s[:], wqa_l)
        cast_load("wkae", wkae_s[:], wka_l)
        cast_load("win", win_s[:], win_l)
        cast_load("wout", wout_s[:], wout_l)
        cast_load("w2bd", w2bd_s[:], w2bd_l)

        def plain_load(key, dst, src):
            dma("sync", lambda e: e.dma_start(out=dst, in_=src), [], [key],
                "load_" + key)

        plain_load("sel", sel_s[:], sel_l)
        plain_load("amask", amask_s[:], amask_l)
        plain_load("asel", asel_s[:], asel_l)
        plain_load("esel", esel_s[:], esel_l)
        plain_load("i128", i128_s[:], i128_l)
        plain_load("w1f", w1f_s[:], w1f_l)
        plain_load("b1raw", b1raw_s[:], b1raw_l)
        plain_load("binf", binf_s[:], binf_l)
        plain_load("bq", bq_s[:], bq_l)
        plain_load("bk", bk_s[:], bk_l)
        plain_load("bt", bt_s[:], bt_l)
        plain_load("bout", bout_s[:], bout_l)
        plain_load("bqa", bqa_s[:], bqa_l)
        plain_load("bka", bka_s[:], bka_l)
        plain_load("b2", b2_s[:], b2_l)

        S_op(lambda e: e.activation(out=wkae_s[:], in_=wkae_s[:], func=AF.Copy,
                                    bias=0.0, scale=0.125),
             ["wkae"], ["wkae"], "scale_wka")
        S_op(lambda e: e.activation(out=w2bd_s[:], in_=w2bd_s[:], func=AF.Copy,
                                    bias=0.0, scale=0.5),
             ["w2bd"], ["w2bd"], "scale_w2bd")

        def b1f_mk(e):
            return [e.tensor_scalar(out=b1f_s[:, c:c + 1], in0=w1f_s[:, c:c + 1],
                                    scalar1=binf_s[:, c:c + 1], scalar2=None,
                                    op0=ALU.mult)
                    for c in range(NCH)]
        V_op(b1f_mk, ["w1f", "binf"], ["b1f_m"], "b1f_m")
        V_op(lambda e: e.tensor_add(b1f_s[:], b1f_s[:], b1raw_s[:]),
             ["b1f_m", "b1raw"], ["b1f"], "b1f")
        V_op(lambda e: e.tensor_copy(bq_bf[:], bq_s[:]), ["bq"], ["bq_bf"])
        V_op(lambda e: e.tensor_copy(bk_bf[:], bk_s[:]), ["bk"], ["bk_bf"])
        V_op(lambda e: e.tensor_copy(b2_bf[:], b2_s[:]), ["b2"], ["b2_bf"])
        V_op(lambda e: e.tensor_add(bqbt_s[:], bq_s[:], bt_s[:]),
             ["bq", "bt"], ["bqbt"])

        def weff_mm(e):
            ins = []
            for dc in range(NCH):
                for jc in range(NCH):
                    ins.append(e.matmul(
                        pb[PB_SC][:, dc * H:(dc + 1) * H],
                        big_s[:, jc, dc * 128:(dc + 1) * 128],
                        wqa_s[:, jc, :],
                        start=(jc == 0), stop=(jc == NCH - 1)))
            return ins
        T_op(weff_mm, ["wqT", "wqa"], ["pb_sc"], "weff_mm")
        S_op(lambda e: e.activation(
                out=weff_s[:].rearrange("p c h -> p (c h)"),
                in_=pb[PB_SC][:, 0:NCH * H],
                func=AF.Copy, bias=0.0, scale=0.125),
             ["pb_sc"], ["weff"], "weff")

        CSQ, CSK, CSWIN, CSWEFF = 0, D, 2 * D, 2 * D + FM

        def cs_generic(name, src, ncols, width, dst_off):
            for j in range(ncols):
                def mm(e, j=j):
                    return [e.matmul(pb[PB_SC][0:1, 0:width], ones_s[:, 0:1],
                                     src[:, kc, j * width:(j + 1) * width],
                                     start=(kc == 0), stop=(kc == NCH - 1))
                            for kc in range(NCH)]
                T_op(mm, [name, "ones"], ["pb_sc"], "cs_mm_%s%d" % (name, j))
                S_op(lambda e, j=j: e.activation(
                        out=rows_s[0:1, dst_off + j * width:dst_off + (j + 1) * width],
                        in_=pb[PB_SC][0:1, 0:width], func=AF.Copy, bias=0.0,
                        scale=1.0),
                     ["pb_sc"], ["rows"], "cs_ev_%s%d" % (name, j))

        cs_generic("wq", wq_s, NCH, 128, CSQ)
        cs_generic("wk", wk_s, NCH, 128, CSK)
        cs_generic("win", win_s, 1, FM, CSWIN)
        cs_generic("weff", weff_s, 1, H, CSWEFF)

        def bqa2_mm(e):
            return [e.matmul(pb[PB_SC][0:H, 0:1], wqa_s[:, kc, :],
                             bq_bf[:, kc:kc + 1],
                             start=(kc == 0), stop=(kc == NCH - 1))
                    for kc in range(NCH)]
        T_op(bqa2_mm, ["wqa", "bq_bf"], ["pb_sc"], "bqa2_mm")
        V_op(lambda e: e.tensor_scalar(out=bqa2_s[:], in0=pb[PB_SC][0:H, 0:1],
                                       scalar1=bqa_s[:, 0:1], scalar2=0.125,
                                       op0=ALU.add, op1=ALU.mult),
             ["pb_sc", "bqa"], ["bqa2"], "bqa2")

        def bout2_mm(e):
            return [e.matmul(pb[PB_SC][:, j:j + 1],
                             wout_s[:, j * 128:(j + 1) * 128],
                             b2_bf[:, 0:1], start=True, stop=True)
                    for j in range(NCH)]
        T_op(bout2_mm, ["wout", "b2_bf"], ["pb_sc"], "bout2_mm")
        V_op(lambda e: [e.tensor_scalar(
                out=bout2_s[:, j:j + 1], in0=pb[PB_SC][:, j:j + 1],
                scalar1=bout_s[:, j:j + 1], scalar2=0.5,
                op0=ALU.add, op1=ALU.mult) for j in range(NCH)],
             ["pb_sc", "bout"], ["bout2"], "bout2")
        V_op(lambda e: e.reduce_sum(cswout_f[:], wout_s[:], axis=AX.X),
             ["wout"], ["cswf"], "cswf")
        V_op(lambda e: e.tensor_scalar(out=cswout_bf[:], in0=cswout_f[:],
                                       scalar1=-1.0 / D, scalar2=None,
                                       op0=ALU.mult),
             ["cswf"], ["cswout"], "cswout")
        V_op(lambda e: e.reduce_sum(sb128_f[:], bout2_s[:], axis=AX.X),
             ["bout2"], ["sb128f"], "sb128f")
        V_op(lambda e: e.tensor_copy(sb128_bf[:], sb128_f[:]),
             ["sb128f"], ["sb128b"], "sb128b")
        T_op(lambda e: e.matmul(pb[PB_S2][0:1, 0:1], ones_s[:, 1:2],
                                sb128_bf[:], start=True, stop=True),
             ["sb128b", "ones"], ["pb%d" % PB_S2], "s2bmm")
        S_op(lambda e: e.activation(out=s2b_s[:], in_=pb[PB_S2][0:1, 0:1],
                                    func=AF.Copy, bias=0.0, scale=1.0),
             ["pb%d" % PB_S2], ["s2b"], "s2bcp")

        def ln_stats(pba, pbb, par, mslot, stat_row, t0, tag,
                     onchip_r64=False, store=True, madd=None):
            key = "stt%d" % par

            if madd is None:
                V_op(lambda e: e.tensor_copy(stat_s[0:1, par, 0, :],
                                             pb[pba][0:1, :]),
                     ["pb%d" % pba], [key + "_m"], tag + "_cm")
            else:
                madd_ap, madd_key = madd
                V_op(lambda e: e.tensor_scalar(
                        out=stat_s[0:1, par, 0, :], in0=pb[pba][0:1, :],
                        scalar1=madd_ap, scalar2=None, op0=ALU.add),
                     ["pb%d" % pba, madd_key], [key + "_m"], tag + "_cm")
            V_op(lambda e: e.tensor_mul(stat_s[0:1, par, 1, :],
                                        stat_s[0:1, par, 0, :],
                                        stat_s[0:1, par, 0, :]),
                 [key + "_m"],
                 [key + "_msq", key + "_v", key + "_ln", key + "_r"],
                 tag + "_msq")
            V_op(lambda e: e.scalar_tensor_tensor(
                    out=stat_s[0:1, par, 1, :], in0=stat_s[0:1, par, 1, :],
                    scalar=-1.0, in1=pb[pbb][0:1, :],
                    op0=ALU.mult, op1=ALU.add),
                 [key + "_msq", "pb%d" % pbb], [key + "_v"], tag + "_v")
            S_op(lambda e: e.activation(out=stat_s[0:1, par, 1, :],
                                        in_=stat_s[0:1, par, 1, :], func=AF.Ln,
                                        bias=eps_s[0:1, 0:1], scale=1.0),
                 [key + "_v", "eps"], [key + "_ln"], tag + "_ln")
            S_op(lambda e: e.activation(out=stat_s[0:1, par, 1, :],
                                        in_=stat_s[0:1, par, 1, :], func=AF.Exp,
                                        bias=0.0, scale=-0.5),
                 [key + "_ln"], [key + "_r"], tag + "_r")
            V_op(lambda e: e.tensor_copy(mrow_s[0:1, mslot, :],
                                         stat_s[0:1, par, 0, :]),
                 [key + "_m"], ["mrow%d" % mslot], tag + "_mbf")
            if store:
                dma("scalar", lambda e: e.dma_start(
                        out=stat_d[stat_row:stat_row + 1, t0:t0 + T],
                        in_=stat_s[0:1, par, 1, :]),
                    [key + "_r"], ["statd%d_%d" % (stat_row, t0)], tag + "_st")
            if onchip_r64:
                V_op(lambda e: e.tensor_copy(eb_s[0:1, par, :],
                                             stat_s[0:1, par, 1, :]),
                     [key + "_r"], ["ebbf%d" % par], tag + "_rb")

        def r64_mm(par, tag):
            T_op(lambda e: e.matmul(pb[PB_SC][0:64, :],
                                    onerow_s[0:1, 0:64],
                                    eb_s[0:1, par, :],
                                    start=True, stop=True),
                 ["ebbf%d" % par, "onerow"], ["pb_sc"], tag + "_rmm")
            S_op(lambda e: e.activation(out=r64_s[:, par, :],
                                        in_=pb[PB_SC][0:64, :],
                                        func=AF.Copy, bias=0.0, scale=1.0),
                 ["pb_sc"], ["r64_%d" % par], tag + "_rcp")

        def bcast_r(stat_row, t0, dst_key, dst_ap, nparts, tag,
                    eng="gpsimd"):
            dma(eng, lambda e: e.dma_start(
                    out=dst_ap.unsqueeze(1),
                    in_=stat_d[stat_row:stat_row + 1, t0:t0 + T]
                        .partition_broadcast(nparts)),
                ["statd%d_%d" % (stat_row, t0)], [dst_key], tag)

        def ffn(xin_bf, xin_keys, mslot, reg, r64_key, fin, tag, tap=False,
                filler=None):
            def win_mm(e):
                ins = [e.matmul(pb[PB_FF][0:FM, :], win_s[:, c, :],
                                xin_bf[:, c, :], start=(c == 0), stop=False)
                       for c in range(NCH)]
                ins.append(e.matmul(pb[PB_FF][0:FM, :],
                                    rows_s[0:1, CSWIN:CSWIN + FM],
                                    mrow_s[0:1, mslot, :], start=False, stop=True))
                return ins
            T_op(win_mm, list(xin_keys) + ["mrow%d" % mslot, "win", "rows"],
                 ["pb_ff"], tag + "_win")
            V_op(lambda e: e.tensor_tensor(out=st_bf[:, reg, :],
                                           in0=pb[PB_FF][0:FM, :],
                                           in1=r64_s[:, reg, :], op=ALU.mult),
                 ["pb_ff", r64_key], ["st%d" % reg], tag + "_st")
            if tap:
                dma("gpsimd", lambda e: e.dma_start(out=dbg_d["dbg_st"],
                                                    in_=st_bf[:, reg, :]),
                    ["st%d" % reg], ["dbgst"], "dbg_st")
            for c in range(NCH):
                eb = PB_E0 if (c % 2 == 0) else PB_E1
                T_op(lambda e, c=c, eb=eb: e.matmul(
                        pb[eb][:], sel_s[:, c, :], st_bf[:, reg, :],
                        start=True, stop=True),
                     ["st%d" % reg, "sel"], ["pb_e%d" % (c % 2)],
                     tag + "_ex%d" % c)
                if filler is not None:
                    filler(c)
                S_op(lambda e, c=c, eb=eb: e.activation(
                        out=h_bf[:, c % 2, :], in_=pb[eb][:], func=AF.Relu,
                        bias=b1f_s[:, c:c + 1], scale=w1f_s[:, c:c + 1]),
                     ["pb_e%d" % (c % 2), "b1f", "w1f"],
                     ["h%d" % (c % 2)], tag + "_rl%d" % c)
                T_op(lambda e, c=c: e.matmul(
                        pb[PB_FF][0:FM, :], w2bd_s[:, c, :], h_bf[:, c % 2, :],
                        start=(c == 0), stop=(c == NCH - 1)),
                     ["h%d" % (c % 2), "w2bd"], ["pb_ff"], tag + "_bd%d" % c)
            S_op(lambda e: e.activation(out=g_bf[:], in_=pb[PB_FF][0:FM, :],
                                        func=AF.Copy, bias=0.0, scale=1.0),
                 ["pb_ff"], ["g"], tag + "_g")
            if tap:
                dma("gpsimd", lambda e: e.dma_start(out=dbg_d["dbg_g"],
                                                    in_=g_bf[:]),
                    ["g"], ["dbgg"], "dbg_g")
            for j in range(NCH):
                wb = PB_W0 if (j % 2 == 0) else PB_W1
                T_op(lambda e, j=j, wb=wb: e.matmul(
                        pb[wb][:], wout_s[:, j * 128:(j + 1) * 128], g_bf[:],
                        start=True, stop=True),
                     ["g", "wout"], ["pb_w%d" % (j % 2)], tag + "_wo%d" % j)
                fin(j, wb)

        def score_pass(si, ti, par, side, t0, tag, sc_bank=None,
                       spill=None):
            base = (si * 2 + side) * 8
            if sc_bank is None:
                sc_bank, sc_key = PB_SC, "pb_sc"
            else:
                sc_bank, sc_key = sc_bank
            rm_prev = tiny_s[:, base + (ti % 2):base + (ti % 2) + 1]
            rm_new = tiny_s[:, base + ((ti + 1) % 2):base + ((ti + 1) % 2) + 1]
            es = 2 * par + ti % 2
            kA = "escA%d" % es
            V_op(lambda e: e.tensor_tensor(
                    out=esc_s[:, es, :], in0=pb[sc_bank][0:H, :],
                    in1=r16_s[:, par, :], op=ALU.mult),
                 [sc_key, "r16_%d" % par], [kA], tag + "_sc")
            V_op(lambda e: e.reduce_max(tiny_s[:, base + 2:base + 3],
                                        esc_s[:, es, :], axis=AX.X),
                 [kA], ["tm%d" % base], tag + "_tm")
            V_op(lambda e: e.tensor_max(rm_new, rm_prev,
                                        tiny_s[:, base + 2:base + 3]),
                 ["tm%d" % base, "rm%d_%d" % (base, ti % 2)],
                 ["rm%d_%d" % (base, (ti + 1) % 2)], tag + "_nm")
            if spill is not None:
                dma("sync" if par == 0 else "scalar",
                    lambda e: e.dma_start(out=spill[:, t0:t0 + T],
                                          in_=esc_s[:, es, :]),
                    [kA], ["escd_%d" % t0], tag + "_escd")

        def eb2_final(si, side, bias2_ap, bias2_k, tag):
            base = (si * 2 + side) * 8
            rmf = base + (cfg.ntile % 2)
            V_op(lambda e: e.tensor_sub(tiny_s[:, base + 7:base + 8], bias2_ap,
                                        tiny_s[:, rmf:rmf + 1]),
                 ["rm%d_%d" % (base, cfg.ntile % 2), bias2_k],
                 ["mx%d" % base], tag + "_eb2")

        def pool_core(si, ti, par, side, pcur, t0, tag,
                      buf, buf_keys):
            base = (si * 2 + side) * 8
            es = 2 * par + ti % 2
            kA = "escA%d" % es
            kB = "escB%d" % par
            S_op(lambda e: e.activation(out=ee_s[:, par, :],
                                        in_=esc_s[:, es, :],
                                        func=AF.Exp,
                                        bias=tiny_s[:, base + 7:base + 8],
                                        scale=1.0,
                                        accum_out=tiny_s[:, base + 2:base + 3]),
                 ["mx%d" % base, kA, "tm%d" % base],
                 [kB, "ts%d" % base], tag + "_exp")
            V_op(lambda e: e.tensor_add(tiny_s[:, base + 5:base + 6],
                                        tiny_s[:, base + 5:base + 6],
                                        tiny_s[:, base + 2:base + 3]),
                 ["ts%d" % base, "S%d" % base], ["S%d" % base], tag + "_sa")
            V_op(lambda e: e.tensor_tensor(out=eb_s[:, par, :],
                                           in0=ee_s[:, par, :],
                                           in1=r16_s[:, par, :], op=ALU.mult),
                 [kB, "r16_%d" % par], ["ebbf%d" % par], tag + "_eb")
            for c in range(NCH):
                ebk = PB_E0 if c % 2 == 0 else PB_E1
                T_op(lambda e, c=c, ebk=ebk: e.matmul(
                        pb[ebk][:], esel_s[:, c, :], eb_s[:, par, :],
                        start=True, stop=True),
                     ["ebbf%d" % par, "esel"], ["pb_e%d" % (c % 2)],
                     tag + "_em%d" % c)
                V_op(lambda e, c=c, ebk=ebk: e.scalar_tensor_tensor(
                        out=h_bf[:, 0, :], in0=buf[:, c, :], scalar=1.0,
                        in1=pb[ebk][:], op0=ALU.mult, op1=ALU.mult,
                        accum_out=pool_s[:, 2, c:c + 1]),
                     list(buf_keys) + ["pb_e%d" % (c % 2)],
                     ["Ppart%d" % c, "h0"], tag + "_ttr%d" % c)
            p_update(si, side, pcur, tag)

        def p_update(si, side, pcur, tag):
            V_op(lambda e: e.tensor_add(pool_s[:, pcur, :], pool_s[:, pcur, :],
                                        pool_s[:, 2, :]),
                 ["P%d" % pcur] + ["Ppart%d" % c for c in range(NCH)],
                 ["P%d" % pcur], tag + "_pu")

        def pool_finalize(si, side, pcur, rslot, tag):
            base = (si * 2 + side) * 8
            V_op(lambda e: e.reciprocal(tiny_s[:, base + 6:base + 7],
                                        tiny_s[:, base + 5:base + 6]),
                 ["S%d" % base], ["rsv%d" % base], tag + "_rs")
            bslot = si * 2 + side
            V_op(lambda e: e.tensor_scalar(
                    out=am_bf[:, bslot, :], in0=amask_s[:],
                    scalar1=tiny_s[:, base + 6:base + 7], scalar2=None,
                    op0=ALU.mult),
                 ["rsv%d" % base, "amask"], ["am%d" % bslot], tag + "_rm")
            T_op(lambda e: e.matmul(pb[PB_S2][:, 0:NCH], asel_s[:],
                                    am_bf[:, bslot, :],
                                    start=True, stop=True),
                 ["am%d" % bslot, "asel"], ["pb%d" % PB_S2], tag + "_rmm2")
            V_op(lambda e: e.tensor_tensor(out=pool_s[:, 6, :],
                                           in0=pool_s[:, pcur, :],
                                           in1=pb[PB_S2][:, 0:NCH],
                                           op=ALU.mult),
                 ["P%d" % pcur, "pb%d" % PB_S2], ["Pmean"], tag + "_pm")

        def p1_A0(si, ti):
            par = si % 2
            t0 = si * cfg.seq + ti * T
            xk, xbk, sqk = "x%d" % par, "xbf%d" % par, "sq%d" % par
            dma("sync", lambda e: e.dma_start(out=x_s[:, par, :, :],
                                              in_=xT_r[:, :, t0:t0 + T]),
                [xk], [xk], "p1_xload")
            V_op(lambda e: e.tensor_copy(x_bf[:, par, :, :],
                                         x_s[:, par, :, :]),
                 [xk], [xbk], "p1_xcast")
            sq_ap = big_s[:, :, par * T:(par + 1) * T]
            S_op(lambda e: e.activation(out=sq_ap,
                                        in_=x_bf[:, par, :, :],
                                        func=AF.Square, bias=0.0, scale=1.0),
                 [xbk], [sqk, "wqT"], "p1_sq")

        def p1_A1(si, ti):
            par = si % 2
            t0 = si * cfg.seq + ti * T
            xk, xbk, sqk = "x%d" % par, "xbf%d" % par, "sq%d" % par
            sq_ap = big_s[:, :, par * T:(par + 1) * T]

            def sums1(e):
                ins = []
                for c in range(NCH):
                    ins.append(e.matmul(pb[PB_S1][0:1, :], ones_s[:, 1:2],
                                        x_bf[:, par, c, :],
                                        start=(c == 0), stop=(c == NCH - 1)))
                for c in range(NCH):
                    ins.append(e.matmul(pb[PB_S2][0:1, :], ones_s[:, 2:3],
                                        sq_ap[:, c, :],
                                        start=(c == 0), stop=(c == NCH - 1)))
                return ins
            T_op(sums1, [xbk, sqk, "ones"],
                 ["pb%d" % PB_S1, "pb%d" % PB_S2], "p1_sums1")
            ln_stats(PB_S1, PB_S2, par, par, 0, t0, "p1ln1_%d_%d" % (si, ti),
                     onchip_r64=True, store=False)


        def p1_A2(si, ti, prev_ti=None):
            par = si % 2
            t0 = si * cfg.seq + ti * T
            xk, xbk, sqk = "x%d" % par, "xbf%d" % par, "sq%d" % par
            sq_ap = big_s[:, :, par * T:(par + 1) * T]
            filler = None
            if prev_ti is not None:
                filler = lambda c: p1_D3_j(si, prev_ti, c)
            r64_mm(par, "p1r64_%d_%d" % (si, ti))

            def fin(j, wb):
                V_op(lambda e, j=j, wb=wb: e.scalar_tensor_tensor(
                        out=x1_bf[:, par, j, :], in0=pb[wb][:],
                        scalar=bout2_s[:, j:j + 1], in1=x_s[:, par, j, :],
                        op0=ALU.add, op1=ALU.add),
                     ["pb_w%d" % (j % 2), xk, "bout2"],
                     ["x1bf%d_%d" % (par, j)], "p1_x1_%d" % j)
            ffn(x_bf[:, par, :, :], [xbk], par, par, "r64_%d" % par, fin,
                "p1f_%d_%d" % (si, ti), tap=(dbg and si == 0 and ti == 0),
                filler=filler)

            x1keys = ["x1bf%d_%d" % (par, j) for j in range(NCH)]
            dma("scalar", lambda e: e.dma_start(out=x1_r[:, :, t0:t0 + T],
                                                in_=x1_bf[:, par, :, :]),
                x1keys, ["x1d_%d" % t0], "p1_x1sp")
            if dbg and si == 0 and ti == 0:
                dma("gpsimd", lambda e: e.dma_start(
                        out=dbg_d["dbg_x1"].rearrange("(c p) t -> p c t", p=128),
                        in_=x1_bf[:, par, :, :]),
                    x1keys, ["dbgx1"], "dbg_x1")

        def p1_A4(si, ti):
            par = si % 2
            t0 = si * cfg.seq + ti * T
            xk, xbk, sqk = "x%d" % par, "xbf%d" % par, "sq%d" % par
            sq_ap = big_s[:, :, par * T:(par + 1) * T]
            x1keys = ["x1bf%d_%d" % (par, j) for j in range(NCH)]
            S_op(lambda e: e.activation(out=sq_ap,
                                        in_=x1_bf[:, par, :, :],
                                        func=AF.Square, bias=0.0, scale=1.0),
                 x1keys, [sqk, "wqT"], "p1_sq1")

            def sums2(e):
                ins = [e.matmul(pb[PB_S1][0:1, :], cswout_bf[:], g_bf[:],
                                start=True, stop=False),
                       e.matmul(pb[PB_S1][0:1, :], onerow_s[0:1, 0:1],
                                mrow_s[0:1, par, :],
                                start=False, stop=True)]
                for c in range(NCH):
                    ins.append(e.matmul(pb[PB_S2][0:1, :], ones_s[:, 2:3],
                                        sq_ap[:, c, :],
                                        start=(c == 0), stop=(c == NCH - 1)))
                return ins
            T_op(sums2, x1keys + [sqk, "ones", "g", "cswout", "onerow",
                                  "mrow%d" % par],
                 ["pb%d" % PB_S1, "pb%d" % PB_S2], "p1_sums2")
            ln_stats(PB_S1, PB_S2, par, 2 + par, 1, t0, "p1ln2_%d_%d" % (si, ti),
                     madd=(s2b_s[0:1, 0:1], "s2b"))
            bcast_r(1, t0, "r16_%d" % par, r16_s[:, par, :], H, "p1_r16")

        BANKS = [(PB_W0, "pb_w0"), (PB_W1, "pb_w1"),
                 (PB_E0, "pb_e0"), (PB_E1, "pb_e1")]

        def p1_D1(si, ti):
            par = si % 2
            t0 = si * cfg.seq + ti * T
            x1keys = ["x1bf%d_%d" % (par, j) for j in range(NCH)]

            def sc_mm(e):
                ins = [e.matmul(pb[PB_SC][0:H, :], weff_s[:, kc, :],
                                x1_bf[:, par, kc, :],
                                start=(kc == 0), stop=False)
                       for kc in range(NCH)]
                ins.append(e.matmul(pb[PB_SC][0:H, :],
                                    rows_s[0:1, CSWEFF:CSWEFF + H],
                                    mrow_s[0:1, 2 + par, :],
                                    start=False, stop=True))
                return ins
            T_op(sc_mm, x1keys + ["mrow%d" % (2 + par), "weff", "rows"],
                 ["pb_sc"], "p1_scq")
            score_pass(si, ti, par, 0, t0, "p1sm_%d_%d" % (si, ti),
                       spill=escq_d)
            if dbg and si == 0 and ti == 0:
                dma("sync", lambda e: e.dma_start(out=dbg_d["dbg_sc"],
                                                  in_=esc_s[:, 2 * par, :]),
                    ["escA%d" % (2 * par)], ["dbgsc"], "dbg_sc")

        def p1_D2(si, ti):
            par = si % 2
            t0 = si * cfg.seq + ti * T
            x1keys = ["x1bf%d_%d" % (par, j) for j in range(NCH)]
            for j in range(NCH):
                wb, wkey = BANKS[j % 4]

                def wq_mm(e, j=j, wb=wb):
                    ins = [e.matmul(pb[wb][:],
                                    wq_s[:, kc, j * 128:(j + 1) * 128],
                                    x1_bf[:, par, kc, :],
                                    start=(kc == 0), stop=False)
                           for kc in range(NCH)]
                    ins.append(e.matmul(
                        pb[wb][:], rows_s[0:1, CSQ + j * 128:CSQ + (j + 1) * 128],
                        mrow_s[0:1, 2 + par, :], start=False, stop=True))
                    return ins
                T_op(wq_mm, x1keys + ["mrow%d" % (2 + par), "wq", "rows"],
                     [wkey], "p1_wq%d" % j)
                V_op(lambda e, j=j, wb=wb: e.tensor_copy(ptq_s[:, j, :],
                                                         pb[wb][:]),
                     [wkey], ["ptq_%d" % j], "p1_ptq%d" % j)

            ptqkeys = ["ptq_%d" % j for j in range(NCH)]
            dma("sync", lambda e: e.dma_start(out=ptq_r[:, :, t0:t0 + T],
                                              in_=ptq_s[:, :, :]),
                ptqkeys, ["ptqd_%d" % t0], "p1_ptqsp")
            if dbg and si == 0 and ti == 0:
                dma("gpsimd", lambda e: e.dma_start(
                        out=dbg_d["dbg_ptq"].rearrange("(c p) t -> p c t", p=128),
                        in_=ptq_s[:, :, :]),
                    ptqkeys, ["dbgptq"], "dbg_ptq")

        def p1_D3_j(si, ti, j):
            par = si % 2
            x1keys = ["x1bf%d_%d" % (par, j2) for j2 in range(NCH)]
            for j in (j,):
                wb, wkey = BANKS[j % 2]

                def wk_mm(e, j=j, wb=wb):
                    ins = [e.matmul(pb[wb][:],
                                    wk_s[:, kc, j * 128:(j + 1) * 128],
                                    x1_bf[:, par, kc, :],
                                    start=(kc == 0), stop=False)
                           for kc in range(NCH)]
                    ins.append(e.matmul(
                        pb[wb][:], rows_s[0:1, CSK + j * 128:CSK + (j + 1) * 128],
                        mrow_s[0:1, 2 + par, :], start=False, stop=True))
                    return ins
                T_op(wk_mm, x1keys + ["mrow%d" % (2 + par), "wk", "rows"],
                     [wkey], "p1_wk%d" % j)
                V_op(lambda e, j=j, wb=wb: e.tensor_copy(ptk_s[:, j, :],
                                                         pb[wb][:]),
                     [wkey], ["ptk_%d" % j], "p1_ptk%d" % j)

        def p1_D3sp(si, ti):
            t0 = si * cfg.seq + ti * T
            ptkkeys = ["ptk_%d" % j for j in range(NCH)]
            dma("sync", lambda e: e.dma_start(out=ptk_r[:, :, t0:t0 + T],
                                              in_=ptk_s[:, :, :]),
                ptkkeys, ["ptkd_%d" % t0], "p1_ptksp")

        def tile_buf(primary, par, ti):
            if ti % 2 == 0:
                return primary
            ap = big_s[:, :, par * T:(par + 1) * T]
            keys = (["sq0", "u", "wqT"] if par == 0
                    else ["sq1", "sqP3", "wqT"])
            return ap, keys

        def pbq_buf(par, ti):
            primary = ((ptq_s, ["ptq_%d" % j for j in range(NCH)])
                       if par == 0
                       else (ptk_s, ["ptk_%d" % j for j in range(NCH)]))
            return tile_buf(primary, par, ti)

        def pbq_load(si, ti):
            par = si % 2
            t0 = si * cfg.seq + ti * T
            ring = "sync" if par == 0 else "scalar"
            buf, bkeys = pbq_buf(par, ti)
            dma(ring, lambda e: e.dma_start(out=buf[:, :, :],
                                            in_=ptq_r[:, :, t0:t0 + T]),
                ["ptqd_%d" % t0] + bkeys, bkeys, "pbq_pld")
            bcast_r(1, t0, "r16_%d" % par, r16_s[:, par, :], H, "pbq_r16")

        def pbq_chain(si, ti):
            par = si % 2
            t0 = si * cfg.seq + ti * T
            buf, bkeys = pbq_buf(par, ti)
            ring = "sync" if par == 0 else "scalar"
            es = 2 * par + ti % 2
            dma(ring, lambda e: e.dma_start(out=esc_s[:, es, :],
                                            in_=escq_d[:, t0:t0 + T]),
                ["escd_%d" % t0, "escA%d" % es], ["escA%d" % es],
                "pbq_escld")
            pool_core(si, ti, par, 0, q_slot(si), t0,
                      "pbq_%d_%d" % (si, ti), buf, bkeys)


        def q_slot(si):
            return 0 if si == 0 else 8

        def k_slot(si):
            return 3 if si == 0 else 9

        def p1_init(si):
            base_q = (si * 2 + 0) * 8
            qs_ = q_slot(si)
            V_op(lambda e: [
                    e.memset(tiny_s[:, base_q + 0:base_q + 1], NEG_INIT),
                    e.memset(tiny_s[:, base_q + 5:base_q + 6], 0.0),
                    e.memset(pool_s[:, qs_, :], 0.0)],
                 ["P%d" % qs_, "S%d" % base_q],
                 ["rm%d_0" % base_q, "S%d" % base_q, "P%d" % qs_], "p1_init")

        def p1_fin(si):
            pool_finalize(si, 0, q_slot(si), 2 * si + 0, "p1pf_%d" % si)
            V_op(lambda e: e.tensor_add(pq_s[:, si, :], pool_s[:, 6, :],
                                        bq_s[:]),
                 ["Pmean", "bq"], ["pq%d" % si], "p1_pq")
            if dbg and si == 0:
                dma("sync", lambda e: e.dma_start(out=dbg_d["dbg_pq"],
                                                  in_=pq_s[:, si, :]),
                    ["pq%d" % si], ["dbgpq"], "dbg_pq")

            wkaq_buf = wkaq_s if si == 0 else wqa_s
            wkaq_key = "wkaq" if si == 0 else "wqa"
            V_op(lambda e: [e.tensor_scalar(
                    out=wkaq_buf[:, c, :], in0=wkae_s[:, c, :],
                    scalar1=pq_s[:, si, c:c + 1], scalar2=None, op0=ALU.mult)
                    for c in range(NCH)],
                 ["wkae", "pq%d" % si], [wkaq_key], "p1_wkaq")
            V_op(lambda e: [e.tensor_scalar(
                    out=pkb_bf[:, c:c + 1], in0=bk_s[:, c:c + 1],
                    scalar1=pq_s[:, si, c:c + 1], scalar2=None, op0=ALU.mult)
                    for c in range(NCH)],
                 ["bk", "pq%d" % si], ["pkb"], "p1_pqbk")

            def bka2_mm(e):
                return [e.matmul(pb[PB_SC][0:H, 0:1], wkaq_buf[:, kc, :],
                                 pkb_bf[:, kc:kc + 1],
                                 start=(kc == 0), stop=(kc == NCH - 1))
                        for kc in range(NCH)]
            T_op(bka2_mm, [wkaq_key, "pkb"], ["pb_sc"], "p1_bka2mm")
            V_op(lambda e: e.tensor_scalar(out=bka2_s[:, si:si + 1],
                                           in0=bka_s[:, 0:1],
                                           scalar1=0.125, scalar2=None,
                                           op0=ALU.mult),
                 ["bka"], ["bka2a"], "p1_bka2a")
            V_op(lambda e: e.tensor_add(bka2_s[:, si:si + 1],
                                        bka2_s[:, si:si + 1],
                                        pb[PB_SC][0:H, 0:1]),
                 ["pb_sc", "bka2a"], ["bka2"], "p1_bka2")

        def p2_buf(par, ti):
            primary = ((ptk_s, ["ptk_%d" % j for j in range(NCH)])
                       if par == 0
                       else (ptq_s, ["ptq_%d" % j for j in range(NCH)]))
            return tile_buf(primary, par, ti)

        def p2_load(si, ti, tag):
            par = si % 2
            t0 = si * cfg.seq + ti * T
            ring = "sync" if par == 0 else "scalar"
            buf, bkeys = p2_buf(par, ti)
            dma(ring, lambda e: e.dma_start(out=buf[:, :, :],
                                            in_=ptk_r[:, :, t0:t0 + T]),
                ["ptkd_%d" % t0] + bkeys, bkeys, tag + "_ld")
            bcast_r(1, t0, "r16_%d" % par, r16_s[:, par, :], H, tag + "_r16")

        def p2_scmm(si, par, buf, bkeys, tag):
            scb = (PB_SC, "pb_sc") if par == 0 else (PB_S1, "pb%d" % PB_S1)
            wkaq_buf = wkaq_s if si == 0 else wqa_s
            wkaq_key = "wkaq" if si == 0 else "wqa"

            def sc_mm(e):
                return [e.matmul(pb[scb[0]][0:H, :], wkaq_buf[:, kc, :],
                                 buf[:, kc, :],
                                 start=(kc == 0), stop=(kc == NCH - 1))
                        for kc in range(NCH)]
            T_op(sc_mm, bkeys + [wkaq_key], [scb[1]], tag + "_sck")
            return scb

        def p2a_chain(si, ti):
            par = si % 2
            t0 = si * cfg.seq + ti * T
            buf, bkeys = p2_buf(par, ti)
            scb = p2_scmm(si, par, buf, bkeys, "p2a_%d_%d" % (si, ti))
            score_pass(si, ti, par, 1, t0, "p2a_%d_%d" % (si, ti), sc_bank=scb)

        def p2b_chain(si, ti):
            par = si % 2
            t0 = si * cfg.seq + ti * T
            buf, bkeys = p2_buf(par, ti)
            scb = p2_scmm(si, par, buf, bkeys, "p2b_%d_%d" % (si, ti))
            es = 2 * par + ti % 2
            kA = "escA%d" % es
            V_op(lambda e: e.tensor_tensor(
                    out=esc_s[:, es, :], in0=pb[scb[0]][0:H, :],
                    in1=r16_s[:, par, :], op=ALU.mult),
                 [scb[1], "r16_%d" % par], [kA], "p2b_%d_%d_sc" % (si, ti))
            pool_core(si, ti, par, 1, k_slot(si), t0,
                      "p2b_%d_%d" % (si, ti), buf, bkeys)

        def p2_init(si):
            base_k = (si * 2 + 1) * 8
            ks_ = k_slot(si)
            V_op(lambda e: [
                    e.memset(tiny_s[:, base_k + 0:base_k + 1], NEG_INIT),
                    e.memset(tiny_s[:, base_k + 5:base_k + 6], 0.0),
                    e.memset(pool_s[:, ks_, :], 0.0)],
                 ["P%d" % ks_, "S%d" % base_k],
                 ["rm%d_0" % base_k, "S%d" % base_k, "P%d" % ks_], "p2_init")

        def p2_fin(si):
            pool_finalize(si, 1, k_slot(si), 2 * si + 1, "p2pf_%d" % si)
            V_op(lambda e: e.tensor_add(pool_s[:, 7, :], pool_s[:, 6, :],
                                        bk_s[:]),
                 ["Pmean", "bk"], ["pmk"], "p2_pmk")
            V_op(lambda e: e.tensor_tensor(out=pk_s[:, si, :],
                                           in0=pool_s[:, 7, :],
                                           in1=pq_s[:, si, :], op=ALU.mult),
                 ["pmk", "pq%d" % si], ["pk%d" % si], "p2_pk")
            if dbg and si == 0:
                dma("sync", lambda e: e.dma_start(out=dbg_d["dbg_pk"],
                                                  in_=pk_s[:, si, :]),
                    ["pk%d" % si], ["dbgpk"], "dbg_pk")
                dma("sync", lambda e: e.dma_start(out=dbg_d["dbg_Pk"],
                                                  in_=pool_s[:, k_slot(si), :]),
                    ["P%d" % k_slot(si)], ["dbgPk"], "dbg_Pk")
                dma("sync", lambda e: e.dma_start(out=dbg_d["dbg_Sk"],
                                                  in_=tiny_s[:, (si*2+1)*8:
                                                             (si*2+1)*8+8]),
                    ["S%d" % ((si*2+1)*8)], ["dbgSk"], "dbg_Sk")
            V_op(lambda e: [e.tensor_scalar(
                    out=pkb_bf[:, c:c + 1], in0=bq_s[:, c:c + 1],
                    scalar1=pk_s[:, si, c:c + 1], scalar2=None, op0=ALU.mult)
                    for c in range(NCH)],
                 ["bq", "pk%d" % si], ["pkb"], "p2_pkbq")

            def btt_mm(e):
                ins = []
                for j in range(NCH):
                    for kc in range(NCH):
                        ins.append(e.matmul(
                            pb[PB_SC][:, j:j + 1],
                            wt_s[:, kc, j * 128:(j + 1) * 128],
                            pkb_bf[:, kc:kc + 1],
                            start=(kc == 0), stop=(kc == NCH - 1)))
                return ins
            T_op(btt_mm, ["pkb", "wt"], ["pb_sc"], "p2_bttmm")
            V_op(lambda e: e.tensor_add(btt_s[:, si, :], pb[PB_SC][:, 0:NCH],
                                        bqbt_s[:]),
                 ["pb_sc", "bqbt"], ["btt%d" % si], "p2_btt")
            if dbg and si == 0:
                dma("sync", lambda e: e.dma_start(out=dbg_d["dbg_btt"],
                                                  in_=btt_s[:, si, :]),
                    ["btt%d" % si], ["dbgbtt"], "dbg_btt")

        def p3_bufs(si, ti):
            pp = ti % 2
            if pp == 0:
                qb = (ptq_s, ["p3ptq0"] + ["ptq_%d" % j for j in range(NCH)])
            else:
                qb = (ptk_s, ["p3ptq1"] + ["ptk_%d" % j for j in range(NCH)])
            x1b = (x1_bf[:, pp], ["p3x1%d" % pp])
            return pp, qb, x1b

        def p3_wtscale(si, reload):
            if reload:
                dma("gpsimd", lambda e: e.dma_start(out=wt_s[:], in_=wt_l),
                    ["wt"], ["wt"], "p3_wtrl")
            V_op(lambda e: [e.tensor_scalar(
                    out=wt_s[:, kc, :], in0=wt_s[:, kc, :],
                    scalar1=pk_s[:, si, kc:kc + 1], scalar2=None,
                    op0=ALU.mult) for kc in range(NCH)],
                 ["wt", "pk%d" % si], ["wt"], "p3_wtsc")

        def p3_load(si, ti):
            t0 = si * cfg.seq + ti * T
            pp, (qbuf, qkeys), (x1buf, x1keysb) = p3_bufs(si, ti)
            dma("sync", lambda e: e.dma_start(out=qbuf[:, :, :],
                                              in_=ptq_r[:, :, t0:t0 + T]),
                ["ptqd_%d" % t0] + qkeys, qkeys, "p3_ptqld")
            dma("sync", lambda e: e.dma_start(out=x1buf[:, :, :],
                                              in_=x1_r[:, :, t0:t0 + T]),
                ["x1d_%d" % t0] + x1keysb, x1keysb, "p3_x1ld")
            dma("scalar", lambda e: e.dma_start(
                    out=rf_s[:, pp, :].unsqueeze(1),
                    in_=stat_d[1:2, t0:t0 + T].partition_broadcast(128)),
                ["statd1_%d" % t0, "rf%d" % pp], ["rf%d" % pp], "p3_rf")

        def p3_wt_j(si, ti, j):
            pp, (qbuf, qkeys), _ = p3_bufs(si, ti)
            wb = PB_W0 if (j % 2 == 0) else PB_W1

            def wt_mm(e, j=j, wb=wb):
                ins = [e.matmul(pb[wb][:],
                                wt_s[:, kc, j * 128:(j + 1) * 128],
                                qbuf[:, kc, :],
                                start=(kc == 0), stop=False)
                       for kc in range(NCH)]
                ins.append(e.matmul(pb[wb][:], i128_s[:],
                                    qbuf[:, j, :],
                                    start=False, stop=True))
                return ins
            T_op(wt_mm, qkeys + ["wt", "i128"], ["pb_w%d" % (j % 2)],
                 "p3_wt%d" % j)

        def p3_vx2(si, ti, j):
            pp, (qbuf, qkeys), (x1buf, x1keysb) = p3_bufs(si, ti)
            x1k = x1keysb[0]
            v_s = x_s[:, 1, 0:2]
            x2_ap = x_bf[:, pp]
            wb = PB_W0 if (j % 2 == 0) else PB_W1
            V_op(lambda e, j=j, wb=wb, pp=pp: e.tensor_tensor(
                    out=v_s[:, j % 2, :], in0=pb[wb][:],
                    in1=rf_s[:, pp, :], op=ALU.mult),
                 ["pb_w%d" % (j % 2), "rf%d" % pp],
                 ["v%d" % (j % 2), "x1"], "p3_v%d" % j)
            V_op(lambda e, j=j: e.scalar_tensor_tensor(
                    out=x2_ap[:, j, :], in0=v_s[:, j % 2, :],
                    scalar=btt_s[:, si, j:j + 1], in1=x1buf[:, j, :],
                    op0=ALU.add, op1=ALU.add),
                 ["v%d" % (j % 2), x1k, "btt%d" % si],
                 ["x2_%d_%d" % (pp, j), "xbf%d" % pp], "p3_x2%d" % j)
            S_op(lambda e, j=j: e.activation(
                    out=big_s[:, j, T:2 * T], in_=x2_ap[:, j, :],
                    func=AF.Square, bias=0.0, scale=1.0),
                 ["x2_%d_%d" % (pp, j)],
                 ["sqc%d" % j, "sq1", "sqP3", "wqT"], "p3_sq%d" % j)

        def mk_p3_filler(si, ti):
            def filler(c):
                if c >= 2:
                    p3_vx2(si, ti, c - 2)
                p3_wt_j(si, ti, c)
                if c == NCH - 1:
                    p3_vx2(si, ti, NCH - 2)
                    p3_vx2(si, ti, NCH - 1)
            return filler

        def p3_wtblock(si, ti):
            for c in range(NCH):
                if c >= 2:
                    p3_vx2(si, ti, c - 2)
                p3_wt_j(si, ti, c)
            p3_vx2(si, ti, NCH - 2)
            p3_vx2(si, ti, NCH - 1)

        def p3_stats(si, ti):
            pp = ti % 2
            t0 = si * cfg.seq + ti * T
            x2_ap = x_bf[:, pp]
            x2keys = ["x2_%d_%d" % (pp, j) for j in range(NCH)]
            sq_ap = big_s[:, :, T:2 * T]
            if dbg and si == 0 and ti == 0:
                dma("gpsimd", lambda e: e.dma_start(
                        out=dbg_d["dbg_x2"].rearrange("(c p) t -> p c t", p=128),
                        in_=x2_ap[:]),
                    x2keys, ["dbgx2"], "dbg_x2")

            def sums3a(e):
                return [e.matmul(pb[PB_S1][0:1, :], ones_s[:, 1:2],
                                 x2_ap[:, c, :],
                                 start=(c == 0), stop=(c == NCH - 1))
                        for c in range(NCH)]
            T_op(sums3a, x2keys + ["ones"], ["pb%d" % PB_S1], "p3_sums3a")

            def sums3b(e):
                return [e.matmul(pb[PB_S2][0:1, :], ones_s[:, 2:3],
                                 sq_ap[:, c, :],
                                 start=(c == 0), stop=(c == NCH - 1))
                        for c in range(NCH)]
            T_op(sums3b, ["sqc%d" % c for c in range(NCH)] + ["ones"],
                 ["pb%d" % PB_S2], "p3_sums3b")
            ln_stats(PB_S1, PB_S2, pp, 4 + pp, 2, t0,
                     "p3ln_%d_%d" % (si, ti), onchip_r64=True, store=False)

        def p3_B(si, ti, filler=None):
            pp = ti % 2
            t0 = si * cfg.seq + ti * T
            out_ap = x_s[:, 0]
            x2_ap = x_bf[:, pp]
            x2keys = ["x2_%d_%d" % (pp, j) for j in range(NCH)]
            r64_mm(pp, "p3r64_%d_%d" % (si, ti))

            def fin(j, wb):
                V_op(lambda e, j=j, wb=wb: e.scalar_tensor_tensor(
                        out=out_ap[:, j, :], in0=pb[wb][:],
                        scalar=bout2_s[:, j:j + 1], in1=x2_ap[:, j, :],
                        op0=ALU.add, op1=ALU.add),
                     ["pb_w%d" % (j % 2), "x2_%d_%d" % (pp, j), "bout2",
                      "outst"],
                     ["outb_%d" % j, "x0"], "p3_out%d" % j)
            ffn(x2_ap[:], x2keys, 4 + pp, pp, "r64_%d" % pp, fin,
                "p3f_%d_%d" % (si, ti), filler=filler)
            outkeys = ["outb_%d" % j for j in range(NCH)]
            dma("sync", lambda e: e.dma_start(out=out_r[:, :, t0:t0 + T],
                                              in_=out_ap[:]),
                outkeys, ["outst"], "p3_store")

        seqs = list(range(cfg.nseq))
        for si in seqs:
            p1_init(si)
        for si in seqs:
            p1_A0(si, 0)
        for ti in range(cfg.ntile):
            for si in seqs:
                p1_A1(si, ti)
                if not only_p1 and ti > 0:
                    p1_D1(si, ti - 1)
                    p1_D2(si, ti - 1)
                p1_A2(si, ti, prev_ti=(ti - 1 if (not only_p1 and ti > 0)
                                       else None))
                if not only_p1 and ti > 0:
                    p1_D3sp(si, ti - 1)
                p1_A4(si, ti)
                if ti + 1 < cfg.ntile:
                    p1_A0(si, ti + 1)
        if not only_p1:
            ti = cfg.ntile - 1
            for si in seqs:
                p1_D1(si, ti)
                p1_D2(si, ti)
                for c in range(NCH):
                    p1_D3_j(si, ti, c)
                p1_D3sp(si, ti)
            for si in seqs:
                eb2_final(si, 0, bqa2_s[:], "bqa2", "p1eb2_%d" % si)
            for ti in range(cfg.ntile + 1):
                for si in seqs:
                    if ti < cfg.ntile:
                        pbq_load(si, ti)
                    if ti > 0:
                        pbq_chain(si, ti - 1)
            for si in seqs:
                p1_fin(si)
            for si in seqs:
                p2_init(si)
            for ti in range(cfg.ntile + 1):
                for si in seqs:
                    if ti < cfg.ntile:
                        p2_load(si, ti, "p2a_%d_%d" % (si, ti))
                    if ti > 0:
                        p2a_chain(si, ti - 1)
            for si in seqs:
                eb2_final(si, 1, bka2_s[:, si:si + 1], "bka2",
                          "p2eb2_%d" % si)
            for ti in range(cfg.ntile + 1):
                for si in seqs:
                    if ti < cfg.ntile:
                        p2_load(si, ti, "p2b_%d_%d" % (si, ti))
                    if ti > 0:
                        p2b_chain(si, ti - 1)
            for si in seqs:
                p2_fin(si)
            for sidx, si in enumerate(seqs):
                p3_wtscale(si, reload=(sidx > 0))
                p3_load(si, 0)
                for ti in range(cfg.ntile):
                    if ti + 1 < cfg.ntile:
                        p3_load(si, ti + 1)
                    if ti == 0:
                        p3_wtblock(si, 0)
                    else:
                        p3_B(si, ti - 1, filler=mk_p3_filler(si, ti))
                    p3_stats(si, ti)
                p3_B(si, cfg.ntile - 1)
        else:
            def cpout(e):
                return e.dma_start(out=out_d.rearrange("(c p) t -> p c t", p=128)[:, :, 0:T],
                                   in_=x1_bf[:, (cfg.ntile - 1) % 2, :, :])
            dma("gpsimd", cpout,
                ["x1bf%d_%d" % ((cfg.ntile - 1) % 2, j) for j in range(NCH)],
                ["outst"], "p1only_out")

        go = stk.enter_context(nc.semaphore("sem_go"))
        clear_sems = list(sems.values()) + [go]
        run_stream, counters, lane_occ = p.finalize(sems, paranoid=paranoid)
        with nc.Block() as block:
            block.sync(lambda e: run_stream("sync", e, go=go))
            block.tensor(lambda e: run_stream("tensor", e, go=go))
            block.vector(lambda e: run_stream("vector", e, go=go))
            block.scalar(lambda e: run_stream("scalar", e, go=go))
            block.gpsimd(lambda e: run_stream("gpsimd", e,
                                              clear_sems=clear_sems, go=go))
            nc.all_engine_barrier()
            for s in clear_sems:
                nc.gpsimd.sem_clear(s)

    return nc


def _weight_maps(W):
    def chunked(a):
        return np.ascontiguousarray(
            np.asarray(a, np.float32).reshape(NCH, 128, -1).transpose(1, 0, 2))

    W1 = np.asarray(W["L0_ffn_W1"], np.float32)
    W2 = np.asarray(W["L0_ffn_W2"], np.float32)
    b1 = np.asarray(W["L0_ffn_b1"], np.float32)

    w2bd = np.zeros((128, NCH, FM), np.float32)
    for c in range(NCH):
        for ml in range(NCH):
            w2bd[ml * FI:(ml + 1) * FI, c, NCH * c + ml] = W2[NCH * c + ml, :]
    sel = np.zeros((FM, NCH, 128), np.float32)
    for c in range(NCH):
        for pp in range(128):
            sel[NCH * c + pp // FI, c, pp] = 1.0

    def col(a, n=128):
        return np.ascontiguousarray(
            np.asarray(a, np.float32).reshape(NCH, n).T)

    m = {
        "wq_l": chunked(W["L0_attn_Wq"]),
        "wk_l": chunked(W["L0_attn_Wk"]),
        "wt_l": chunked(W["L0_attn_Wt"]),
        "wqT_l": chunked(np.asarray(W["L0_attn_Wq"], np.float32).T),
        "wqa_l": chunked(W["L0_attn_Wqa"]),
        "wka_l": chunked(W["L0_attn_Wka"]),
        "win_l": chunked(W["L0_ffn_Win"]),
        "wout_l": np.ascontiguousarray(np.asarray(W["L0_ffn_Wout"], np.float32)),
        "w2bd_l": w2bd,
        "sel_l": sel.astype(BF16_NP),
        "i128_l": np.eye(128, dtype=BF16_NP),
        "amask_l": np.equal(np.arange(H)[:, None] // 2,
                            np.arange(NCH)[None, :]).astype(BF16_NP),
        "asel_l": np.equal(np.arange(H)[:, None] % 2,
                           np.arange(128)[None, :] // 64).astype(BF16_NP),
        "esel_l": np.equal(np.arange(H)[:, None, None],
                           2 * np.arange(NCH)[None, :, None]
                           + np.arange(128)[None, None, :] // 64)
                    .astype(BF16_NP),
        "w1f_l": np.ascontiguousarray(W1.reshape(-1).reshape(NCH, 128).T),
        "b1raw_l": np.ascontiguousarray(b1.reshape(-1).reshape(NCH, 128).T),
        "binf_l": np.ascontiguousarray(
            np.repeat(np.asarray(W["L0_ffn_bin_"], np.float32), FI)
            .reshape(NCH, 128).T),
        "bq_l": col(W["L0_attn_bq"]),
        "bk_l": col(W["L0_attn_bk"]),
        "bt_l": col(W["L0_attn_bt"]),
        "bout_l": col(W["L0_ffn_bout"]),
        "bqa_l": np.ascontiguousarray(
            np.asarray(W["L0_attn_bqa"], np.float32).reshape(H, 1)),
        "bka_l": np.ascontiguousarray(
            np.asarray(W["L0_attn_bka"], np.float32).reshape(H, 1)),
        "b2_l": np.ascontiguousarray(
            np.asarray(W["L0_ffn_b2"], np.float32).reshape(FM, 1)),
    }
    return m


_NC_CACHE = {}


def kernel(**inputs):
    x = np.asarray(inputs["L0_attn_hidden_states"], np.float32)
    B, S, Dm = x.shape
    assert Dm == D
    ncores = 8
    per = B // ncores
    cfg = CFG(per, S, 512)

    key = (per, S)
    if key not in _NC_CACHE:
        _NC_CACHE[key] = build_nc(cfg)
    nc = _NC_CACHE[key]

    wm = _weight_maps(inputs)
    in_maps = []
    for c in range(ncores):
        m = dict(wm)
        m["xT"] = np.ascontiguousarray(
            x[c * per:(c + 1) * per].transpose(2, 0, 1).reshape(D, per * S))
        in_maps.append(m)

    res = run_bass_kernel_spmd(nc, in_maps, core_ids=list(range(ncores)))
    outs = []
    for c in range(ncores):
        o = np.asarray(res.results[c]["out"])
        outs.append(o.reshape(D, per, S).transpose(1, 2, 0))
    return np.ascontiguousarray(np.concatenate(outs, axis=0),
                                dtype=np.float32)


# version 27 (final)
# speedup vs baseline: 1.0213x; 1.0027x over previous
import numpy as np
import ml_dtypes
from contextlib import ExitStack

import concourse.bass as bass
import concourse.mybir as mybir
from concourse.bass_utils import run_bass_kernel_spmd

F32 = mybir.dt.float32
BF16 = mybir.dt.bfloat16
AF = mybir.ActivationFunctionType
ALU = mybir.AluOpType
AX = mybir.AxisListType
BF16_NP = ml_dtypes.bfloat16

D = 1024
NCH = 8
H = 16
DH = 64
FM = 64
FI = 16
NEG_INIT = -30.0


class CFG:
    def __init__(self, nseq, seq, T):
        assert seq % T == 0
        self.nseq = nseq
        self.seq = seq
        self.T = T
        self.ntile = seq // T
        self.tok = nseq * seq


class _Op:
    __slots__ = ("eng", "emit", "deps", "is_dma", "need_inc", "val", "idx",
                 "tag", "ndma", "lanes")


class Prog:
    def __init__(self):
        self.ops = []
        self.bufs = {}

    def add(self, eng, emit, reads=(), writes=(), deps=(), dma=False, tag="",
            ndma=1):
        dl = list(deps)
        for k in reads:
            b = self.bufs.get(k)
            if b is not None and b["w"] is not None:
                dl.append(b["w"])
        for k in writes:
            b = self.bufs.get(k)
            if b is not None:
                if b["w"] is not None:
                    dl.append(b["w"])
                dl.extend(b["r"])
        op = _Op()
        op.eng = eng
        op.emit = emit
        op.is_dma = dma
        op.need_inc = False
        op.val = 0
        op.idx = len(self.ops)
        op.tag = tag
        op.ndma = ndma
        seen = set()
        op.deps = []
        for d in dl:
            if d is not None and id(d) not in seen:
                seen.add(id(d))
                op.deps.append(d)
        self.ops.append(op)
        for k in reads:
            self.bufs.setdefault(k, {"w": None, "r": []})["r"].append(op)
        for k in writes:
            self.bufs[k] = {"w": op, "r": []}
        return op

    @staticmethod
    def _needs_wait(dep, op):
        return True

    NLANES = 8

    def finalize(self, sems, paranoid=False):
        if paranoid:
            for i in range(1, len(self.ops)):
                prev = self.ops[i - 1]
                if prev not in self.ops[i].deps:
                    self.ops[i].deps.append(prev)
        for op in self.ops:
            if op.is_dma:
                op.need_inc = True
            for dep in op.deps:
                if self._needs_wait(dep, op):
                    dep.need_inc = True
        counters = {}
        lane_next = {}
        lane_occ = {}
        for op in self.ops:
            if op.is_dma:
                op.lanes = []
                for i in range(op.ndma):
                    ln = lane_next.get(op.eng, 0)
                    lane_next[op.eng] = (ln + 1) % self.NLANES
                    occ = lane_occ.get((op.eng, ln), 0) + 1
                    lane_occ[(op.eng, ln)] = occ
                    op.lanes.append((ln, occ))
            elif op.need_inc:
                key = (op.eng, False)
                counters[key] = counters.get(key, 0) + 1
                op.val = counters[key]
        by_eng = {}
        for op in self.ops:
            by_eng.setdefault(op.eng, []).append(op)

        def run_stream(eng_name, e, clear_sems=None, go=None):
            if eng_name == "gpsimd":
                for s in clear_sems or ():
                    e.sem_clear(s)
                if go is not None:
                    e.sem_inc(go, 1)
            elif go is not None:
                e.wait_ge(go, 1)
            waited = {}

            def wait_to(skey, val):
                if waited.get(skey, -1) < val:
                    e.wait_ge(sems[skey], val)
                    waited[skey] = val

            for op in by_eng.get(eng_name, ()):
                for dep in op.deps:
                    if not self._needs_wait(dep, op):
                        continue
                    if dep.is_dma:
                        for (ln, occ) in dep.lanes:
                            wait_to((dep.eng, "d", ln), 16 * occ)
                    else:
                        wait_to((dep.eng, False), dep.val)
                if op.is_dma:
                    for (ln, occ) in op.lanes:
                        if occ > 1:
                            wait_to((eng_name, "d", ln), 16 * (occ - 1))
                ret = op.emit(e)
                insts = ret if isinstance(ret, (list, tuple)) else [ret]
                if op.is_dma:
                    assert len(insts) == op.ndma, (op.tag, len(insts), op.ndma)
                    for it, (ln, occ) in zip(insts, op.lanes):
                        it.then_inc(sems[(eng_name, "d", ln)], 16)
                elif op.need_inc:
                    insts[-1].then_inc(sems[(eng_name, False)], 1)
            for (eng2, ln), occ in lane_occ.items():
                if eng2 == eng_name:
                    wait_to((eng_name, "d", ln), 16 * occ)
            e.drain()

        return run_stream, counters, lane_occ


def build_nc(cfg: CFG, dbg=False, paranoid=False, only_p1=False):
    T = cfg.T
    nc = bass.Bass("TRN2", target_bir_lowering=False, debug=False)

    def din(name, shape, dt=F32):
        return nc.dram_tensor(name, list(shape), dt, kind="ExternalInput").ap()

    xT = din("xT", [D, cfg.tok])
    wq_l = din("wq_l", [128, NCH, D]); wk_l = din("wk_l", [128, NCH, D])
    wt_l = din("wt_l", [128, NCH, D]); wqT_l = din("wqT_l", [128, NCH, D])
    wqa_l = din("wqa_l", [128, NCH, H]); wka_l = din("wka_l", [128, NCH, H])
    win_l = din("win_l", [128, NCH, FM]); wout_l = din("wout_l", [FM, D])
    w2bd_l = din("w2bd_l", [128, NCH, FM])
    sel_l = din("sel_l", [FM, NCH, 128], BF16)
    i128_l = din("i128_l", [128, 128], BF16)
    amask_l = din("amask_l", [H, NCH], BF16)
    asel_l = din("asel_l", [H, 128], BF16)
    esel_l = din("esel_l", [H, NCH, 128], BF16)
    w1f_l = din("w1f_l", [128, NCH]); b1raw_l = din("b1raw_l", [128, NCH])
    binf_l = din("binf_l", [128, NCH])
    bq_l = din("bq_l", [128, NCH]); bk_l = din("bk_l", [128, NCH])
    bt_l = din("bt_l", [128, NCH]); bout_l = din("bout_l", [128, NCH])
    bqa_l = din("bqa_l", [H, 1]); bka_l = din("bka_l", [H, 1])
    b2_l = din("b2_l", [FM, 1])

    out_d = nc.dram_tensor("out", [D, cfg.tok], F32, kind="ExternalOutput").ap()
    dbg_d = {}
    if dbg:
        for nm, shape in [("dbg_x1", [D, T]), ("dbg_ptq", [D, T]),
                          ("dbg_ptk", [D, T]), ("dbg_sc", [H, T]),
                          ("dbg_e", [H, T]), ("dbg_pq", [128, NCH]),
                          ("dbg_pk", [128, NCH]), ("dbg_x2", [D, T]),
                          ("dbg_r", [4, T]), ("dbg_btt", [128, NCH]),
                          ("dbg_st", [FM, T]), ("dbg_g", [FM, T]),
                          ("dbg_E", [D, T]), ("dbg_pp", [128, NCH]),
                          ("dbg_Pk", [128, NCH]), ("dbg_Sk", [H, 8]),
                          ("dbg_tiny", [H, 40]), ("dbg_Pcur", [128, NCH]),
                          ("dbg_S", [H, 40]), ("dbg_rs", [128, NCH])]:
            dbg_d[nm] = nc.dram_tensor(nm, shape, F32, kind="ExternalOutput").ap()

    ptq_d = nc.dram_tensor("ptq_d", [D, cfg.tok], BF16, kind="Internal").ap()
    ptk_d = nc.dram_tensor("ptk_d", [D, cfg.tok], BF16, kind="Internal").ap()
    x1_d = nc.dram_tensor("x1_d", [D, cfg.tok], BF16, kind="Internal").ap()
    stat_d = nc.dram_tensor("stat_d", [4, cfg.tok], F32, kind="Internal").ap()
    eq_d = nc.dram_tensor("eq_d", [H, cfg.tok], BF16, kind="Internal").ap()
    escq_d = nc.dram_tensor("escq_d", [H, cfg.tok], F32, kind="Internal").ap()
    ek_d = nc.dram_tensor("ek_d", [H, cfg.tok], BF16, kind="Internal").ap()
    nslot = 2 * cfg.nseq * cfg.ntile
    alpha_d = nc.dram_tensor("alpha_d", [nslot, H], F32, kind="Internal").ap()
    rs_d = nc.dram_tensor("rs_d", [2 * cfg.nseq, H], F32, kind="Internal").ap()

    xT_r = xT.rearrange("(c p) t -> p c t", p=128)
    out_r = out_d.rearrange("(c p) t -> p c t", p=128)
    ptq_r = ptq_d.rearrange("(c p) t -> p c t", p=128)
    ptk_r = ptk_d.rearrange("(c p) t -> p c t", p=128)
    x1_r = x1_d.rearrange("(c p) t -> p c t", p=128)
    eq_dr = eq_d.rearrange("(c a) t -> a c t", a=2)
    ek_dr = ek_d.rearrange("(c a) t -> a c t", a=2)

    p = Prog()

    with ExitStack() as stk:
        def sbt(name, shape, dt=F32):
            return stk.enter_context(nc.sbuf_tensor(name, list(shape), dt))

        wq_s = sbt("wq_s", [128, NCH, D], BF16)
        wk_s = sbt("wk_s", [128, NCH, D], BF16)
        wt_s = sbt("wt_s", [128, NCH, D], BF16)
        big_s = sbt("big_s", [128, NCH, D], BF16)
        wqa_s = sbt("wqa_s", [128, NCH, H], BF16)
        wkae_s = sbt("wkae_s", [128, NCH, H], BF16)
        wkaq_s = sbt("wkaq_s", [128, NCH, H], BF16)
        weff_s = sbt("weff_s", [128, NCH, H], BF16)
        win_s = sbt("win_s", [128, NCH, FM], BF16)
        wout_s = sbt("wout_s", [FM, D], BF16)
        w2bd_s = sbt("w2bd_s", [128, NCH, FM], BF16)
        sel_s = sbt("sel_s", [FM, NCH, 128], BF16)
        i128_s = sbt("i128_s", [128, 128], BF16)
        ones_s = sbt("ones_s", [128, 4], BF16)
        onerow_s = sbt("onerow_s", [1, 128], BF16)
        rows_s = sbt("rows_s", [1, 2 * D + FM + H], BF16)
        w1f_s = sbt("w1f_s", [128, NCH]); b1raw_s = sbt("b1raw_s", [128, NCH])
        binf_s = sbt("binf_s", [128, NCH]); b1f_s = sbt("b1f_s", [128, NCH])
        bq_s = sbt("bq_s", [128, NCH]); bk_s = sbt("bk_s", [128, NCH])
        bt_s = sbt("bt_s", [128, NCH]); bout_s = sbt("bout_s", [128, NCH])
        bout2_s = sbt("bout2_s", [128, NCH]); bqbt_s = sbt("bqbt_s", [128, NCH])
        bq_bf = sbt("bq_bf", [128, NCH], BF16); bk_bf = sbt("bk_bf", [128, NCH], BF16)
        bqa_s = sbt("bqa_s", [H, 1]); bka_s = sbt("bka_s", [H, 1])
        bqa2_s = sbt("bqa2_s", [H, 1]); bka2_s = sbt("bka2_s", [H, 2])
        b2_s = sbt("b2_s", [FM, 1]); b2_bf = sbt("b2_bf", [FM, 1], BF16)
        cswout_f = sbt("cswout_f", [FM, 1]); cswout_bf = sbt("cswout_bf", [FM, 1], BF16)
        sb128_f = sbt("sb128_f", [128, 1]); sb128_bf = sbt("sb128_bf", [128, 1], BF16)
        s2b_s = sbt("s2b_s", [1, 1])
        eps_s = sbt("eps_s", [1, 1]); zero_s = sbt("zero_s", [128, 1])

        x_s = sbt("x_s", [128, 2, NCH, T])
        x_bf = sbt("x_bf", [128, 2, NCH, T], BF16)
        x1_bf = sbt("x1_bf", [128, 2, NCH, T], BF16)
        ptq_s = sbt("ptq_s", [128, NCH, T], BF16)
        ptk_s = sbt("ptk_s", [128, NCH, T], BF16)
        st_bf = sbt("st_bf", [FM, 2, T], BF16)
        h_bf = sbt("h_bf", [128, 2, T], BF16)
        g_bf = sbt("g_bf", [FM, T], BF16)
        r64_s = sbt("r64_s", [64, 2, T], BF16)
        r16_s = sbt("r16_s", [16, 4, T], BF16)
        rf_s = sbt("rf_s", [128, 2, T])
        mrow_s = sbt("mrow_s", [1, 6, T], BF16)
        stat_s = sbt("stat_s", [1, 2, 2, T])
        esc_s = sbt("esc_s", [H, 4, T])
        ee_s = sbt("ee_s", [H, 2, T], BF16)
        eb_s = sbt("eb_s", [H, 2, T], BF16)
        tiny_s = sbt("tiny_s", [H, 40])
        amask_s = sbt("amask_s", [H, NCH], BF16)
        asel_s = sbt("asel_s", [H, 128], BF16)
        esel_s = sbt("esel_s", [H, NCH, 128], BF16)
        am_bf = sbt("am_bf", [H, 4, NCH], BF16)
        pool_s = sbt("pool_s", [128, 11, NCH])
        pq_s = sbt("pq_s", [128, 2, NCH]); pk_s = sbt("pk_s", [128, 2, NCH])
        pkb_bf = sbt("pkb_bf", [128, NCH], BF16)
        btt_s = sbt("btt_s", [128, 2, NCH])

        pb = [stk.enter_context(nc.psum_tensor("pb%d" % i, [128, T], F32))
              for i in range(8)]
        PB_S1, PB_FF, PB_E0, PB_E1, PB_W0, PB_W1, PB_S2, PB_SC = range(8)

        sems = {}
        for eng in ("tensor", "vector", "scalar", "gpsimd", "sync"):
            sems[(eng, False)] = stk.enter_context(
                nc.semaphore("sem_%s_c" % eng))
        for eng in ("sync", "scalar", "gpsimd"):
            for ln in range(Prog.NLANES):
                sems[(eng, "d", ln)] = stk.enter_context(
                    nc.semaphore("sem_%s_d%d" % (eng, ln)))

        def T_op(emit, reads, writes, tag=""):
            return p.add("tensor", emit, reads, writes, tag=tag)

        def V_op(emit, reads, writes, tag=""):
            return p.add("vector", emit, reads, writes, tag=tag)

        def S_op(emit, reads, writes, tag=""):
            return p.add("scalar", emit, reads, writes, tag=tag)

        def dma(eng, emit, reads, writes, tag="", ndma=1):
            return p.add(eng, emit, reads, writes, dma=True, tag=tag, ndma=ndma)

        V_op(lambda e: [
                e.memset(onerow_s[:], 1.0),
                e.memset(ones_s[:, 0:1], 1.0),
                e.memset(ones_s[:, 1:2], -1.0 / D),
                e.memset(ones_s[:, 2:3], 1.0 / D),
                e.memset(eps_s[:], 1e-5),
                e.memset(zero_s[:], 0.0)],
             [], ["ones", "eps", "zero", "onerow"], "memset")

        def cast_load(key, dst, src):
            dma("gpsimd", lambda e: e.dma_start(out=dst, in_=src), [], [key],
                "cast_" + key)

        cast_load("wq", wq_s[:], wq_l)
        cast_load("wk", wk_s[:], wk_l)
        cast_load("wt", wt_s[:], wt_l)
        cast_load("wqT", big_s[:], wqT_l)
        cast_load("wqa", wqa_s[:], wqa_l)
        cast_load("wkae", wkae_s[:], wka_l)
        cast_load("win", win_s[:], win_l)
        cast_load("wout", wout_s[:], wout_l)
        cast_load("w2bd", w2bd_s[:], w2bd_l)

        def plain_load(key, dst, src):
            dma("sync", lambda e: e.dma_start(out=dst, in_=src), [], [key],
                "load_" + key)

        plain_load("sel", sel_s[:], sel_l)
        plain_load("amask", amask_s[:], amask_l)
        plain_load("asel", asel_s[:], asel_l)
        plain_load("esel", esel_s[:], esel_l)
        plain_load("i128", i128_s[:], i128_l)
        plain_load("w1f", w1f_s[:], w1f_l)
        plain_load("b1raw", b1raw_s[:], b1raw_l)
        plain_load("binf", binf_s[:], binf_l)
        plain_load("bq", bq_s[:], bq_l)
        plain_load("bk", bk_s[:], bk_l)
        plain_load("bt", bt_s[:], bt_l)
        plain_load("bout", bout_s[:], bout_l)
        plain_load("bqa", bqa_s[:], bqa_l)
        plain_load("bka", bka_s[:], bka_l)
        plain_load("b2", b2_s[:], b2_l)

        S_op(lambda e: e.activation(out=wkae_s[:], in_=wkae_s[:], func=AF.Copy,
                                    bias=0.0, scale=0.125),
             ["wkae"], ["wkae"], "scale_wka")
        S_op(lambda e: e.activation(out=w2bd_s[:], in_=w2bd_s[:], func=AF.Copy,
                                    bias=0.0, scale=0.5),
             ["w2bd"], ["w2bd"], "scale_w2bd")

        def b1f_mk(e):
            return [e.tensor_scalar(out=b1f_s[:, c:c + 1], in0=w1f_s[:, c:c + 1],
                                    scalar1=binf_s[:, c:c + 1], scalar2=None,
                                    op0=ALU.mult)
                    for c in range(NCH)]
        V_op(b1f_mk, ["w1f", "binf"], ["b1f_m"], "b1f_m")
        V_op(lambda e: e.tensor_add(b1f_s[:], b1f_s[:], b1raw_s[:]),
             ["b1f_m", "b1raw"], ["b1f"], "b1f")
        V_op(lambda e: e.tensor_copy(bq_bf[:], bq_s[:]), ["bq"], ["bq_bf"])
        V_op(lambda e: e.tensor_copy(bk_bf[:], bk_s[:]), ["bk"], ["bk_bf"])
        V_op(lambda e: e.tensor_copy(b2_bf[:], b2_s[:]), ["b2"], ["b2_bf"])
        V_op(lambda e: e.tensor_add(bqbt_s[:], bq_s[:], bt_s[:]),
             ["bq", "bt"], ["bqbt"])

        def weff_mm(e):
            ins = []
            for dc in range(NCH):
                for jc in range(NCH):
                    ins.append(e.matmul(
                        pb[PB_SC][:, dc * H:(dc + 1) * H],
                        big_s[:, jc, dc * 128:(dc + 1) * 128],
                        wqa_s[:, jc, :],
                        start=(jc == 0), stop=(jc == NCH - 1)))
            return ins
        T_op(weff_mm, ["wqT", "wqa"], ["pb_sc"], "weff_mm")
        S_op(lambda e: e.activation(
                out=weff_s[:].rearrange("p c h -> p (c h)"),
                in_=pb[PB_SC][:, 0:NCH * H],
                func=AF.Copy, bias=0.0, scale=0.125),
             ["pb_sc"], ["weff"], "weff")

        CSQ, CSK, CSWIN, CSWEFF = 0, D, 2 * D, 2 * D + FM

        def cs_generic(name, src, ncols, width, dst_off):
            for j in range(ncols):
                def mm(e, j=j):
                    return [e.matmul(pb[PB_SC][0:1, 0:width], ones_s[:, 0:1],
                                     src[:, kc, j * width:(j + 1) * width],
                                     start=(kc == 0), stop=(kc == NCH - 1))
                            for kc in range(NCH)]
                T_op(mm, [name, "ones"], ["pb_sc"], "cs_mm_%s%d" % (name, j))
                S_op(lambda e, j=j: e.activation(
                        out=rows_s[0:1, dst_off + j * width:dst_off + (j + 1) * width],
                        in_=pb[PB_SC][0:1, 0:width], func=AF.Copy, bias=0.0,
                        scale=1.0),
                     ["pb_sc"], ["rows"], "cs_ev_%s%d" % (name, j))

        cs_generic("wq", wq_s, NCH, 128, CSQ)
        cs_generic("wk", wk_s, NCH, 128, CSK)
        cs_generic("win", win_s, 1, FM, CSWIN)
        cs_generic("weff", weff_s, 1, H, CSWEFF)

        def bqa2_mm(e):
            return [e.matmul(pb[PB_SC][0:H, 0:1], wqa_s[:, kc, :],
                             bq_bf[:, kc:kc + 1],
                             start=(kc == 0), stop=(kc == NCH - 1))
                    for kc in range(NCH)]
        T_op(bqa2_mm, ["wqa", "bq_bf"], ["pb_sc"], "bqa2_mm")
        V_op(lambda e: e.tensor_scalar(out=bqa2_s[:], in0=pb[PB_SC][0:H, 0:1],
                                       scalar1=bqa_s[:, 0:1], scalar2=0.125,
                                       op0=ALU.add, op1=ALU.mult),
             ["pb_sc", "bqa"], ["bqa2"], "bqa2")

        def bout2_mm(e):
            return [e.matmul(pb[PB_SC][:, j:j + 1],
                             wout_s[:, j * 128:(j + 1) * 128],
                             b2_bf[:, 0:1], start=True, stop=True)
                    for j in range(NCH)]
        T_op(bout2_mm, ["wout", "b2_bf"], ["pb_sc"], "bout2_mm")
        V_op(lambda e: [e.tensor_scalar(
                out=bout2_s[:, j:j + 1], in0=pb[PB_SC][:, j:j + 1],
                scalar1=bout_s[:, j:j + 1], scalar2=0.5,
                op0=ALU.add, op1=ALU.mult) for j in range(NCH)],
             ["pb_sc", "bout"], ["bout2"], "bout2")
        V_op(lambda e: e.reduce_sum(cswout_f[:], wout_s[:], axis=AX.X),
             ["wout"], ["cswf"], "cswf")
        V_op(lambda e: e.tensor_scalar(out=cswout_bf[:], in0=cswout_f[:],
                                       scalar1=-1.0 / D, scalar2=None,
                                       op0=ALU.mult),
             ["cswf"], ["cswout"], "cswout")
        V_op(lambda e: e.reduce_sum(sb128_f[:], bout2_s[:], axis=AX.X),
             ["bout2"], ["sb128f"], "sb128f")
        V_op(lambda e: e.tensor_copy(sb128_bf[:], sb128_f[:]),
             ["sb128f"], ["sb128b"], "sb128b")
        T_op(lambda e: e.matmul(pb[PB_S2][0:1, 0:1], ones_s[:, 1:2],
                                sb128_bf[:], start=True, stop=True),
             ["sb128b", "ones"], ["pb%d" % PB_S2], "s2bmm")
        S_op(lambda e: e.activation(out=s2b_s[:], in_=pb[PB_S2][0:1, 0:1],
                                    func=AF.Copy, bias=0.0, scale=1.0),
             ["pb%d" % PB_S2], ["s2b"], "s2bcp")

        def ln_stats(pba, pbb, par, mslot, stat_row, t0, tag,
                     onchip_r64=False, store=True, madd=None):
            key = "stt%d" % par

            if madd is None:
                V_op(lambda e: e.tensor_copy(stat_s[0:1, par, 0, :],
                                             pb[pba][0:1, :]),
                     ["pb%d" % pba], [key + "_m"], tag + "_cm")
            else:
                madd_ap, madd_key = madd
                V_op(lambda e: e.tensor_scalar(
                        out=stat_s[0:1, par, 0, :], in0=pb[pba][0:1, :],
                        scalar1=madd_ap, scalar2=None, op0=ALU.add),
                     ["pb%d" % pba, madd_key], [key + "_m"], tag + "_cm")
            V_op(lambda e: e.tensor_mul(stat_s[0:1, par, 1, :],
                                        stat_s[0:1, par, 0, :],
                                        stat_s[0:1, par, 0, :]),
                 [key + "_m"],
                 [key + "_msq", key + "_v", key + "_ln", key + "_r"],
                 tag + "_msq")
            V_op(lambda e: e.scalar_tensor_tensor(
                    out=stat_s[0:1, par, 1, :], in0=stat_s[0:1, par, 1, :],
                    scalar=-1.0, in1=pb[pbb][0:1, :],
                    op0=ALU.mult, op1=ALU.add),
                 [key + "_msq", "pb%d" % pbb], [key + "_v"], tag + "_v")
            S_op(lambda e: e.activation(out=stat_s[0:1, par, 1, :],
                                        in_=stat_s[0:1, par, 1, :], func=AF.Ln,
                                        bias=eps_s[0:1, 0:1], scale=1.0),
                 [key + "_v", "eps"], [key + "_ln"], tag + "_ln")
            S_op(lambda e: e.activation(out=stat_s[0:1, par, 1, :],
                                        in_=stat_s[0:1, par, 1, :], func=AF.Exp,
                                        bias=0.0, scale=-0.5),
                 [key + "_ln"], [key + "_r"], tag + "_r")
            V_op(lambda e: e.tensor_copy(mrow_s[0:1, mslot, :],
                                         stat_s[0:1, par, 0, :]),
                 [key + "_m"], ["mrow%d" % mslot], tag + "_mbf")
            if store:
                dma("scalar", lambda e: e.dma_start(
                        out=stat_d[stat_row:stat_row + 1, t0:t0 + T],
                        in_=stat_s[0:1, par, 1, :]),
                    [key + "_r"], ["statd%d_%d" % (stat_row, t0)], tag + "_st")
            if onchip_r64:
                V_op(lambda e: e.tensor_copy(eb_s[0:1, par, :],
                                             stat_s[0:1, par, 1, :]),
                     [key + "_r"], ["ebbf%d" % par], tag + "_rb")

        def r64_mm(par, tag):
            T_op(lambda e: e.matmul(pb[PB_SC][0:64, :],
                                    onerow_s[0:1, 0:64],
                                    eb_s[0:1, par, :],
                                    start=True, stop=True),
                 ["ebbf%d" % par, "onerow"], ["pb_sc"], tag + "_rmm")
            S_op(lambda e: e.activation(out=r64_s[:, par, :],
                                        in_=pb[PB_SC][0:64, :],
                                        func=AF.Copy, bias=0.0, scale=1.0),
                 ["pb_sc"], ["r64_%d" % par], tag + "_rcp")

        def bcast_r(stat_row, t0, dst_key, dst_ap, nparts, tag,
                    eng="gpsimd"):
            dma(eng, lambda e: e.dma_start(
                    out=dst_ap.unsqueeze(1),
                    in_=stat_d[stat_row:stat_row + 1, t0:t0 + T]
                        .partition_broadcast(nparts)),
                ["statd%d_%d" % (stat_row, t0)], [dst_key], tag)

        def ffn(xin_bf, xin_keys, mslot, reg, r64_key, fin, tag, tap=False,
                filler=None):
            def win_mm(e):
                ins = [e.matmul(pb[PB_FF][0:FM, :], win_s[:, c, :],
                                xin_bf[:, c, :], start=(c == 0), stop=False)
                       for c in range(NCH)]
                ins.append(e.matmul(pb[PB_FF][0:FM, :],
                                    rows_s[0:1, CSWIN:CSWIN + FM],
                                    mrow_s[0:1, mslot, :], start=False, stop=True))
                return ins
            T_op(win_mm, list(xin_keys) + ["mrow%d" % mslot, "win", "rows"],
                 ["pb_ff"], tag + "_win")
            V_op(lambda e: e.tensor_tensor(out=st_bf[:, reg, :],
                                           in0=pb[PB_FF][0:FM, :],
                                           in1=r64_s[:, reg, :], op=ALU.mult),
                 ["pb_ff", r64_key], ["st%d" % reg], tag + "_st")
            if tap:
                dma("gpsimd", lambda e: e.dma_start(out=dbg_d["dbg_st"],
                                                    in_=st_bf[:, reg, :]),
                    ["st%d" % reg], ["dbgst"], "dbg_st")
            for c in range(NCH):
                eb = PB_E0 if (c % 2 == 0) else PB_E1
                T_op(lambda e, c=c, eb=eb: e.matmul(
                        pb[eb][:], sel_s[:, c, :], st_bf[:, reg, :],
                        start=True, stop=True),
                     ["st%d" % reg, "sel"], ["pb_e%d" % (c % 2)],
                     tag + "_ex%d" % c)
                if filler is not None:
                    filler(c)
                S_op(lambda e, c=c, eb=eb: e.activation(
                        out=h_bf[:, c % 2, :], in_=pb[eb][:], func=AF.Relu,
                        bias=b1f_s[:, c:c + 1], scale=w1f_s[:, c:c + 1]),
                     ["pb_e%d" % (c % 2), "b1f", "w1f"],
                     ["h%d" % (c % 2)], tag + "_rl%d" % c)
                T_op(lambda e, c=c: e.matmul(
                        pb[PB_FF][0:FM, :], w2bd_s[:, c, :], h_bf[:, c % 2, :],
                        start=(c == 0), stop=(c == NCH - 1)),
                     ["h%d" % (c % 2), "w2bd"], ["pb_ff"], tag + "_bd%d" % c)
            S_op(lambda e: e.activation(out=g_bf[:], in_=pb[PB_FF][0:FM, :],
                                        func=AF.Copy, bias=0.0, scale=1.0),
                 ["pb_ff"], ["g"], tag + "_g")
            if tap:
                dma("gpsimd", lambda e: e.dma_start(out=dbg_d["dbg_g"],
                                                    in_=g_bf[:]),
                    ["g"], ["dbgg"], "dbg_g")
            for j in range(NCH):
                wb = PB_W0 if (j % 2 == 0) else PB_W1
                T_op(lambda e, j=j, wb=wb: e.matmul(
                        pb[wb][:], wout_s[:, j * 128:(j + 1) * 128], g_bf[:],
                        start=True, stop=True),
                     ["g", "wout"], ["pb_w%d" % (j % 2)], tag + "_wo%d" % j)
                fin(j, wb)

        def score_pass(si, ti, par, side, t0, tag, sc_bank=None,
                       spill=None, rslot=None):
            if rslot is None:
                rslot = par
            base = (si * 2 + side) * 8
            if sc_bank is None:
                sc_bank, sc_key = PB_SC, "pb_sc"
            else:
                sc_bank, sc_key = sc_bank
            rm_prev = tiny_s[:, base + (ti % 2):base + (ti % 2) + 1]
            rm_new = tiny_s[:, base + ((ti + 1) % 2):base + ((ti + 1) % 2) + 1]
            es = 2 * par + ti % 2
            kA = "escA%d" % es
            V_op(lambda e: e.tensor_tensor(
                    out=esc_s[:, es, :], in0=pb[sc_bank][0:H, :],
                    in1=r16_s[:, rslot, :], op=ALU.mult),
                 [sc_key, "r16_%d" % rslot], [kA], tag + "_sc")
            V_op(lambda e: e.reduce_max(tiny_s[:, base + 2:base + 3],
                                        esc_s[:, es, :], axis=AX.X),
                 [kA], ["tm%d" % base], tag + "_tm")
            V_op(lambda e: e.tensor_max(rm_new, rm_prev,
                                        tiny_s[:, base + 2:base + 3]),
                 ["tm%d" % base, "rm%d_%d" % (base, ti % 2)],
                 ["rm%d_%d" % (base, (ti + 1) % 2)], tag + "_nm")
            if spill is not None:
                dma("sync" if par == 0 else "scalar",
                    lambda e: e.dma_start(out=spill[:, t0:t0 + T],
                                          in_=esc_s[:, es, :]),
                    [kA], ["escd_%d" % t0], tag + "_escd")

        def eb2_final(si, side, bias2_ap, bias2_k, tag):
            base = (si * 2 + side) * 8
            rmf = base + (cfg.ntile % 2)
            V_op(lambda e: e.tensor_sub(tiny_s[:, base + 7:base + 8], bias2_ap,
                                        tiny_s[:, rmf:rmf + 1]),
                 ["rm%d_%d" % (base, cfg.ntile % 2), bias2_k],
                 ["mx%d" % base], tag + "_eb2")

        def pool_core(si, ti, par, side, pcur, t0, tag,
                      buf, buf_keys, rslot=None):
            if rslot is None:
                rslot = par
            base = (si * 2 + side) * 8
            es = 2 * par + ti % 2
            kA = "escA%d" % es
            kB = "escB%d" % par
            S_op(lambda e: e.activation(out=ee_s[:, par, :],
                                        in_=esc_s[:, es, :],
                                        func=AF.Exp,
                                        bias=tiny_s[:, base + 7:base + 8],
                                        scale=1.0,
                                        accum_out=tiny_s[:, base + 2:base + 3]),
                 ["mx%d" % base, kA, "tm%d" % base],
                 [kB, "ts%d" % base], tag + "_exp")
            V_op(lambda e: e.tensor_add(tiny_s[:, base + 5:base + 6],
                                        tiny_s[:, base + 5:base + 6],
                                        tiny_s[:, base + 2:base + 3]),
                 ["ts%d" % base, "S%d" % base], ["S%d" % base], tag + "_sa")
            V_op(lambda e: e.tensor_tensor(out=eb_s[:, par, :],
                                           in0=ee_s[:, par, :],
                                           in1=r16_s[:, rslot, :],
                                           op=ALU.mult),
                 [kB, "r16_%d" % rslot], ["ebbf%d" % par], tag + "_eb")
            for c in range(NCH):
                ebk = PB_E0 if c % 2 == 0 else PB_E1
                T_op(lambda e, c=c, ebk=ebk: e.matmul(
                        pb[ebk][:], esel_s[:, c, :], eb_s[:, par, :],
                        start=True, stop=True),
                     ["ebbf%d" % par, "esel"], ["pb_e%d" % (c % 2)],
                     tag + "_em%d" % c)
                V_op(lambda e, c=c, ebk=ebk: e.scalar_tensor_tensor(
                        out=h_bf[:, 0, :], in0=buf[:, c, :], scalar=1.0,
                        in1=pb[ebk][:], op0=ALU.mult, op1=ALU.mult,
                        accum_out=pool_s[:, 2, c:c + 1]),
                     list(buf_keys) + ["pb_e%d" % (c % 2)],
                     ["Ppart%d" % c, "h0"], tag + "_ttr%d" % c)
            p_update(si, side, pcur, tag)

        def p_update(si, side, pcur, tag):
            V_op(lambda e: e.tensor_add(pool_s[:, pcur, :], pool_s[:, pcur, :],
                                        pool_s[:, 2, :]),
                 ["P%d" % pcur] + ["Ppart%d" % c for c in range(NCH)],
                 ["P%d" % pcur], tag + "_pu")

        def pool_finalize(si, side, pcur, rslot, tag):
            base = (si * 2 + side) * 8
            V_op(lambda e: e.reciprocal(tiny_s[:, base + 6:base + 7],
                                        tiny_s[:, base + 5:base + 6]),
                 ["S%d" % base], ["rsv%d" % base], tag + "_rs")
            bslot = si * 2 + side
            V_op(lambda e: e.tensor_scalar(
                    out=am_bf[:, bslot, :], in0=amask_s[:],
                    scalar1=tiny_s[:, base + 6:base + 7], scalar2=None,
                    op0=ALU.mult),
                 ["rsv%d" % base, "amask"], ["am%d" % bslot], tag + "_rm")
            T_op(lambda e: e.matmul(pb[PB_S2][:, 0:NCH], asel_s[:],
                                    am_bf[:, bslot, :],
                                    start=True, stop=True),
                 ["am%d" % bslot, "asel"], ["pb%d" % PB_S2], tag + "_rmm2")
            V_op(lambda e: e.tensor_tensor(out=pool_s[:, 6, :],
                                           in0=pool_s[:, pcur, :],
                                           in1=pb[PB_S2][:, 0:NCH],
                                           op=ALU.mult),
                 ["P%d" % pcur, "pb%d" % PB_S2], ["Pmean"], tag + "_pm")

        def p1_A0(si, ti):
            par = si % 2
            t0 = si * cfg.seq + ti * T
            xk, xbk, sqk = "x%d" % par, "xbf%d" % par, "sq%d" % par
            dma("sync", lambda e: e.dma_start(out=x_s[:, par, :, :],
                                              in_=xT_r[:, :, t0:t0 + T]),
                [xk], [xk], "p1_xload")
            V_op(lambda e: e.tensor_copy(x_bf[:, par, :, :],
                                         x_s[:, par, :, :]),
                 [xk], [xbk], "p1_xcast")
            sq_ap = big_s[:, :, par * T:(par + 1) * T]
            S_op(lambda e: e.activation(out=sq_ap,
                                        in_=x_bf[:, par, :, :],
                                        func=AF.Square, bias=0.0, scale=1.0),
                 [xbk], [sqk, "wqT"], "p1_sq")

        def p1_A1(si, ti):
            par = si % 2
            t0 = si * cfg.seq + ti * T
            xk, xbk, sqk = "x%d" % par, "xbf%d" % par, "sq%d" % par
            sq_ap = big_s[:, :, par * T:(par + 1) * T]

            def sums1(e):
                ins = []
                for c in range(NCH):
                    ins.append(e.matmul(pb[PB_S1][0:1, :], ones_s[:, 1:2],
                                        x_bf[:, par, c, :],
                                        start=(c == 0), stop=(c == NCH - 1)))
                for c in range(NCH):
                    ins.append(e.matmul(pb[PB_S2][0:1, :], ones_s[:, 2:3],
                                        sq_ap[:, c, :],
                                        start=(c == 0), stop=(c == NCH - 1)))
                return ins
            T_op(sums1, [xbk, sqk, "ones"],
                 ["pb%d" % PB_S1, "pb%d" % PB_S2], "p1_sums1")
            ln_stats(PB_S1, PB_S2, par, par, 0, t0, "p1ln1_%d_%d" % (si, ti),
                     onchip_r64=True, store=False)


        def p1_A2(si, ti, prev_ti=None):
            par = si % 2
            t0 = si * cfg.seq + ti * T
            xk, xbk, sqk = "x%d" % par, "xbf%d" % par, "sq%d" % par
            sq_ap = big_s[:, :, par * T:(par + 1) * T]
            filler = None
            if prev_ti is not None:
                filler = lambda c: p1_D3_j(si, prev_ti, c)
            r64_mm(par, "p1r64_%d_%d" % (si, ti))

            def fin(j, wb):
                V_op(lambda e, j=j, wb=wb: e.scalar_tensor_tensor(
                        out=x1_bf[:, par, j, :], in0=pb[wb][:],
                        scalar=bout2_s[:, j:j + 1], in1=x_s[:, par, j, :],
                        op0=ALU.add, op1=ALU.add),
                     ["pb_w%d" % (j % 2), xk, "bout2"],
                     ["x1bf%d_%d" % (par, j)], "p1_x1_%d" % j)
            ffn(x_bf[:, par, :, :], [xbk], par, par, "r64_%d" % par, fin,
                "p1f_%d_%d" % (si, ti), tap=(dbg and si == 0 and ti == 0),
                filler=filler)

            x1keys = ["x1bf%d_%d" % (par, j) for j in range(NCH)]
            dma("scalar", lambda e: e.dma_start(out=x1_r[:, :, t0:t0 + T],
                                                in_=x1_bf[:, par, :, :]),
                x1keys, ["x1d_%d" % t0], "p1_x1sp")
            if dbg and si == 0 and ti == 0:
                dma("gpsimd", lambda e: e.dma_start(
                        out=dbg_d["dbg_x1"].rearrange("(c p) t -> p c t", p=128),
                        in_=x1_bf[:, par, :, :]),
                    x1keys, ["dbgx1"], "dbg_x1")

        def p1_A4(si, ti):
            par = si % 2
            t0 = si * cfg.seq + ti * T
            xk, xbk, sqk = "x%d" % par, "xbf%d" % par, "sq%d" % par
            sq_ap = big_s[:, :, par * T:(par + 1) * T]
            x1keys = ["x1bf%d_%d" % (par, j) for j in range(NCH)]
            S_op(lambda e: e.activation(out=sq_ap,
                                        in_=x1_bf[:, par, :, :],
                                        func=AF.Square, bias=0.0, scale=1.0),
                 x1keys, [sqk, "wqT"], "p1_sq1")

            def sums2(e):
                ins = [e.matmul(pb[PB_S1][0:1, :], cswout_bf[:], g_bf[:],
                                start=True, stop=False),
                       e.matmul(pb[PB_S1][0:1, :], onerow_s[0:1, 0:1],
                                mrow_s[0:1, par, :],
                                start=False, stop=True)]
                for c in range(NCH):
                    ins.append(e.matmul(pb[PB_S2][0:1, :], ones_s[:, 2:3],
                                        sq_ap[:, c, :],
                                        start=(c == 0), stop=(c == NCH - 1)))
                return ins
            T_op(sums2, x1keys + [sqk, "ones", "g", "cswout", "onerow",
                                  "mrow%d" % par],
                 ["pb%d" % PB_S1, "pb%d" % PB_S2], "p1_sums2")
            ln_stats(PB_S1, PB_S2, par, 2 + par, 1, t0, "p1ln2_%d_%d" % (si, ti),
                     madd=(s2b_s[0:1, 0:1], "s2b"))
            bcast_r(1, t0, "r16_%d" % par, r16_s[:, par, :], H, "p1_r16")

        BANKS = [(PB_W0, "pb_w0"), (PB_W1, "pb_w1"),
                 (PB_E0, "pb_e0"), (PB_E1, "pb_e1")]

        def p1_D1(si, ti):
            par = si % 2
            t0 = si * cfg.seq + ti * T
            x1keys = ["x1bf%d_%d" % (par, j) for j in range(NCH)]

            def sc_mm(e):
                ins = [e.matmul(pb[PB_SC][0:H, :], weff_s[:, kc, :],
                                x1_bf[:, par, kc, :],
                                start=(kc == 0), stop=False)
                       for kc in range(NCH)]
                ins.append(e.matmul(pb[PB_SC][0:H, :],
                                    rows_s[0:1, CSWEFF:CSWEFF + H],
                                    mrow_s[0:1, 2 + par, :],
                                    start=False, stop=True))
                return ins
            T_op(sc_mm, x1keys + ["mrow%d" % (2 + par), "weff", "rows"],
                 ["pb_sc"], "p1_scq")
            score_pass(si, ti, par, 0, t0, "p1sm_%d_%d" % (si, ti),
                       spill=escq_d)
            if dbg and si == 0 and ti == 0:
                dma("sync", lambda e: e.dma_start(out=dbg_d["dbg_sc"],
                                                  in_=esc_s[:, 2 * par, :]),
                    ["escA%d" % (2 * par)], ["dbgsc"], "dbg_sc")

        def p1_D2(si, ti):
            par = si % 2
            t0 = si * cfg.seq + ti * T
            x1keys = ["x1bf%d_%d" % (par, j) for j in range(NCH)]
            for j in range(NCH):
                wb, wkey = BANKS[j % 4]

                def wq_mm(e, j=j, wb=wb):
                    ins = [e.matmul(pb[wb][:],
                                    wq_s[:, kc, j * 128:(j + 1) * 128],
                                    x1_bf[:, par, kc, :],
                                    start=(kc == 0), stop=False)
                           for kc in range(NCH)]
                    ins.append(e.matmul(
                        pb[wb][:], rows_s[0:1, CSQ + j * 128:CSQ + (j + 1) * 128],
                        mrow_s[0:1, 2 + par, :], start=False, stop=True))
                    return ins
                T_op(wq_mm, x1keys + ["mrow%d" % (2 + par), "wq", "rows"],
                     [wkey], "p1_wq%d" % j)
                V_op(lambda e, j=j, wb=wb: e.tensor_copy(ptq_s[:, j, :],
                                                         pb[wb][:]),
                     [wkey], ["ptq_%d" % j], "p1_ptq%d" % j)

            ptqkeys = ["ptq_%d" % j for j in range(NCH)]
            dma("sync", lambda e: e.dma_start(out=ptq_r[:, :, t0:t0 + T],
                                              in_=ptq_s[:, :, :]),
                ptqkeys, ["ptqd_%d" % t0], "p1_ptqsp")
            if dbg and si == 0 and ti == 0:
                dma("gpsimd", lambda e: e.dma_start(
                        out=dbg_d["dbg_ptq"].rearrange("(c p) t -> p c t", p=128),
                        in_=ptq_s[:, :, :]),
                    ptqkeys, ["dbgptq"], "dbg_ptq")

        def p1_D3_j(si, ti, j):
            par = si % 2
            x1keys = ["x1bf%d_%d" % (par, j2) for j2 in range(NCH)]
            for j in (j,):
                wb, wkey = BANKS[j % 2]

                def wk_mm(e, j=j, wb=wb):
                    ins = [e.matmul(pb[wb][:],
                                    wk_s[:, kc, j * 128:(j + 1) * 128],
                                    x1_bf[:, par, kc, :],
                                    start=(kc == 0), stop=False)
                           for kc in range(NCH)]
                    ins.append(e.matmul(
                        pb[wb][:], rows_s[0:1, CSK + j * 128:CSK + (j + 1) * 128],
                        mrow_s[0:1, 2 + par, :], start=False, stop=True))
                    return ins
                T_op(wk_mm, x1keys + ["mrow%d" % (2 + par), "wk", "rows"],
                     [wkey], "p1_wk%d" % j)
                V_op(lambda e, j=j, wb=wb: e.tensor_copy(ptk_s[:, j, :],
                                                         pb[wb][:]),
                     [wkey], ["ptk_%d" % j], "p1_ptk%d" % j)

        def p1_D3sp(si, ti):
            t0 = si * cfg.seq + ti * T
            ptkkeys = ["ptk_%d" % j for j in range(NCH)]
            dma("sync", lambda e: e.dma_start(out=ptk_r[:, :, t0:t0 + T],
                                              in_=ptk_s[:, :, :]),
                ptkkeys, ["ptkd_%d" % t0], "p1_ptksp")

        def tile_buf(primary, par, ti):
            if ti % 2 == 0:
                return primary
            ap = big_s[:, :, par * T:(par + 1) * T]
            keys = (["sq0", "u", "wqT"] if par == 0
                    else ["sq1", "sqP3", "wqT"])
            return ap, keys

        def pbq_buf(par, ti):
            primary = ((ptq_s, ["ptq_%d" % j for j in range(NCH)])
                       if par == 0
                       else (ptk_s, ["ptk_%d" % j for j in range(NCH)]))
            return tile_buf(primary, par, ti)

        def pbq_load(si, ti):
            par = si % 2
            t0 = si * cfg.seq + ti * T
            ring = "sync" if par == 0 else "scalar"
            buf, bkeys = pbq_buf(par, ti)
            dma(ring, lambda e: e.dma_start(out=buf[:, :, :],
                                            in_=ptq_r[:, :, t0:t0 + T]),
                ["ptqd_%d" % t0] + bkeys, bkeys, "pbq_pld")
            rs_ = 2 * par + ti % 2
            bcast_r(1, t0, "r16_%d" % rs_, r16_s[:, rs_, :], H, "pbq_r16")

        def pbq_chain(si, ti):
            par = si % 2
            t0 = si * cfg.seq + ti * T
            buf, bkeys = pbq_buf(par, ti)
            ring = "sync" if par == 0 else "scalar"
            es = 2 * par + ti % 2
            dma(ring, lambda e: e.dma_start(out=esc_s[:, es, :],
                                            in_=escq_d[:, t0:t0 + T]),
                ["escd_%d" % t0, "escA%d" % es], ["escA%d" % es],
                "pbq_escld")
            pool_core(si, ti, par, 0, q_slot(si), t0,
                      "pbq_%d_%d" % (si, ti), buf, bkeys,
                      rslot=2 * par + ti % 2)
            if dbg and si == 0 and ti == 0:
                dma("gpsimd", lambda e: e.dma_start(out=dbg_d["dbg_e"],
                                                    in_=ee_s[:, par, :]),
                    ["escB%d" % par], ["dbge"], "dbg_e")


        def q_slot(si):
            return 0 if si == 0 else 8

        def k_slot(si):
            return 3 if si == 0 else 9

        def p1_init(si):
            base_q = (si * 2 + 0) * 8
            qs_ = q_slot(si)
            V_op(lambda e: [
                    e.memset(tiny_s[:, base_q + 0:base_q + 1], NEG_INIT),
                    e.memset(tiny_s[:, base_q + 5:base_q + 6], 0.0),
                    e.memset(pool_s[:, qs_, :], 0.0)],
                 ["P%d" % qs_, "S%d" % base_q],
                 ["rm%d_0" % base_q, "S%d" % base_q, "P%d" % qs_], "p1_init")

        def p1_fin(si):
            if dbg and si == 0:
                dma("gpsimd", lambda e: e.dma_start(out=dbg_d["dbg_Pk"],
                                                    in_=pool_s[:, q_slot(si), :]),
                    ["P%d" % q_slot(si)], ["dbgPq"], "dbg_Pq")
                dma("gpsimd", lambda e: e.dma_start(
                        out=dbg_d["dbg_Sk"][:, 0:1],
                        in_=tiny_s[:, 5:6],
                        allow_slow_non_contiguous=True),
                    ["S0"], ["dbgS0"], "dbg_S0")
            pool_finalize(si, 0, q_slot(si), 2 * si + 0, "p1pf_%d" % si)
            V_op(lambda e: e.tensor_add(pq_s[:, si, :], pool_s[:, 6, :],
                                        bq_s[:]),
                 ["Pmean", "bq"], ["pq%d" % si], "p1_pq")
            if dbg and si == 0:
                dma("sync", lambda e: e.dma_start(out=dbg_d["dbg_pq"],
                                                  in_=pq_s[:, si, :]),
                    ["pq%d" % si], ["dbgpq"], "dbg_pq")

            wkaq_buf = wkaq_s if si == 0 else wqa_s
            wkaq_key = "wkaq" if si == 0 else "wqa"
            V_op(lambda e: [e.tensor_scalar(
                    out=wkaq_buf[:, c, :], in0=wkae_s[:, c, :],
                    scalar1=pq_s[:, si, c:c + 1], scalar2=None, op0=ALU.mult)
                    for c in range(NCH)],
                 ["wkae", "pq%d" % si], [wkaq_key], "p1_wkaq")
            V_op(lambda e: [e.tensor_scalar(
                    out=pkb_bf[:, c:c + 1], in0=bk_s[:, c:c + 1],
                    scalar1=pq_s[:, si, c:c + 1], scalar2=None, op0=ALU.mult)
                    for c in range(NCH)],
                 ["bk", "pq%d" % si], ["pkb"], "p1_pqbk")

            def bka2_mm(e):
                return [e.matmul(pb[PB_SC][0:H, 0:1], wkaq_buf[:, kc, :],
                                 pkb_bf[:, kc:kc + 1],
                                 start=(kc == 0), stop=(kc == NCH - 1))
                        for kc in range(NCH)]
            T_op(bka2_mm, [wkaq_key, "pkb"], ["pb_sc"], "p1_bka2mm")
            V_op(lambda e: e.tensor_scalar(out=bka2_s[:, si:si + 1],
                                           in0=bka_s[:, 0:1],
                                           scalar1=0.125, scalar2=None,
                                           op0=ALU.mult),
                 ["bka"], ["bka2a"], "p1_bka2a")
            V_op(lambda e: e.tensor_add(bka2_s[:, si:si + 1],
                                        bka2_s[:, si:si + 1],
                                        pb[PB_SC][0:H, 0:1]),
                 ["pb_sc", "bka2a"], ["bka2"], "p1_bka2")

        def p2_buf(par, ti):
            primary = ((ptk_s, ["ptk_%d" % j for j in range(NCH)])
                       if par == 0
                       else (ptq_s, ["ptq_%d" % j for j in range(NCH)]))
            return tile_buf(primary, par, ti)

        def p2_load(si, ti, tag):
            par = si % 2
            t0 = si * cfg.seq + ti * T
            ring = "sync" if par == 0 else "scalar"
            buf, bkeys = p2_buf(par, ti)
            dma(ring, lambda e: e.dma_start(out=buf[:, :, :],
                                            in_=ptk_r[:, :, t0:t0 + T]),
                ["ptkd_%d" % t0] + bkeys, bkeys, tag + "_ld")
            rs_ = 2 * par + ti % 2
            bcast_r(1, t0, "r16_%d" % rs_, r16_s[:, rs_, :], H, tag + "_r16")

        def p2_scmm(si, par, buf, bkeys, tag):
            scb = (PB_SC, "pb_sc") if par == 0 else (PB_S1, "pb%d" % PB_S1)
            wkaq_buf = wkaq_s if si == 0 else wqa_s
            wkaq_key = "wkaq" if si == 0 else "wqa"

            def sc_mm(e):
                return [e.matmul(pb[scb[0]][0:H, :], wkaq_buf[:, kc, :],
                                 buf[:, kc, :],
                                 start=(kc == 0), stop=(kc == NCH - 1))
                        for kc in range(NCH)]
            T_op(sc_mm, bkeys + [wkaq_key], [scb[1]], tag + "_sck")
            return scb

        def p2a_chain(si, ti):
            par = si % 2
            t0 = si * cfg.seq + ti * T
            buf, bkeys = p2_buf(par, ti)
            scb = p2_scmm(si, par, buf, bkeys, "p2a_%d_%d" % (si, ti))
            score_pass(si, ti, par, 1, t0, "p2a_%d_%d" % (si, ti), sc_bank=scb,
                       rslot=2 * par + ti % 2)

        def p2b_chain(si, ti):
            par = si % 2
            t0 = si * cfg.seq + ti * T
            buf, bkeys = p2_buf(par, ti)
            scb = p2_scmm(si, par, buf, bkeys, "p2b_%d_%d" % (si, ti))
            es = 2 * par + ti % 2
            rs_ = 2 * par + ti % 2
            kA = "escA%d" % es
            V_op(lambda e: e.tensor_tensor(
                    out=esc_s[:, es, :], in0=pb[scb[0]][0:H, :],
                    in1=r16_s[:, rs_, :], op=ALU.mult),
                 [scb[1], "r16_%d" % rs_], [kA], "p2b_%d_%d_sc" % (si, ti))
            pool_core(si, ti, par, 1, k_slot(si), t0,
                      "p2b_%d_%d" % (si, ti), buf, bkeys, rslot=rs_)

        def p2_init(si):
            base_k = (si * 2 + 1) * 8
            ks_ = k_slot(si)
            V_op(lambda e: [
                    e.memset(tiny_s[:, base_k + 0:base_k + 1], NEG_INIT),
                    e.memset(tiny_s[:, base_k + 5:base_k + 6], 0.0),
                    e.memset(pool_s[:, ks_, :], 0.0)],
                 ["P%d" % ks_, "S%d" % base_k],
                 ["rm%d_0" % base_k, "S%d" % base_k, "P%d" % ks_], "p2_init")

        def p2_fin(si):
            pool_finalize(si, 1, k_slot(si), 2 * si + 1, "p2pf_%d" % si)
            V_op(lambda e: e.tensor_add(pool_s[:, 7, :], pool_s[:, 6, :],
                                        bk_s[:]),
                 ["Pmean", "bk"], ["pmk"], "p2_pmk")
            V_op(lambda e: e.tensor_tensor(out=pk_s[:, si, :],
                                           in0=pool_s[:, 7, :],
                                           in1=pq_s[:, si, :], op=ALU.mult),
                 ["pmk", "pq%d" % si], ["pk%d" % si], "p2_pk")
            if dbg and si == 0:
                dma("sync", lambda e: e.dma_start(out=dbg_d["dbg_pk"],
                                                  in_=pk_s[:, si, :]),
                    ["pk%d" % si], ["dbgpk"], "dbg_pk")

            V_op(lambda e: [e.tensor_scalar(
                    out=pkb_bf[:, c:c + 1], in0=bq_s[:, c:c + 1],
                    scalar1=pk_s[:, si, c:c + 1], scalar2=None, op0=ALU.mult)
                    for c in range(NCH)],
                 ["bq", "pk%d" % si], ["pkb"], "p2_pkbq")

            def btt_mm(e):
                ins = []
                for j in range(NCH):
                    for kc in range(NCH):
                        ins.append(e.matmul(
                            pb[PB_SC][:, j:j + 1],
                            wt_s[:, kc, j * 128:(j + 1) * 128],
                            pkb_bf[:, kc:kc + 1],
                            start=(kc == 0), stop=(kc == NCH - 1)))
                return ins
            T_op(btt_mm, ["pkb", "wt"], ["pb_sc"], "p2_bttmm")
            V_op(lambda e: e.tensor_add(btt_s[:, si, :], pb[PB_SC][:, 0:NCH],
                                        bqbt_s[:]),
                 ["pb_sc", "bqbt"], ["btt%d" % si], "p2_btt")
            if dbg and si == 0:
                dma("sync", lambda e: e.dma_start(out=dbg_d["dbg_btt"],
                                                  in_=btt_s[:, si, :]),
                    ["btt%d" % si], ["dbgbtt"], "dbg_btt")

        def p3_bufs(si, ti):
            pp = ti % 2
            if pp == 0:
                qb = (ptq_s, ["p3ptq0"] + ["ptq_%d" % j for j in range(NCH)])
            else:
                qb = (ptk_s, ["p3ptq1"] + ["ptk_%d" % j for j in range(NCH)])
            x1b = (x1_bf[:, pp], ["p3x1%d" % pp])
            return pp, qb, x1b

        def p3_wtscale(si, reload):
            if reload:
                dma("gpsimd", lambda e: e.dma_start(out=wt_s[:], in_=wt_l),
                    ["wt"], ["wt"], "p3_wtrl")
            V_op(lambda e: [e.tensor_scalar(
                    out=wt_s[:, kc, :], in0=wt_s[:, kc, :],
                    scalar1=pk_s[:, si, kc:kc + 1], scalar2=None,
                    op0=ALU.mult) for kc in range(NCH)],
                 ["wt", "pk%d" % si], ["wt"], "p3_wtsc")

        def p3_load(si, ti):
            t0 = si * cfg.seq + ti * T
            pp, (qbuf, qkeys), (x1buf, x1keysb) = p3_bufs(si, ti)
            dma("sync", lambda e: e.dma_start(out=qbuf[:, :, :],
                                              in_=ptq_r[:, :, t0:t0 + T]),
                ["ptqd_%d" % t0] + qkeys, qkeys, "p3_ptqld")
            dma("sync", lambda e: e.dma_start(out=x1buf[:, :, :],
                                              in_=x1_r[:, :, t0:t0 + T]),
                ["x1d_%d" % t0] + x1keysb, x1keysb, "p3_x1ld")
            dma("scalar", lambda e: e.dma_start(
                    out=rf_s[:, pp, :].unsqueeze(1),
                    in_=stat_d[1:2, t0:t0 + T].partition_broadcast(128)),
                ["statd1_%d" % t0, "rf%d" % pp], ["rf%d" % pp], "p3_rf")

        def p3_wt_j(si, ti, j):
            pp, (qbuf, qkeys), _ = p3_bufs(si, ti)
            wb = PB_W0 if (j % 2 == 0) else PB_W1

            def wt_mm(e, j=j, wb=wb):
                ins = [e.matmul(pb[wb][:],
                                wt_s[:, kc, j * 128:(j + 1) * 128],
                                qbuf[:, kc, :],
                                start=(kc == 0), stop=False)
                       for kc in range(NCH)]
                ins.append(e.matmul(pb[wb][:], i128_s[:],
                                    qbuf[:, j, :],
                                    start=False, stop=True))
                return ins
            T_op(wt_mm, qkeys + ["wt", "i128"], ["pb_w%d" % (j % 2)],
                 "p3_wt%d" % j)

        def p3_vx2(si, ti, j):
            pp, (qbuf, qkeys), (x1buf, x1keysb) = p3_bufs(si, ti)
            x1k = x1keysb[0]
            v_s = x_s[:, 1, 0:2]
            x2_ap = x_bf[:, pp]
            wb = PB_W0 if (j % 2 == 0) else PB_W1
            V_op(lambda e, j=j, wb=wb, pp=pp: e.tensor_tensor(
                    out=v_s[:, j % 2, :], in0=pb[wb][:],
                    in1=rf_s[:, pp, :], op=ALU.mult),
                 ["pb_w%d" % (j % 2), "rf%d" % pp],
                 ["v%d" % (j % 2), "x1"], "p3_v%d" % j)
            V_op(lambda e, j=j: e.scalar_tensor_tensor(
                    out=x2_ap[:, j, :], in0=v_s[:, j % 2, :],
                    scalar=btt_s[:, si, j:j + 1], in1=x1buf[:, j, :],
                    op0=ALU.add, op1=ALU.add),
                 ["v%d" % (j % 2), x1k, "btt%d" % si],
                 ["x2_%d_%d" % (pp, j), "xbf%d" % pp], "p3_x2%d" % j)
            S_op(lambda e, j=j: e.activation(
                    out=big_s[:, j, T:2 * T], in_=x2_ap[:, j, :],
                    func=AF.Square, bias=0.0, scale=1.0),
                 ["x2_%d_%d" % (pp, j)],
                 ["sqc%d" % j, "sq1", "sqP3", "wqT"], "p3_sq%d" % j)

        def mk_p3_filler(si, ti):
            def filler(c):
                if c >= 2:
                    p3_vx2(si, ti, c - 2)
                p3_wt_j(si, ti, c)
                if c == NCH - 1:
                    p3_vx2(si, ti, NCH - 2)
                    p3_vx2(si, ti, NCH - 1)
            return filler

        def p3_wtblock(si, ti):
            for c in range(NCH):
                if c >= 2:
                    p3_vx2(si, ti, c - 2)
                p3_wt_j(si, ti, c)
            p3_vx2(si, ti, NCH - 2)
            p3_vx2(si, ti, NCH - 1)

        def p3_stats(si, ti):
            pp = ti % 2
            t0 = si * cfg.seq + ti * T
            x2_ap = x_bf[:, pp]
            x2keys = ["x2_%d_%d" % (pp, j) for j in range(NCH)]
            sq_ap = big_s[:, :, T:2 * T]
            if dbg and si == 0 and ti == 0:
                dma("gpsimd", lambda e: e.dma_start(
                        out=dbg_d["dbg_x2"].rearrange("(c p) t -> p c t", p=128),
                        in_=x2_ap[:]),
                    x2keys, ["dbgx2"], "dbg_x2")

            def sums3a(e):
                return [e.matmul(pb[PB_S1][0:1, :], ones_s[:, 1:2],
                                 x2_ap[:, c, :],
                                 start=(c == 0), stop=(c == NCH - 1))
                        for c in range(NCH)]
            T_op(sums3a, x2keys + ["ones"], ["pb%d" % PB_S1], "p3_sums3a")

            def sums3b(e):
                return [e.matmul(pb[PB_S2][0:1, :], ones_s[:, 2:3],
                                 sq_ap[:, c, :],
                                 start=(c == 0), stop=(c == NCH - 1))
                        for c in range(NCH)]
            T_op(sums3b, ["sqc%d" % c for c in range(NCH)] + ["ones"],
                 ["pb%d" % PB_S2], "p3_sums3b")
            ln_stats(PB_S1, PB_S2, pp, 4 + pp, 2, t0,
                     "p3ln_%d_%d" % (si, ti), onchip_r64=True, store=False)

        def p3_B(si, ti, filler=None):
            pp = ti % 2
            t0 = si * cfg.seq + ti * T
            out_ap = x_s[:, 0]
            x2_ap = x_bf[:, pp]
            x2keys = ["x2_%d_%d" % (pp, j) for j in range(NCH)]
            r64_mm(pp, "p3r64_%d_%d" % (si, ti))

            def fin(j, wb):
                V_op(lambda e, j=j, wb=wb: e.scalar_tensor_tensor(
                        out=out_ap[:, j, :], in0=pb[wb][:],
                        scalar=bout2_s[:, j:j + 1], in1=x2_ap[:, j, :],
                        op0=ALU.add, op1=ALU.add),
                     ["pb_w%d" % (j % 2), "x2_%d_%d" % (pp, j), "bout2",
                      "outst"],
                     ["outb_%d" % j, "x0"], "p3_out%d" % j)
            ffn(x2_ap[:], x2keys, 4 + pp, pp, "r64_%d" % pp, fin,
                "p3f_%d_%d" % (si, ti), filler=filler)
            outkeys = ["outb_%d" % j for j in range(NCH)]
            dma("sync", lambda e: e.dma_start(out=out_r[:, :, t0:t0 + T],
                                              in_=out_ap[:]),
                outkeys, ["outst"], "p3_store")

        seqs = list(range(cfg.nseq))
        for si in seqs:
            p1_init(si)
        for si in seqs:
            p1_A0(si, 0)
        for ti in range(cfg.ntile):
            for si in seqs:
                p1_A1(si, ti)
                if not only_p1 and ti > 0:
                    p1_D1(si, ti - 1)
                    p1_D2(si, ti - 1)
                p1_A2(si, ti, prev_ti=(ti - 1 if (not only_p1 and ti > 0)
                                       else None))
                if not only_p1 and ti > 0:
                    p1_D3sp(si, ti - 1)
                p1_A4(si, ti)
                if ti + 1 < cfg.ntile:
                    p1_A0(si, ti + 1)
        if not only_p1:
            ti = cfg.ntile - 1
            for si in seqs:
                p1_D1(si, ti)
                p1_D2(si, ti)
                for c in range(NCH):
                    p1_D3_j(si, ti, c)
                p1_D3sp(si, ti)
            for si in seqs:
                eb2_final(si, 0, bqa2_s[:], "bqa2", "p1eb2_%d" % si)
            for ti in range(cfg.ntile + 1):
                for si in seqs:
                    if ti < cfg.ntile:
                        pbq_load(si, ti)
                    if ti > 0:
                        pbq_chain(si, ti - 1)
            for si in seqs:
                p1_fin(si)
            for si in seqs:
                p2_init(si)
            for ti in range(cfg.ntile + 1):
                for si in seqs:
                    if ti < cfg.ntile:
                        p2_load(si, ti, "p2a_%d_%d" % (si, ti))
                    if ti > 0:
                        p2a_chain(si, ti - 1)
            for si in seqs:
                eb2_final(si, 1, bka2_s[:, si:si + 1], "bka2",
                          "p2eb2_%d" % si)
            for ti in range(cfg.ntile + 1):
                for si in seqs:
                    if ti < cfg.ntile:
                        p2_load(si, ti, "p2b_%d_%d" % (si, ti))
                    if ti > 0:
                        p2b_chain(si, ti - 1)
            for si in seqs:
                p2_fin(si)
            for sidx, si in enumerate(seqs):
                p3_wtscale(si, reload=(sidx > 0))
                p3_load(si, 0)
                for ti in range(cfg.ntile):
                    if ti + 1 < cfg.ntile:
                        p3_load(si, ti + 1)
                    if ti == 0:
                        p3_wtblock(si, 0)
                    else:
                        p3_B(si, ti - 1, filler=mk_p3_filler(si, ti))
                    p3_stats(si, ti)
                p3_B(si, cfg.ntile - 1)
        else:
            def cpout(e):
                return e.dma_start(out=out_d.rearrange("(c p) t -> p c t", p=128)[:, :, 0:T],
                                   in_=x1_bf[:, (cfg.ntile - 1) % 2, :, :])
            dma("gpsimd", cpout,
                ["x1bf%d_%d" % ((cfg.ntile - 1) % 2, j) for j in range(NCH)],
                ["outst"], "p1only_out")

        go = stk.enter_context(nc.semaphore("sem_go"))
        clear_sems = list(sems.values()) + [go]
        run_stream, counters, lane_occ = p.finalize(sems, paranoid=paranoid)
        with nc.Block() as block:
            block.sync(lambda e: run_stream("sync", e, go=go))
            block.tensor(lambda e: run_stream("tensor", e, go=go))
            block.vector(lambda e: run_stream("vector", e, go=go))
            block.scalar(lambda e: run_stream("scalar", e, go=go))
            block.gpsimd(lambda e: run_stream("gpsimd", e,
                                              clear_sems=clear_sems, go=go))
            nc.all_engine_barrier()
            for s in clear_sems:
                nc.gpsimd.sem_clear(s)

    return nc


def _weight_maps(W):
    def chunked(a):
        return np.ascontiguousarray(
            np.asarray(a, np.float32).reshape(NCH, 128, -1).transpose(1, 0, 2))

    W1 = np.asarray(W["L0_ffn_W1"], np.float32)
    W2 = np.asarray(W["L0_ffn_W2"], np.float32)
    b1 = np.asarray(W["L0_ffn_b1"], np.float32)

    w2bd = np.zeros((128, NCH, FM), np.float32)
    for c in range(NCH):
        for ml in range(NCH):
            w2bd[ml * FI:(ml + 1) * FI, c, NCH * c + ml] = W2[NCH * c + ml, :]
    sel = np.zeros((FM, NCH, 128), np.float32)
    for c in range(NCH):
        for pp in range(128):
            sel[NCH * c + pp // FI, c, pp] = 1.0

    def col(a, n=128):
        return np.ascontiguousarray(
            np.asarray(a, np.float32).reshape(NCH, n).T)

    m = {
        "wq_l": chunked(W["L0_attn_Wq"]),
        "wk_l": chunked(W["L0_attn_Wk"]),
        "wt_l": chunked(W["L0_attn_Wt"]),
        "wqT_l": chunked(np.asarray(W["L0_attn_Wq"], np.float32).T),
        "wqa_l": chunked(W["L0_attn_Wqa"]),
        "wka_l": chunked(W["L0_attn_Wka"]),
        "win_l": chunked(W["L0_ffn_Win"]),
        "wout_l": np.ascontiguousarray(np.asarray(W["L0_ffn_Wout"], np.float32)),
        "w2bd_l": w2bd,
        "sel_l": sel.astype(BF16_NP),
        "i128_l": np.eye(128, dtype=BF16_NP),
        "amask_l": np.equal(np.arange(H)[:, None] // 2,
                            np.arange(NCH)[None, :]).astype(BF16_NP),
        "asel_l": np.equal(np.arange(H)[:, None] % 2,
                           np.arange(128)[None, :] // 64).astype(BF16_NP),
        "esel_l": np.equal(np.arange(H)[:, None, None],
                           2 * np.arange(NCH)[None, :, None]
                           + np.arange(128)[None, None, :] // 64)
                    .astype(BF16_NP),
        "w1f_l": np.ascontiguousarray(W1.reshape(-1).reshape(NCH, 128).T),
        "b1raw_l": np.ascontiguousarray(b1.reshape(-1).reshape(NCH, 128).T),
        "binf_l": np.ascontiguousarray(
            np.repeat(np.asarray(W["L0_ffn_bin_"], np.float32), FI)
            .reshape(NCH, 128).T),
        "bq_l": col(W["L0_attn_bq"]),
        "bk_l": col(W["L0_attn_bk"]),
        "bt_l": col(W["L0_attn_bt"]),
        "bout_l": col(W["L0_ffn_bout"]),
        "bqa_l": np.ascontiguousarray(
            np.asarray(W["L0_attn_bqa"], np.float32).reshape(H, 1)),
        "bka_l": np.ascontiguousarray(
            np.asarray(W["L0_attn_bka"], np.float32).reshape(H, 1)),
        "b2_l": np.ascontiguousarray(
            np.asarray(W["L0_ffn_b2"], np.float32).reshape(FM, 1)),
    }
    return m


_NC_CACHE = {}


def kernel(**inputs):
    x = np.asarray(inputs["L0_attn_hidden_states"], np.float32)
    B, S, Dm = x.shape
    assert Dm == D
    ncores = 8
    per = B // ncores
    cfg = CFG(per, S, 512)

    key = (per, S)
    if key not in _NC_CACHE:
        _NC_CACHE[key] = build_nc(cfg)
    nc = _NC_CACHE[key]

    wm = _weight_maps(inputs)
    in_maps = []
    for c in range(ncores):
        m = dict(wm)
        m["xT"] = np.ascontiguousarray(
            x[c * per:(c + 1) * per].transpose(2, 0, 1).reshape(D, per * S))
        in_maps.append(m)

    res = run_bass_kernel_spmd(nc, in_maps, core_ids=list(range(ncores)))
    outs = []
    for c in range(ncores):
        o = np.asarray(res.results[c]["out"])
        outs.append(o.reshape(D, per, S).transpose(1, 2, 0))
    return np.ascontiguousarray(np.concatenate(outs, axis=0),
                                dtype=np.float32)
